# version 24
# baseline (speedup 1.0000x reference)
import contextlib
import numpy as np
import concourse.bass as bass
import concourse.mybir as mybir
from concourse.bass_utils import run_bass_kernel_spmd

F32 = mybir.dt.float32
BF16 = mybir.dt.bfloat16
U8 = mybir.dt.uint8
I32 = mybir.dt.int32
AF = mybir.ActivationFunctionType
ALU = mybir.AluOpType
AXX = mybir.AxisListType.X

B, L, DM, E, DI, N = 4, 8192, 1024, 4096, 2048, 16
TOK = 4096
T1 = 256
NS1 = TOK // T1
Q2 = 1024
NC2 = L // Q2
T3 = 256
NS3 = TOK // T3
DH = DI // 2
AGROWS = 2084
K1C = float(2 ** 20 + 1)
C2C = float(3 * 2 ** 13)

SECTIONS = ("v", "s", "t", "g", "q")
SECNAME = {"v": "vector", "s": "scalar", "t": "tensor", "g": "gpsimd", "q": "sync"}


class Op:
    __slots__ = ("eng", "fn", "waits", "sem_key", "inc", "group")

    def __init__(self, eng, fn, waits, sem_key, inc, group=False):
        self.eng, self.fn, self.waits = eng, fn, waits
        self.sem_key, self.inc, self.group = sem_key, inc, group


class Plan:
    def __init__(self):
        self.ops = []
        self.count = {}
        self.waited = {}
        self.writer = {}
        self.readers = {}
        self.n_colls = 0
        self._pending = {}

    def _need(self, sec, waits, stream, value):
        if stream is None or value <= 0:
            return
        if stream.startswith("D:"):
            value = self.count[stream]
        key = (sec, stream)
        if self.waited.get(key, 0) >= value:
            return
        waits.append((stream, value))
        self.waited[key] = value

    def _dep_waits(self, sec, reads, writes):
        waits = []
        pend = self._pending.pop(sec, None)
        if pend:
            waits.extend(pend)
        for b in reads:
            w = self.writer.get(b)
            if w is not None:
                self._need(sec, waits, w[0], w[1])
        for b in writes:
            w = self.writer.get(b)
            if w is not None:
                self._need(sec, waits, w[0], w[1])
            for pe, pc in self.readers.get(b, {}).items():
                self._need(sec, waits, pe, pc)
        if len(waits) > 1:
            best = {}
            for stream, value in waits:
                if best.get(stream, -1) < value:
                    best[stream] = value
            waits = list(best.items())
        return waits

    def _commit(self, stream, inc, reads, writes):
        self.count[stream] = self.count.get(stream, 0) + inc
        c = self.count[stream]
        for b in writes:
            self.writer[b] = (stream, c)
            self.readers[b] = {}
        for b in reads:
            self.readers.setdefault(b, {})[stream] = c

    def emit(self, eng, fn, reads=(), writes=(), dma=False, fam=None):
        if dma:
            assert fam is not None, "dma emit requires fam"
            stream = "D:" + fam
        else:
            stream = eng
        waits = self._dep_waits(eng, reads, writes)
        self._commit(stream, 16 if dma else 1, reads, writes)
        self.ops.append(Op(eng, fn, waits, stream, 16 if dma else 1))

    def emit_group(self, eng, fn, reads=(), writes=()):
        waits = self._dep_waits(eng, reads, writes)
        self._commit(eng, 1, reads, writes)
        self.ops.append(Op(eng, fn, waits, eng, 1, group=True))

    def coll(self, fn, reads=(), writes=()):
        name = f"coll{self.n_colls}"
        self.n_colls += 1
        waits = self._dep_waits("g", reads, writes)
        self.count[name] = 1
        for b in writes:
            self.writer[b] = (name, 1)
            self.readers[b] = {}
        for b in reads:
            self.readers.setdefault(b, {})[name] = 1
        self.ops.append(Op("g", fn, waits, name, 1))

    def fence(self, secs=SECTIONS):
        snapshot = dict(self.count)
        for sec in secs:
            waits = []
            for stream, value in snapshot.items():
                self._need(sec, waits, stream, value)
            self._pending.setdefault(sec, []).extend(waits)

    def replay(self, nc, block, es):
        sems = {s: es.enter_context(nc.semaphore(f"S_{s}")) for s in self.count}
        by_sec = {s: [] for s in SECTIONS}
        for op in self.ops:
            by_sec[op.eng].append(op)

        def body_for(sec):
            ops = by_sec[sec]

            def body(h):
                for op in ops:
                    for stream, value in op.waits:
                        h.wait_ge(sems[stream], value)
                    if op.group:
                        op.fn(h)[-1].then_inc(sems[op.sem_key], op.inc)
                    else:
                        op.fn(h).then_inc(sems[op.sem_key], op.inc)
            return body

        for sec in SECTIONS:
            if by_sec[sec]:
                getattr(block, SECNAME[sec])(body_for(sec))


def build(debug=False):
    nc = bass.Bass("TRN2", target_bir_lowering=False, debug=False)
    ins = {}

    def I(name, shape, dtype=F32):
        ins[name] = nc.dram_tensor(name, shape, dtype, kind="ExternalInput")

    I("xs", [TOK, DM]); I("xsT", [DM, TOK]); I("xh", [3, DM])
    I("identin", [128, 128])
    I("winT", [DM, E]); I("wxT", [DI, 33]); I("woutT", [DI, DM]); I("wfT", [DM, DM])
    I("cw", [DI, 4]); I("cb", [DI, 1])
    I("dtw_s", [DH, 1]); I("dtb_s", [DH, 1]); I("alog_s", [DH, N])
    I("dparam", [DI, 1]); I("fbias", [DM, 1]); I("sin", [1, 1]); I("sw", [1, 1])
    I("ridx", [4, 1], I32)
    out_d = nc.dram_tensor("out_d", [DM, TOK], F32, kind="ExternalOutput")

    ag_inA = nc.dram_tensor("ag_inA", [AGROWS, TOK // 2], BF16)
    ag_inB = nc.dram_tensor("ag_inB", [AGROWS, TOK // 2], BF16)
    ag_outA = nc.dram_tensor("ag_outA", [8 * AGROWS, TOK // 2], BF16, addr_space="Shared")
    ag_outB = nc.dram_tensor("ag_outB", [8 * AGROWS, TOK // 2], BF16, addr_space="Shared")
    agy_inA = nc.dram_tensor("agy_inA", [DH // 2, L], BF16)
    agy_inB = nc.dram_tensor("agy_inB", [DH // 2, L], BF16)
    agy_outA = nc.dram_tensor("agy_outA", [4 * DH, L], BF16, addr_space="Shared")
    agy_outB = nc.dram_tensor("agy_outB", [4 * DH, L], BF16, addr_space="Shared")
    zs = nc.dram_tensor("zs", [DI, TOK], BF16)
    qs = nc.dram_tensor("qs", [DM, TOK], BF16)
    ssw_d = nc.dram_tensor("ssw_d", [1, 2], F32)
    xcp = nc.dram_tensor("xcp", [DH, L], BF16)
    projp = nc.dram_tensor("projp", [33, L], BF16)
    yp = nc.dram_tensor("yp", [DI, TOK], BF16)
    dregs = nc.dram_tensor("dregs", [1, 8], I32, kind="ExternalOutput") if debug else None
    if debug:
        d_y2 = nc.dram_tensor("d_y2", [DI, TOK], BF16, kind="ExternalOutput")
        d_x3d = nc.dram_tensor("d_x3d", [DM, TOK], F32, kind="ExternalOutput")
        d_e1 = nc.dram_tensor("d_e1", [DM, TOK], F32, kind="ExternalOutput")
        d_e2 = nc.dram_tensor("d_e2", [DM, TOK], F32, kind="ExternalOutput")
    dbg = {}
    if debug:
        for nm, t in [("d_aginA", ag_inA), ("d_aginB", ag_inB),
                      ("d_agyinA", agy_inA), ("d_agyinB", agy_inB), ("d_zs", zs),
                      ("d_qs", qs), ("d_ssw", ssw_d), ("d_xcp", xcp),
                      ("d_projp", projp), ("d_yp", yp)]:
            dbg[nm] = (nc.dram_tensor(nm, list(t.shape), t.dtype,
                                      kind="ExternalOutput"), t)

    pl = Plan()
    sb = {}
    REG = {}

    es = contextlib.ExitStack()
    with es:
        def ST(name, shape, dtype=F32, stack=es):
            sb[name] = stack.enter_context(nc.sbuf_tensor(name, shape, dtype))

        for n_ in ["eps", "mhalf", "one", "invs", "sswb"]:
            ST(n_, [128, 1])
        ST("sinb", [1, 1]); ST("swb", [1, 1])
        ST("s_ridx", [1, 4], I32)
        ST("identf", [128, 128])
        for d in range(8):
            ST(f"dtw{d}", [128, 1]); ST(f"dtb{d}", [128, 1])
            ST(f"negA{d}", [128, N]); ST(f"carry{d}", [128, N])
        psum = [es.enter_context(nc.psum_tensor(f"pb{i}", [128, 512], F32))
                for i in range(8)]

        def EMIT(eng, fn, reads=(), writes=(), dma=False, fam=None):
            pl.emit(eng, fn, reads, writes, dma, fam)

        EMIT("q", lambda h: h.dma_start(out=sb["s_ridx"][:], in_=ins["ridx"].rearrange("a b -> b a")),
             writes=["s_ridx"], dma=True, fam="ridx")
        EMIT("q", lambda h: h.dma_start(out=sb["identf"][:], in_=ins["identin"][:, :]),
             writes=["identf"], dma=True, fam="identf")
        EMIT("v", lambda h: h.memset(sb["eps"][:], 1e-5), writes=["eps"])
        EMIT("v", lambda h: h.memset(sb["mhalf"][:], -0.5), writes=["mhalf"])
        EMIT("v", lambda h: h.memset(sb["one"][:], 1.0), writes=["one"])
        EMIT("q", lambda h: h.dma_start(out=sb["sinb"][:], in_=ins["sin"][:, :]),
             writes=["sinb"], dma=True, fam="scales")
        EMIT("q", lambda h: h.dma_start(out=sb["swb"][:], in_=ins["sw"][:, :]),
             writes=["swb"], dma=True, fam="scales")
        EMIT("v", lambda h: h.tensor_tensor(out=sb["swb"][:], in0=sb["sinb"][:],
             in1=sb["swb"][:], op=ALU.mult), reads=["sinb", "swb"], writes=["swb"])
        EMIT("v", lambda h: h.reciprocal(out=sb["sinb"][:], in_=sb["sinb"][:]),
             reads=["sinb"], writes=["sinb"])
        EMIT("g", lambda h: h.dma_start(out=ssw_d.ap()[0:1, 0:1], in_=sb["sinb"][:]),
             reads=["sinb"], writes=["ssw_d"], dma=True, fam="sswd")
        EMIT("g", lambda h: h.dma_start(out=ssw_d.ap()[0:1, 1:2], in_=sb["swb"][:]),
             reads=["ssw_d", "swb"], writes=["ssw_d"], dma=True, fam="sswd")
        EMIT("q", lambda h: h.dma_start(
            out=sb["invs"][:], in_=bass.AP(ssw_d, 0, [[0, 128], [1, 1]])),
            reads=["ssw_d"], writes=["invs"], dma=True, fam="consts2")
        EMIT("q", lambda h: h.dma_start(
            out=sb["sswb"][:], in_=bass.AP(ssw_d, 1, [[0, 128], [1, 1]])),
            reads=["ssw_d"], writes=["sswb"], dma=True, fam="consts2")

        def init_regs(h):
            insts = []
            for i, nm in enumerate(["r_xc", "r_pj", "r_y", "r_ow"]):
                REG[nm] = h.alloc_register(nm)
                insts.append(h.reg_load(REG[nm], sb["s_ridx"][0:1, i:i + 1]))
            REG["tmp"] = h.alloc_register("tmp")
            if debug:
                for i, nm in enumerate(["r_xc", "r_pj", "r_y", "r_ow"]):
                    insts.append(h.reg_save(dregs.ap()[0:1, i:i + 1], REG[nm]))
            insts.append(h.nop())
            return insts
        pl.emit_group("q", init_regs, reads=["s_ridx"])

        def dyn_copy(dst_ap, src_tensor, base, static_off, pattern, reads, writes,
                     fam="dyncp"):
            def fn(h):
                h.reg_mov(REG["tmp"], int(static_off))
                h.reg_add(REG["tmp"], REG["tmp"], REG[base])
                return h.dma_start(out=dst_ap,
                                   in_=bass.AP(src_tensor, REG["tmp"], pattern))
            EMIT("q", fn, reads=reads, writes=writes, dma=True, fam=fam)

        ST("alod", [128, N])
        ST("ident16", [128, 128], BF16)
        EMIT("v", lambda h: h.tensor_copy(out=sb["ident16"][:], in_=sb["identf"][:]),
             reads=["identf"], writes=["ident16"])

        for d in range(8):
            EMIT("q", (lambda dd=d: (lambda h: h.dma_start(
                out=sb["alod"][:],
                in_=ins["alog_s"][dd * 128:(dd + 1) * 128, :])))(),
                writes=["alod"], dma=True, fam="alod")
            EMIT("s", (lambda dd=d: (lambda h: h.activation(
                out=sb[f"negA{dd}"][:], in_=sb["alod"][:], func=AF.Exp)))(),
                reads=["alod"], writes=[f"negA{d}"])
            EMIT("v", (lambda dd=d: (lambda h: h.tensor_scalar(
                out=sb[f"negA{dd}"][:], in0=sb[f"negA{dd}"][:], scalar1=-1.0,
                scalar2=None, op0=ALU.mult)))(),
                reads=[f"negA{d}"], writes=[f"negA{d}"])
            EMIT("q", (lambda dd=d: (lambda h: h.dma_start(
                out=sb[f"dtw{dd}"][:],
                in_=ins["dtw_s"][dd * 128:(dd + 1) * 128, :])))(),
                writes=[f"dtw{d}"], dma=True, fam="consts3")
            EMIT("q", (lambda dd=d: (lambda h: h.dma_start(
                out=sb[f"dtb{dd}"][:],
                in_=ins["dtb_s"][dd * 128:(dd + 1) * 128, :])))(),
                writes=[f"dtb{d}"], dma=True, fam="consts3")
            EMIT("v", (lambda dd=d: (lambda h: h.memset(sb[f"carry{dd}"][:], 0.0)))(),
                 writes=[f"carry{d}"])

        p1 = contextlib.ExitStack()
        with p1:
            def S1(name, shape, dtype=F32):
                ST(name, shape, dtype, stack=p1)

            for i in range(2):
                for sp in range(2):
                    S1(f"xT{i}_{sp}", [128, DM])
                    S1(f"mean{i}_{sp}", [128, 1]); S1(f"ex2{i}_{sp}", [128, 1])
                    S1(f"var{i}_{sp}", [128, 1]); S1(f"rstd{i}_{sp}", [128, 1])
                    S1(f"nmr{i}_{sp}", [128, 1])
                S1(f"sq{i}", [128, DM])
            for k in range(8):
                S1(f"winT{k}", [128, E], BF16)
                S1(f"hxnd16_{k}", [128, 3], BF16)
            for sp in range(2):
                S1(f"xndC_{sp}", [128, 8 * T1]); S1(f"xndC16_{sp}", [128, 8 * T1], BF16)
                S1(f"q16C_{sp}", [128, 8 * T1], BF16)
            for d in range(16):
                S1(f"wxT{d}", [128, 33], BF16)
                S1(f"xss{d}", [128, T1 + 3], BF16)
            S1("xcw", [128, 16 * T1], BF16)
            S1("z16w", [128, 16 * T1], BF16)
            S1("wld", [128, 1024])
            S1("cacc_0", [128, 8 * T1]); S1("cacc_1", [128, 8 * T1])
            S1("proj_0", [33, T1], BF16); S1("proj_1", [33, T1], BF16)
            S1("qv", [128, 8 * T1]); S1("qc", [128, 8 * T1]); S1("qw", [128, 8 * T1])
            S1("qhi", [128, 8 * T1])
            S1("qm", [128, 8 * T1], U8)
            S1("c2p", [128, 1]); S1("c2n", [128, 1])
            for d in range(16):
                S1(f"cw{d}", [128, 4]); S1(f"cb{d}", [128, 1])

            EMIT("v", lambda h: h.memset(sb["c2p"][:], C2C), writes=["c2p"])
            EMIT("v", lambda h: h.memset(sb["c2n"][:], -C2C), writes=["c2n"])
            for d in range(16):
                EMIT("q", (lambda dd=d: (lambda h: h.dma_start(
                    out=sb[f"cw{dd}"][:], in_=ins["cw"][dd * 128:(dd + 1) * 128, :])))(),
                    writes=[f"cw{d}"], dma=True, fam="consts")
                EMIT("q", (lambda dd=d: (lambda h: h.dma_start(
                    out=sb[f"cb{dd}"][:], in_=ins["cb"][dd * 128:(dd + 1) * 128, :])))(),
                    writes=[f"cb{d}"], dma=True, fam="consts")
            for k in range(8):
                for c4 in range(4):
                    EMIT("q", (lambda kk=k, cc=c4: (lambda h: h.dma_start(
                        out=sb["wld"][:],
                        in_=ins["winT"].ap()[kk * 128:(kk + 1) * 128,
                                             cc * 1024:(cc + 1) * 1024])))(),
                        writes=["wld"], dma=True, fam="wld")
                    EMIT("v", (lambda kk=k, cc=c4: (lambda h: h.tensor_copy(
                        out=sb[f"winT{kk}"][:, cc * 1024:(cc + 1) * 1024],
                        in_=sb["wld"][:])))(),
                        reads=["wld"], writes=[f"winT{k}"])
            for d in range(16):
                EMIT("q", (lambda dd=d: (lambda h: h.dma_start(
                    out=sb["wld"][:, 0:33],
                    in_=ins["wxT"][dd * 128:(dd + 1) * 128, :])))(),
                    writes=["wld"], dma=True, fam="wld")
                EMIT("v", (lambda dd=d: (lambda h: h.tensor_copy(
                    out=sb[f"wxT{dd}"][:], in_=sb["wld"][:, 0:33])))(),
                    reads=["wld"], writes=[f"wxT{d}"])

            def ln_apply(kx, ksq, km, ke, kv, kr, kn):
                kxn = kx
                xT, sq = sb[kx], sb[ksq]
                mean, ex2, var, rstd, nmr = sb[km], sb[ke], sb[kv], sb[kr], sb[kn]
                EMIT("v", lambda h: h.tensor_tensor(out=sq[:], in0=xT[:], in1=xT[:],
                     op=ALU.mult), reads=[kx], writes=[ksq])
                EMIT("v", lambda h: h.reduce_sum(out=mean[:], in_=xT[:], axis=AXX),
                     reads=[kx], writes=[km])
                EMIT("v", lambda h: h.reduce_sum(out=ex2[:], in_=sq[:], axis=AXX),
                     reads=[ksq], writes=[ke])
                EMIT("v", lambda h: h.tensor_scalar(out=mean[:], in0=mean[:],
                     scalar1=1.0 / DM, scalar2=None, op0=ALU.mult),
                     reads=[km], writes=[km])
                EMIT("v", lambda h: h.tensor_scalar(out=ex2[:], in0=ex2[:],
                     scalar1=1.0 / DM, scalar2=None, op0=ALU.mult),
                     reads=[ke], writes=[ke])
                EMIT("v", lambda h: h.scalar_tensor_tensor(out=var[:], in0=mean[:],
                     scalar=mean[:], in1=ex2[:], op0=ALU.mult, op1=ALU.subtract),
                     reads=[km, ke], writes=[kv])
                EMIT("v", lambda h: h.tensor_scalar(out=var[:], in0=var[:],
                     scalar1=-1.0, scalar2=None, op0=ALU.mult),
                     reads=[kv], writes=[kv])
                EMIT("s", lambda h: h.activation(out=var[:], in_=var[:], func=AF.Ln,
                     bias=sb["eps"][:]), reads=[kv, "eps"], writes=[kv])
                EMIT("s", lambda h: h.activation(out=rstd[:], in_=var[:], func=AF.Exp,
                     scale=sb["mhalf"][:]), reads=[kv, "mhalf"], writes=[kr])
                EMIT("v", lambda h: h.scalar_tensor_tensor(out=nmr[:], in0=mean[:],
                     scalar=-1.0, in1=rstd[:], op0=ALU.mult, op1=ALU.mult),
                     reads=[km, kr], writes=[kn])
                EMIT("s", lambda h: h.activation(out=sb[kxn][:], in_=xT[:],
                     func=AF.Identity, scale=rstd[:], bias=nmr[:]),
                     reads=[kx, kr, kn], writes=[kxn])

            EMIT("v", lambda h: h.memset(sb["xT0_0"][:], 0.0), writes=["xT0_0"])
            EMIT("q", lambda h: h.dma_start(out=sb["xT0_0"][0:3, :], in_=ins["xh"][:, :]),
                 reads=["xT0_0"], writes=["xT0_0"], dma=True, fam="xh")
            ln_apply("xT0_0", "sq0", "mean0_0", "ex20_0", "var0_0", "rstd0_0", "nmr0_0")
            for k in range(8):
                pk = k % 2
                EMIT("t", (lambda kk=k, p=pk: (lambda h: h.transpose(
                    out=psum[p][:, 0:128],
                    in_=sb["xT0_0"][:, kk * 128:(kk + 1) * 128],
                    identity=sb["identf"][:])))(),
                    reads=["xT0_0", "identf"], writes=[f"pb{pk}"])
                EMIT("s", (lambda kk=k, p=pk: (lambda h: h.activation(
                    out=sb[f"hxnd16_{kk}"][:], in_=psum[p][:, 0:3],
                    func=AF.Identity)))(),
                    reads=[f"pb{pk}"], writes=[f"hxnd16_{k}"])
            for e in range(16):
                pe_ = 2 + e % 2

                def mmgroup(h, ee=e, p=pe_):
                    return [h.matmul(out=psum[p][:, 0:3],
                                     lhsT=sb[f"winT{k}"][:, ee * 128:(ee + 1) * 128],
                                     rhs=sb[f"hxnd16_{k}"][:],
                                     start=(k == 0), stop=(k == 7))
                            for k in range(8)]
                pl.emit_group("t", mmgroup,
                              reads=[f"hxnd16_{k}" for k in range(8)]
                              + [f"winT{k}" for k in range(8)],
                              writes=[f"pb{pe_}"])
                EMIT("s", (lambda ee=e, p=pe_: (lambda h: h.activation(
                    out=sb[f"xss{ee}"][:, 0:3], in_=psum[p][:, 0:3],
                    func=AF.Identity)))(),
                    reads=[f"pb{pe_}"], writes=[f"xss{e}"])

            def issue_agA():
                aginA_keys = ([f"agin_{s}" for s in range(NS1 // 2)]
                              + [f"agin_{s}_p" for s in range(NS1 // 2)])
                pl.coll(lambda h: h.collective_compute(
                    "AllGather", ALU.bypass, replica_groups=[list(range(8))],
                    ins=[ag_inA.ap().opt()], outs=[ag_outA.ap().opt()]),
                    reads=aginA_keys, writes=["ag_outA"])

            for s1 in range(NS1):
                if s1 == NS1 // 2:
                    issue_agA()
                c0 = s1 * T1
                sp = s1 % 2
                xTn = [f"xT{i}_{sp}" for i in range(2)]
                xndC = f"xndC_{sp}"
                xndC16 = f"xndC16_{sp}"
                q16C = f"q16C_{sp}"
                xcwn = "xcw"
                caccn = ("cacc_0", "cacc_1")
                projn = f"proj_{sp}"
                for i in range(2):
                    EMIT("q", (lambda ii=i, cc=c0, bx=xTn[i]: (lambda h: h.dma_start(
                        out=sb[bx][:],
                        in_=ins["xs"][cc + ii * 128: cc + (ii + 1) * 128, :])))(),
                        writes=[xTn[i]], dma=True, fam=f"xT{sp}")
                for i in range(2):
                    ln_apply(xTn[i], f"sq{i}", f"mean{i}_{sp}", f"ex2{i}_{sp}",
                             f"var{i}_{sp}", f"rstd{i}_{sp}", f"nmr{i}_{sp}")
                for kc in range(8):
                    p = kc % 2
                    for i in range(2):
                        EMIT("t", (lambda kk=kc, ii=i, pp=p, bx=xTn[i]: (lambda h: h.transpose(
                            out=psum[pp][:, ii * 128:(ii + 1) * 128],
                            in_=sb[bx][:, kk * 128:(kk + 1) * 128],
                            identity=sb["identf"][:])))(),
                            reads=[xTn[i], "identf"], writes=[f"pb{p}"])
                    EMIT("s", (lambda kk=kc, pp=p, bx=xndC: (lambda h: h.activation(
                        out=sb[bx][:, kk * T1:(kk + 1) * T1], in_=psum[pp][:, 0:T1],
                        func=AF.Identity)))(),
                        reads=[f"pb{p}"], writes=[f"{xndC}:{kc}"])
                EMIT("v", (lambda b1=xndC, b2=xndC16: (lambda h: h.tensor_copy(
                    out=sb[b2][:], in_=sb[b1][:])))(),
                    reads=[f"{xndC}:{kc2}" for kc2 in range(8)], writes=[xndC16])
                def conv_d(d):
                    ca = caccn[d % 2]
                    sl = (d // 2) * T1
                    EMIT("v", (lambda dd=d, c_=ca, s_=sl: (lambda h: h.tensor_scalar(
                        out=sb[c_][:, s_:s_ + T1], in0=sb[f"xss{dd}"][:, 0:T1],
                        scalar1=sb[f"cw{dd}"][:, 0:1], scalar2=None, op0=ALU.mult)))(),
                        reads=[f"xss{d}", f"cw{d}"], writes=[f"{ca}:{d}"])
                    for kk in range(1, 4):
                        EMIT("v", (lambda dd=d, k2=kk, c_=ca, s_=sl: (lambda h: h.scalar_tensor_tensor(
                            out=sb[c_][:, s_:s_ + T1], in0=sb[f"xss{dd}"][:, k2:T1 + k2],
                            scalar=sb[f"cw{dd}"][:, k2:k2 + 1], in1=sb[c_][:, s_:s_ + T1],
                            op0=ALU.mult, op1=ALU.add)))(),
                            reads=[f"xss{d}", f"cw{d}", f"{ca}:{d}"], writes=[f"{ca}:{d}"])
                    EMIT("v", (lambda dd=d: (lambda h: h.tensor_copy(
                        out=sb[f"xss{dd}"][:, 0:3],
                        in_=sb[f"xss{dd}"][:, T1:T1 + 3])))(),
                        reads=[f"xss{d}"], writes=[f"xss{d}"])

                def conv_tail(d):
                    ca = caccn[d % 2]
                    sl = (d // 2) * T1
                    EMIT("s", (lambda dd=d, c_=ca, s_=sl, bx=xcwn: (lambda h: h.activation(
                        out=sb[bx][:, dd * T1:(dd + 1) * T1],
                        in_=sb[c_][:, s_:s_ + T1], func=AF.Silu,
                        bias=sb[f"cb{dd}"][:])))(),
                        reads=[f"{ca}:{d}", f"cb{d}"], writes=[f"{xcwn}:{d}"])

                for e in range(32):
                    p = (2, 3, 5, 6)[e % 4]

                    def mmg(h, ee=e, pp=p, bx=xndC16):
                        return [h.matmul(out=psum[pp][:, 0:T1],
                                         lhsT=sb[f"winT{k}"][:, ee * 128:(ee + 1) * 128],
                                         rhs=sb[bx][:, k * T1:(k + 1) * T1],
                                         start=(k == 0), stop=(k == 7))
                                for k in range(8)]
                    pl.emit_group("t", mmg,
                                  reads=[xndC16]
                                  + [f"winT{k}" for k in range(8)],
                                  writes=[f"pb{p}"])
                    if e < 16:
                        EMIT("s", (lambda ee=e, pp=p: (lambda h: h.activation(
                            out=sb[f"xss{ee}"][:, 3:T1 + 3], in_=psum[pp][:, 0:T1],
                            func=AF.Identity)))(),
                            reads=[f"pb{p}"], writes=[f"xss{e}"])
                        conv_d(e)
                    else:
                        EMIT("s", (lambda ee=e, pp=p: (lambda h: h.activation(
                            out=sb["z16w"][:, (ee - 16) * T1:(ee - 15) * T1],
                            in_=psum[pp][:, 0:T1],
                            func=AF.Identity)))(),
                            reads=[f"pb{p}"], writes=[f"z16w:{e - 16}"])
                EMIT("g", (lambda cc=c0: (lambda h: h.dma_start(
                    out=bass.AP(zs, cc, [[TOK, 128], [128 * TOK, 16], [1, T1]]),
                    in_=sb["z16w"][:].rearrange("p (k j) -> p k j", k=16))))(),
                    reads=[f"z16w:{j}" for j in range(16)],
                    writes=[f"zs_{s1}"], dma=True, fam="zs")

                for d in range(16):
                    conv_tail(d)
                agt = ag_inA if s1 < NS1 // 2 else ag_inB
                cl = c0 % (TOK // 2)
                EMIT("g", (lambda cc=cl, at=agt, bx=xcwn: (lambda h: h.dma_start(
                    out=bass.AP(at, cc,
                                [[TOK // 2, 128], [128 * (TOK // 2), 16], [1, T1]]),
                    in_=sb[bx][:].rearrange("p (k j) -> p k j", k=16))))(),
                    reads=[f"{xcwn}:{d}" for d in range(16)],
                    writes=[f"agin_{s1}"], dma=True, fam="agin")
                def wxg(h, bx=xcwn):
                    return [h.matmul(out=psum[4][0:33, 0:T1], lhsT=sb[f"wxT{d}"][:],
                                     rhs=sb[bx][:, d * T1:(d + 1) * T1],
                                     start=(d == 0), stop=(d == 15))
                            for d in range(16)]
                pl.emit_group("t", wxg,
                              reads=[f"{xcwn}:{d}" for d in range(16)]
                              + [f"wxT{d}" for d in range(16)],
                              writes=["pb4"])
                EMIT("v", (lambda pj=projn: (lambda h: h.tensor_copy(out=sb[pj][:],
                     in_=psum[4][0:33, 0:T1])))(),
                     reads=["pb4"], writes=[projn])
                agt2 = ag_inA if s1 < NS1 // 2 else ag_inB
                cl2 = c0 % (TOK // 2)
                EMIT("g", (lambda cc=cl2, pj=projn, at=agt2: (lambda h: h.dma_start(
                    out=at.ap()[DI:DI + 33, cc:cc + T1], in_=sb[pj][:])))(),
                    reads=[projn], writes=[f"agin_{s1}_p"], dma=True, fam="agin")

                EMIT("v", (lambda bx=xndC: (lambda h: h.tensor_scalar(
                    out=sb["qv"][:], in0=sb[bx][:], scalar1=sb["invs"][:],
                    scalar2=-448.0, op0=ALU.mult, op1=ALU.max)))(),
                    reads=[f"{xndC}:{kc2}" for kc2 in range(8)] + ["invs"],
                    writes=["qv"])
                EMIT("v", lambda h: h.tensor_scalar(out=sb["qv"][:], in0=sb["qv"][:],
                     scalar1=448.0, scalar2=None, op0=ALU.min),
                     reads=["qv"], writes=["qv"])
                EMIT("v", lambda h: h.tensor_scalar(out=sb["qc"][:], in0=sb["qv"][:],
                     scalar1=K1C, scalar2=None, op0=ALU.mult),
                     reads=["qv"], writes=["qc"])
                EMIT("v", lambda h: h.scalar_tensor_tensor(out=sb["qw"][:],
                     in0=sb["qv"][:], scalar=K1C, in1=sb["qv"][:],
                     op0=ALU.mult, op1=ALU.subtract),
                     reads=["qv"], writes=["qw"])
                EMIT("v", lambda h: h.tensor_tensor(out=sb["qhi"][:], in0=sb["qc"][:],
                     in1=sb["qw"][:], op=ALU.subtract),
                     reads=["qc", "qw"], writes=["qhi"])
                EMIT("v", lambda h: h.tensor_scalar(out=sb["qc"][:], in0=sb["qv"][:],
                     scalar1=C2C, scalar2=None, op0=ALU.add),
                     reads=["qv"], writes=["qc"])
                EMIT("v", lambda h: h.tensor_scalar(out=sb["qw"][:], in0=sb["qc"][:],
                     scalar1=C2C, scalar2=None, op0=ALU.subtract),
                     reads=["qc"], writes=["qw"])
                EMIT("v", lambda h: h.tensor_tensor(out=sb["qc"][:], in0=sb["qv"][:],
                     in1=sb["qv"][:], op=ALU.mult),
                     reads=["qv"], writes=["qc"])
                EMIT("v", lambda h: h.tensor_scalar(out=sb["qm"][:], in0=sb["qc"][:],
                     scalar1=float(2.0 ** -12), scalar2=None, op0=ALU.is_lt),
                     reads=["qc"], writes=["qm"])
                EMIT("v", lambda h: h.select(out=sb["qc"][:], mask=sb["qm"][:],
                     on_true=sb["qw"][:], on_false=sb["qhi"][:]),
                     reads=["qm", "qw", "qhi"], writes=["qc"])
                EMIT("v", (lambda q=q16C: (lambda h: h.tensor_copy(
                    out=sb[q][:], in_=sb["qc"][:])))(),
                    reads=["qc"], writes=[q16C])
                EMIT("g", (lambda q=q16C, cc=c0: (lambda h: h.dma_start(
                    out=bass.AP(qs, cc, [[TOK, 128], [128 * TOK, 8], [1, T1]]),
                    in_=bass.AP(sb[q].tensor if hasattr(sb[q], "tensor") else None, 0,
                                [[8 * T1, 128], [T1, 8], [1, T1]])
                    if False else sb[q][:].rearrange("p (k j) -> p k j", k=8))))(),
                    reads=[q16C], writes=[f"qs_{s1}"], dma=True, fam="qs")

        pl.fence()
        aginB_keys = ([f"agin_{s}" for s in range(NS1 // 2, NS1)]
                      + [f"agin_{s}_p" for s in range(NS1 // 2, NS1)])
        pl.coll(lambda h: h.collective_compute(
            "AllGather", ALU.bypass, replica_groups=[list(range(8))],
            ins=[ag_inB.ap().opt()], outs=[ag_outB.ap().opt()]),
            reads=aginB_keys, writes=["ag_outB"])
        HT = TOK // 2
        for half in range(2):
            q_ = 2 * half
            dyn_copy(xcp.ap()[:, q_ * HT:(q_ + 1) * HT], ag_outA, "r_xc",
                     half * AGROWS * HT, [[HT, DH], [1, HT]],
                     reads=["ag_outA"], writes=[f"xcp_q{q_}"], fam=f"xcq{q_}")
            dyn_copy(projp.ap()[:, q_ * HT:(q_ + 1) * HT], ag_outA, "r_pj",
                     half * AGROWS * HT, [[HT, 33], [1, HT]],
                     reads=["ag_outA"], writes=[f"projp_q{q_}"], fam=f"pjq{q_}")
        for half in range(2):
            q_ = 2 * half + 1
            dyn_copy(xcp.ap()[:, q_ * HT:(q_ + 1) * HT], ag_outB, "r_xc",
                     half * AGROWS * HT, [[HT, DH], [1, HT]],
                     reads=["ag_outB"], writes=[f"xcp_q{q_}"], fam=f"xcq{q_}")
            dyn_copy(projp.ap()[:, q_ * HT:(q_ + 1) * HT], ag_outB, "r_pj",
                     half * AGROWS * HT, [[HT, 33], [1, HT]],
                     reads=["ag_outB"], writes=[f"projp_q{q_}"], fam=f"pjq{q_}")
        p23 = contextlib.ExitStack()
        with p23:
            def S23(name, shape, dtype=F32):
                ST(name, shape, dtype, stack=p23)

            S23("wld3a", [128, DM]); S23("wld3b", [128, DM])
            for d in range(16):
                S23(f"woT{d}", [128, DM], BF16)
                S23(f"dp{d}", [128, 1])
            for k in range(8):
                S23(f"wfT{k}", [128, DM], BF16)
            for m in range(8):
                S23(f"fb{m}", [128, 1])

            p2 = contextlib.ExitStack()
            with p2:
                def S2(name, shape, dtype=F32):
                    ST(name, shape, dtype, stack=p2)

                for d in range(8):
                    S2(f"dt16_{d}", [128, Q2], BF16)
                    S2(f"u16_{d}", [128, Q2], BF16)
                    S2(f"xc2_{d}", [128, Q2], BF16)
                S2("dtbc", [128, Q2], BF16)
                S2("zp", [128, Q2]); S2("rel", [128, Q2]); S2("nab", [128, Q2])
                S2("lnb", [128, Q2])
                for p in range(2):
                    S2(f"Bb{p}", [128, Q2], BF16); S2(f"Cb{p}", [128, Q2], BF16)
                    S2(f"dA{p}", [128, Q2]); S2(f"dBx{p}", [128, Q2], BF16)
                    S2(f"hh{p}", [128, Q2], BF16); S2(f"gy{p}", [128, Q2], BF16)
                    S2(f"y16w_{p}", [128, 1024], BF16)
                for d in range(16):
                    EMIT("q", (lambda dd=d: (lambda h: h.dma_start(
                        out=sb[f"dp{dd}"][:],
                        in_=ins["dparam"][dd * 128:(dd + 1) * 128, :])))(),
                        writes=[f"dp{d}"], dma=True, fam="consts4")
                for m in range(8):
                    EMIT("q", (lambda mm=m: (lambda h: h.dma_start(
                        out=sb[f"fb{mm}"][:],
                        in_=ins["fbias"][mm * 128:(mm + 1) * 128, :])))(),
                        writes=[f"fb{m}"], dma=True, fam="consts4")

                for c in range(NC2):
                    EMIT("q", (lambda cc=c: (lambda h: h.dma_start(out=sb["dtbc"][:],
                         in_=bass.AP(projp, cc * Q2, [[0, 128], [1, Q2]]))))(),
                         reads=[f"projp_q{c // 2}"], writes=["dtbc"], dma=True, fam="dtbc")
                    for d in range(8):
                        EMIT("q", (lambda dd=d, cc=c: (lambda h: h.dma_start(
                            out=sb[f"xc2_{dd}"][:],
                            in_=xcp.ap()[dd * 128:(dd + 1) * 128,
                                         cc * Q2:(cc + 1) * Q2])))(),
                            reads=[f"xcp_q{c // 2}"], writes=[f"xc2_{d}"], dma=True, fam="xc2")
                    for j in range(3):
                        w = 3 * c + j
                        stg = "wld3a" if w % 2 == 0 else "wld3b"
                        famw = "wl3a" if w % 2 == 0 else "wl3b"
                        if w < 16:
                            srct, row, dstb = ins["woutT"], w, f"woT{w}"
                        else:
                            srct, row, dstb = ins["wfT"], w - 16, f"wfT{w - 16}"
                        EMIT("q", (lambda s_=srct, r_=row, g_=stg: (lambda h: h.dma_start(
                            out=sb[g_][:], in_=s_[r_ * 128:(r_ + 1) * 128, :])))(),
                            writes=[stg], dma=True, fam=famw)
                        EMIT("g", (lambda g_=stg, b_=dstb: (lambda h: h.tensor_copy(
                            out=sb[b_][:], in_=sb[g_][:])))(),
                            reads=[stg], writes=[dstb])
                    for d in range(8):
                        EMIT("s", (lambda dd=d: (lambda h: h.activation(
                            out=sb["zp"][:], in_=sb["dtbc"][:], func=AF.Identity,
                            scale=sb[f"dtw{dd}"][:], bias=sb[f"dtb{dd}"][:])))(),
                            reads=["dtbc", f"dtw{d}", f"dtb{d}"], writes=["zp"])
                        EMIT("v", lambda h: h.tensor_scalar(out=sb["rel"][:], in0=sb["zp"][:],
                             scalar1=0.0, scalar2=None, op0=ALU.max),
                             reads=["zp"], writes=["rel"])
                        EMIT("v", lambda h: h.scalar_tensor_tensor(out=sb["nab"][:],
                             in0=sb["zp"][:], scalar=-1.0, in1=sb["zp"][:],
                             op0=ALU.mult, op1=ALU.min),
                             reads=["zp"], writes=["nab"])
                        EMIT("s", lambda h: h.activation(out=sb["nab"][:], in_=sb["nab"][:],
                             func=AF.Exp), reads=["nab"], writes=["nab"])
                        EMIT("s", lambda h: h.activation(out=sb["lnb"][:], in_=sb["nab"][:],
                             func=AF.Ln, bias=sb["one"][:]),
                             reads=["nab", "one"], writes=["lnb"])
                        EMIT("v", (lambda dd=d: (lambda h: h.tensor_tensor(
                            out=sb[f"dt16_{dd}"][:], in0=sb["rel"][:], in1=sb["lnb"][:],
                            op=ALU.add)))(),
                            reads=["rel", "lnb"], writes=[f"dt16_{d}"])
                        EMIT("v", (lambda dd=d: (lambda h: h.tensor_tensor(
                            out=sb[f"u16_{dd}"][:], in0=sb[f"dt16_{dd}"][:],
                            in1=sb[f"xc2_{dd}"][:], op=ALU.mult)))(),
                            reads=[f"dt16_{d}", f"xc2_{d}"], writes=[f"u16_{d}"])
                    for dgrp in range(2):
                        d4 = dgrp * 4
                        for n in range(N):
                            p = n % 2
                            EMIT("q", (lambda nn=n, pp=p, cc=c: (lambda h: h.dma_start(
                                out=sb[f"Bb{pp}"][:],
                                in_=bass.AP(projp, (1 + nn) * L + cc * Q2,
                                            [[0, 128], [1, Q2]]))))(),
                                reads=[f"projp_q{c // 2}"], writes=[f"Bb{p}"], dma=True, fam=f"bb{p}")
                            EMIT("q", (lambda nn=n, pp=p, cc=c: (lambda h: h.dma_start(
                                out=sb[f"Cb{pp}"][:],
                                in_=bass.AP(projp, (17 + nn) * L + cc * Q2,
                                            [[0, 128], [1, Q2]]))))(),
                                reads=[f"projp_q{c // 2}"], writes=[f"Cb{p}"], dma=True, fam=f"cb2{p}")
                            for d in range(d4, d4 + 4):
                                q = (n * 4 + d) % 2
                                EMIT("s", (lambda dd=d, nn=n, qq=q: (lambda h: h.activation(
                                    out=sb[f"dA{qq}"][:], in_=sb[f"dt16_{dd}"][:],
                                    func=AF.Exp,
                                    scale=sb[f"negA{dd}"][:, nn:nn + 1])))(),
                                    reads=[f"dt16_{d}", f"negA{d}"], writes=[f"dA{q}"])
                                EMIT("v", (lambda dd=d, pp=p, qq=q: (lambda h: h.tensor_tensor(
                                    out=sb[f"dBx{qq}"][:], in0=sb[f"u16_{dd}"][:],
                                    in1=sb[f"Bb{pp}"][:], op=ALU.mult)))(),
                                    reads=[f"u16_{d}", f"Bb{p}"], writes=[f"dBx{q}"])
                                EMIT("v", (lambda dd=d, nn=n, qq=q: (lambda h: h.tensor_tensor_scan(
                                    out=sb[f"hh{qq}"][:], data0=sb[f"dA{qq}"][:],
                                    data1=sb[f"dBx{qq}"][:],
                                    initial=sb[f"carry{dd}"][:, nn:nn + 1],
                                    op0=ALU.mult, op1=ALU.add)))(),
                                    reads=[f"dA{q}", f"dBx{q}", f"carry{d}"],
                                    writes=[f"hh{q}"])
                                EMIT("g", (lambda dd=d, nn=n, qq=q: (lambda h: h.tensor_copy(
                                    out=sb[f"carry{dd}"][:, nn:nn + 1],
                                    in_=sb[f"hh{qq}"][:, Q2 - 1:Q2])))(),
                                    reads=[f"hh{q}"], writes=[f"carry{d}"])
                                EMIT("v", (lambda dd=d, pp=p, qq=q: (lambda h: h.tensor_tensor(
                                    out=sb[f"gy{qq}"][:], in0=sb[f"hh{qq}"][:],
                                    in1=sb[f"Cb{pp}"][:], op=ALU.mult)))(),
                                    reads=[f"hh{q}", f"Cb{p}"], writes=[f"gy{q}"])
                                for hf in range(2):
                                    bk = 2 * (d - d4) + hf
                                    EMIT("t", (lambda dd=d, nn=n, qq=q, hh2=hf, bb=bk:
                                         (lambda h: h.matmul(
                                            out=psum[bb][:],
                                            lhsT=sb["ident16"][:],
                                            rhs=sb[f"gy{qq}"][:, hh2 * 512:(hh2 + 1) * 512],
                                            start=(nn == 0), stop=(nn == N - 1),
                                            skip_group_check=True)))(),
                                        reads=[f"gy{q}", "ident16"], writes=[f"pb{bk}"])
                        for d in range(d4, d4 + 4):
                            yb = f"y16w_{d % 2}"
                            for hf in range(2):
                                bk = 2 * (d - d4) + hf
                                EMIT("s", (lambda bb=bk, y_=yb, hh2=hf: (lambda h: h.activation(
                                    out=sb[y_][:, hh2 * 512:(hh2 + 1) * 512],
                                    in_=psum[bb][:],
                                    func=AF.Identity)))(),
                                    reads=[f"pb{bk}"], writes=[f"{yb}:{hf}"])
                            agt3 = agy_inA if d < 4 else agy_inB
                            EMIT("g", (lambda dd=d, y_=yb, cc=c, at=agt3: (lambda h: h.dma_start(
                                out=at.ap()[(dd % 4) * 128:(dd % 4 + 1) * 128,
                                            cc * Q2:(cc + 1) * Q2],
                                in_=sb[y_][:])))(),
                                reads=[f"{yb}:0", f"{yb}:1"], writes=[f"agy_{c}_{d}"],
                                dma=True, fam=f"agy{d % 2}")
                        if c == NC2 - 1 and dgrp == 0:
                            agyA_keys = [f"agy_{cc}_{dd}" for cc in range(NC2)
                                         for dd in range(4)]
                            pl.coll(lambda h: h.collective_compute(
                                "AllGather", ALU.bypass, replica_groups=[list(range(8))],
                                ins=[agy_inA.ap().opt()], outs=[agy_outA.ap().opt()]),
                                reads=agyA_keys, writes=["agy_outA"])

            pl.fence()
            agyB_keys = [f"agy_{c}_{d}" for c in range(NC2) for d in range(4, 8)]
            pl.coll(lambda h: h.collective_compute(
                "AllGather", ALU.bypass, replica_groups=[list(range(8))],
                ins=[agy_inB.ap().opt()], outs=[agy_outB.ap().opt()]),
                reads=agyB_keys, writes=["agy_outB"])
            for half in range(2):
                dyn_copy(yp.ap()[half * DH:half * DH + 512, :], agy_outA, "r_y",
                         half * 512 * L, [[L, 512], [1, TOK]],
                         reads=["agy_outA"], writes=[f"yp_q{2 * half}"],
                         fam=f"ypq{2 * half}")
            for half in range(2):
                dyn_copy(yp.ap()[half * DH + 512:(half + 1) * DH, :], agy_outB, "r_y",
                         half * 512 * L, [[L, 512], [1, TOK]],
                         reads=["agy_outB"], writes=[f"yp_q{2 * half + 1}"],
                         fam=f"ypq{2 * half + 1}")
            p3 = contextlib.ExitStack()
            with p3:
                def S3(name, shape, dtype=F32):
                    ST(name, shape, dtype, stack=p3)

                for sp in range(2):
                    S3(f"y3w_{sp}", [128, 16 * T3], BF16)
                    S3(f"xc3w_{sp}", [128, 16 * T3], BF16)
                    S3(f"z3w_{sp}", [128, 16 * T3], BF16)
                    S3(f"q3w_{sp}", [128, 8 * T3], BF16)
                    S3(f"x3w_{sp}", [128, 8 * T3])
                for p in range(8):
                    S3(f"e1_{p}", [128, T3])
                for p in range(2):
                    S3(f"e2_{p}", [128, T3])
                    S3(f"odw_{p}", [128, 8 * T3])

                for s3 in range(NS3):
                    c0 = s3 * T3
                    sp = s3 % 2
                    y3w, xc3w, z3w = f"y3w_{sp}", f"xc3w_{sp}", f"z3w_{sp}"
                    q3w, x3w = f"q3w_{sp}", f"x3w_{sp}"
                    EMIT("q", (lambda cc=c0, b=q3w: (lambda h: h.dma_start(
                        out=sb[b][:].rearrange("p (k j) -> p k j", k=8),
                        in_=bass.AP(qs, cc, [[TOK, 128], [128 * TOK, 8], [1, T3]]))))(),
                        reads=["qsall"], writes=[f"{q3w}:{k}" for k in range(8)],
                        dma=True, fam=f"q3{sp}")
                    EMIT("q", (lambda cc=c0, b=x3w: (lambda h: h.dma_start(
                        out=sb[b][:].rearrange("p (k j) -> p k j", k=8),
                        in_=bass.AP(ins["xsT"], cc,
                                    [[TOK, 128], [128 * TOK, 8], [1, T3]]))))(),
                        writes=[f"{x3w}:{m}" for m in range(8)],
                        dma=True, fam=f"x3{sp}")
                    EMIT("q", (lambda cc=c0, b=y3w: (lambda h: h.dma_start(
                        out=sb[b][:].rearrange("p (k j) -> p k j", k=16),
                        in_=bass.AP(yp, cc, [[TOK, 128], [128 * TOK, 16], [1, T3]]))))(),
                        reads=[f"yp_q{qq}" for qq in range(4)],
                        writes=[f"{y3w}:{d}" for d in range(16)],
                        dma=True, fam=f"y3{sp}")
                    ag3 = ag_inA if s3 < NS3 // 2 else ag_inB
                    cl3 = c0 % (TOK // 2)
                    EMIT("q", (lambda cc=cl3, b=xc3w, at=ag3: (lambda h: h.dma_start(
                        out=sb[b][:].rearrange("p (k j) -> p k j", k=16),
                        in_=bass.AP(at, cc,
                                    [[TOK // 2, 128], [128 * (TOK // 2), 16], [1, T3]]))))(),
                        reads=["againall"], writes=[f"{xc3w}:{d}" for d in range(16)],
                        dma=True, fam=f"xc3{sp}")
                    EMIT("q", (lambda cc=c0, b=z3w: (lambda h: h.dma_start(
                        out=sb[b][:].rearrange("p (k j) -> p k j", k=16),
                        in_=bass.AP(zs, cc, [[TOK, 128], [128 * TOK, 16], [1, T3]]))))(),
                        reads=["zsall"], writes=[f"{z3w}:{d}" for d in range(16)],
                        dma=True, fam=f"z3{sp}")
                    for m in range(8):
                        pf = (4, 7)[m % 2]
                        pe1 = m

                        def ffg(h, mm=m, pp=pf, b=q3w):
                            return [h.matmul(out=psum[pp][:, 0:T3],
                                             lhsT=sb[f"wfT{k}"][:, mm * 128:(mm + 1) * 128],
                                             rhs=sb[b][:, k * T3:(k + 1) * T3],
                                             start=(k == 0), stop=(k == 7))
                                    for k in range(8)]
                        pl.emit_group("t", ffg,
                                      reads=[f"{q3w}:{k}" for k in range(8)]
                                      + [f"wfT{k}" for k in range(8)],
                                      writes=[f"pb{pf}"])
                        EMIT("s", (lambda mm=m, pf_=pf, p_=pe1: (lambda h: h.activation(
                            out=sb[f"e1_{p_}"][:], in_=psum[pf_][:, 0:T3],
                            func=AF.Identity, scale=sb["sswb"][:],
                            bias=sb[f"fb{mm}"][:])))(),
                            reads=[f"pb{pf}", "sswb", f"fb{m}"], writes=[f"e1_{pe1}"])
                    for d in range(16):
                        sl = slice(d * T3, (d + 1) * T3)
                        EMIT("s", (lambda b=z3w, s_=sl: (lambda h: h.activation(
                            out=sb[b][:, s_], in_=sb[b][:, s_], func=AF.Silu)))(),
                            reads=[f"{z3w}:{d}"], writes=[f"{z3w}:{d}"])
                        EMIT("v", (lambda dd=d, by=y3w, bx=xc3w, s_=sl: (lambda h: h.scalar_tensor_tensor(
                            out=sb[by][:, s_], in0=sb[bx][:, s_],
                            scalar=sb[f"dp{dd}"][:], in1=sb[by][:, s_],
                            op0=ALU.mult, op1=ALU.add)))(),
                            reads=[f"{xc3w}:{d}", f"dp{d}", f"{y3w}:{d}"],
                            writes=[f"{y3w}:{d}"])
                        EMIT("v", (lambda by=y3w, bz=z3w, s_=sl: (lambda h: h.tensor_tensor(
                            out=sb[by][:, s_], in0=sb[by][:, s_],
                            in1=sb[bz][:, s_], op=ALU.mult)))(),
                            reads=[f"{y3w}:{d}", f"{z3w}:{d}"], writes=[f"{y3w}:{d}"])
                    for m in range(8):
                        pw = (2, 3, 5, 6)[m % 4]
                        pe1 = m
                        pp = m % 2

                        def wog(h, mm=m, pp=pw, b=y3w):
                            return [h.matmul(out=psum[pp][:, 0:T3],
                                             lhsT=sb[f"woT{d}"][:, mm * 128:(mm + 1) * 128],
                                             rhs=sb[b][:, d * T3:(d + 1) * T3],
                                             start=(d == 0), stop=(d == 15))
                                    for d in range(16)]
                        pl.emit_group("t", wog,
                                      reads=[f"{y3w}:{d}" for d in range(16)]
                                      + [f"woT{d}" for d in range(16)],
                                      writes=[f"pb{pw}"])
                        EMIT("v", (lambda mm=m, pw_=pw, p_=pp, pe=pe1: (lambda h: h.tensor_tensor(
                            out=sb[f"e2_{p_}"][:], in0=psum[pw_][:, 0:T3],
                            in1=sb[f"e1_{pe}"][:], op=ALU.add)))(),
                            reads=[f"pb{pw}", f"e1_{pe1}"], writes=[f"e2_{pp}"])
                        EMIT("v", (lambda mm=m, p_=sp, bx=x3w, pe=pp: (lambda h: h.tensor_tensor(
                            out=sb[f"odw_{p_}"][:, mm * T3:(mm + 1) * T3],
                            in0=sb[f"e2_{pe}"][:],
                            in1=sb[bx][:, mm * T3:(mm + 1) * T3], op=ALU.add)))(),
                            reads=[f"e2_{pp}", f"{x3w}:{m}"], writes=[f"odw_{sp}:{m}"])
                    EMIT("g", (lambda p_=sp, cc=c0: (lambda h: h.dma_start(
                        out=bass.AP(out_d, cc, [[TOK, 128], [128 * TOK, 8], [1, T3]]),
                        in_=sb[f"odw_{p_}"][:].rearrange("p (k j) -> p k j", k=8))))(),
                        reads=[f"odw_{sp}:{m}" for m in range(8)],
                        writes=[f"out_{s3}"], dma=True, fam="outd")

        if debug:
            pl.fence(("q",))
            for nm, (dst, srct) in dbg.items():
                pl.emit("q", (lambda d_=dst, s_=srct: (lambda h: h.dma_start(
                    out=d_.ap()[:, :], in_=s_.ap()[:, :])))(),
                    reads=[], writes=[nm], dma=True, fam="outd")
        pl.fence(("g",))
        pl.emit("g", lambda h: h.nop())

        global _LAST_PLAN
        _LAST_PLAN = pl
        with nc.Block() as block:
            pl.replay(nc, block, es)
    return nc


_NC_CACHE = None
_LAST_PLAN = None


def kernel(L0_attn_x, L0_attn_W_in, L0_attn_conv_w, L0_attn_conv_b, L0_attn_W_x,
           L0_attn_dt_w, L0_attn_dt_b, L0_attn_A_log, L0_attn_D_param,
           L0_attn_W_out, L0_ffn_weight_fp8, L0_ffn_bias, L0_ffn_input_scale,
           L0_ffn_weight_scale, _debug=False):
    f32 = np.float32
    x = np.asarray(L0_attn_x, f32)
    winT = np.ascontiguousarray(np.asarray(L0_attn_W_in, f32).T)
    wxT = np.ascontiguousarray(np.asarray(L0_attn_W_x, f32).T)
    woutT = np.ascontiguousarray(np.asarray(L0_attn_W_out, f32).T)
    wfT = np.ascontiguousarray(np.asarray(L0_ffn_weight_fp8, f32).T)
    cw = np.ascontiguousarray(np.asarray(L0_attn_conv_w, f32))
    cb = np.asarray(L0_attn_conv_b, f32).reshape(DI, 1)
    dtw = np.asarray(L0_attn_dt_w, f32).reshape(DI, 1)
    dtb = np.asarray(L0_attn_dt_b, f32).reshape(DI, 1)
    alog = np.ascontiguousarray(np.asarray(L0_attn_A_log, f32))
    dparam = np.asarray(L0_attn_D_param, f32).reshape(DI, 1)
    fbias = np.asarray(L0_ffn_bias, f32).reshape(DM, 1)
    sin = np.asarray(L0_ffn_input_scale, f32).reshape(1, 1)
    sw = np.asarray(L0_ffn_weight_scale, f32).reshape(1, 1)
    ident = np.eye(128, dtype=f32)

    global _NC_CACHE
    if _debug:
        nc = build(debug=True)
    else:
        if _NC_CACHE is None:
            _NC_CACHE = build()
        nc = _NC_CACHE

    in_maps = []
    for k in range(8):
        b, h = k // 2, k % 2
        t0 = h * TOK
        xs_ = np.ascontiguousarray(x[b, t0:t0 + TOK, :])
        xh_ = (np.zeros((3, DM), f32) if h == 0
               else np.ascontiguousarray(x[b, t0 - 3:t0, :]))
        dh0 = h * DH
        ridx_ = np.array([
            [(2 * b * AGROWS + h * DH) * (TOK // 2)],
            [(2 * b * AGROWS + DI) * (TOK // 2)],
            [2 * b * 512 * L + h * TOK],
            [k * AGROWS * TOK]], dtype=np.int32)
        in_maps.append(dict(
            xs=xs_, xsT=np.ascontiguousarray(xs_.T), xh=xh_, identin=ident,
            winT=winT, wxT=wxT, woutT=woutT,
            wfT=wfT, cw=cw, cb=cb,
            dtw_s=np.ascontiguousarray(dtw[dh0:dh0 + DH]),
            dtb_s=np.ascontiguousarray(dtb[dh0:dh0 + DH]),
            alog_s=np.ascontiguousarray(alog[dh0:dh0 + DH]),
            dparam=dparam, fbias=fbias, sin=sin, sw=sw, ridx=ridx_))
    res = run_bass_kernel_spmd(nc, in_maps, core_ids=list(range(8)))
    out = np.empty((B, L, DM), f32)
    for k in range(8):
        b, h = k // 2, k % 2
        out[b, h * TOK:(h + 1) * TOK, :] = np.asarray(res.results[k]["out_d"]).T
    if _debug:
        return out, res
    return out


# version 25
# speedup vs baseline: 1.0156x; 1.0156x over previous
import contextlib
import numpy as np
import concourse.bass as bass
import concourse.mybir as mybir
from concourse.bass_utils import run_bass_kernel_spmd

F32 = mybir.dt.float32
BF16 = mybir.dt.bfloat16
U8 = mybir.dt.uint8
I32 = mybir.dt.int32
AF = mybir.ActivationFunctionType
ALU = mybir.AluOpType
AXX = mybir.AxisListType.X

B, L, DM, E, DI, N = 4, 8192, 1024, 4096, 2048, 16
TOK = 4096
T1 = 256
NS1 = TOK // T1
Q2 = 1024
NC2 = L // Q2
T3 = 256
NS3 = TOK // T3
DH = DI // 2
AGROWS = 2084
K1C = float(2 ** 20 + 1)
C2C = float(3 * 2 ** 13)

SECTIONS = ("v", "s", "t", "g", "q")
SECNAME = {"v": "vector", "s": "scalar", "t": "tensor", "g": "gpsimd", "q": "sync"}


class Op:
    __slots__ = ("eng", "fn", "waits", "sem_key", "inc", "group")

    def __init__(self, eng, fn, waits, sem_key, inc, group=False):
        self.eng, self.fn, self.waits = eng, fn, waits
        self.sem_key, self.inc, self.group = sem_key, inc, group


class Plan:
    def __init__(self):
        self.ops = []
        self.count = {}
        self.waited = {}
        self.writer = {}
        self.readers = {}
        self.n_colls = 0
        self._pending = {}

    def _need(self, sec, waits, stream, value):
        if stream is None or value <= 0:
            return
        if stream.startswith("D:"):
            value = self.count[stream]
        key = (sec, stream)
        if self.waited.get(key, 0) >= value:
            return
        waits.append((stream, value))
        self.waited[key] = value

    def _dep_waits(self, sec, reads, writes):
        waits = []
        pend = self._pending.pop(sec, None)
        if pend:
            waits.extend(pend)
        for b in reads:
            w = self.writer.get(b)
            if w is not None:
                self._need(sec, waits, w[0], w[1])
        for b in writes:
            w = self.writer.get(b)
            if w is not None:
                self._need(sec, waits, w[0], w[1])
            for pe, pc in self.readers.get(b, {}).items():
                self._need(sec, waits, pe, pc)
        if len(waits) > 1:
            best = {}
            for stream, value in waits:
                if best.get(stream, -1) < value:
                    best[stream] = value
            waits = list(best.items())
        return waits

    def _commit(self, stream, inc, reads, writes):
        self.count[stream] = self.count.get(stream, 0) + inc
        c = self.count[stream]
        for b in writes:
            self.writer[b] = (stream, c)
            self.readers[b] = {}
        for b in reads:
            self.readers.setdefault(b, {})[stream] = c

    def emit(self, eng, fn, reads=(), writes=(), dma=False, fam=None):
        if dma:
            assert fam is not None, "dma emit requires fam"
            stream = "D:" + fam
        else:
            stream = eng
        waits = self._dep_waits(eng, reads, writes)
        self._commit(stream, 16 if dma else 1, reads, writes)
        self.ops.append(Op(eng, fn, waits, stream, 16 if dma else 1))

    def emit_group(self, eng, fn, reads=(), writes=()):
        waits = self._dep_waits(eng, reads, writes)
        self._commit(eng, 1, reads, writes)
        self.ops.append(Op(eng, fn, waits, eng, 1, group=True))

    def coll(self, fn, reads=(), writes=()):
        name = f"coll{self.n_colls}"
        self.n_colls += 1
        waits = self._dep_waits("g", reads, writes)
        self.count[name] = 1
        for b in writes:
            self.writer[b] = (name, 1)
            self.readers[b] = {}
        for b in reads:
            self.readers.setdefault(b, {})[name] = 1
        self.ops.append(Op("g", fn, waits, name, 1))

    def fence(self, secs=SECTIONS):
        snapshot = dict(self.count)
        for sec in secs:
            waits = []
            for stream, value in snapshot.items():
                self._need(sec, waits, stream, value)
            self._pending.setdefault(sec, []).extend(waits)

    def replay(self, nc, block, es):
        sems = {s: es.enter_context(nc.semaphore(f"S_{s}")) for s in self.count}
        by_sec = {s: [] for s in SECTIONS}
        for op in self.ops:
            by_sec[op.eng].append(op)

        def body_for(sec):
            ops = by_sec[sec]

            def body(h):
                for op in ops:
                    for stream, value in op.waits:
                        h.wait_ge(sems[stream], value)
                    if op.group:
                        op.fn(h)[-1].then_inc(sems[op.sem_key], op.inc)
                    else:
                        op.fn(h).then_inc(sems[op.sem_key], op.inc)
            return body

        for sec in SECTIONS:
            if by_sec[sec]:
                getattr(block, SECNAME[sec])(body_for(sec))


def build(debug=False):
    nc = bass.Bass("TRN2", target_bir_lowering=False, debug=False)
    ins = {}

    def I(name, shape, dtype=F32):
        ins[name] = nc.dram_tensor(name, shape, dtype, kind="ExternalInput")

    I("xs", [TOK, DM]); I("xsT", [DM, TOK]); I("xh", [3, DM])
    I("identin", [128, 128])
    I("winT", [DM, E]); I("wxT", [DI, 33]); I("woutT", [DI, DM]); I("wfT", [DM, DM])
    I("cw", [DI, 4]); I("cb", [DI, 1])
    I("dtw_s", [DH, 1]); I("dtb_s", [DH, 1]); I("alog_s", [DH, N])
    I("dparam", [DI, 1]); I("fbias", [DM, 1]); I("sin", [1, 1]); I("sw", [1, 1])
    I("ridx", [4, 1], I32)
    out_d = nc.dram_tensor("out_d", [DM, TOK], F32, kind="ExternalOutput")

    ag_inA = nc.dram_tensor("ag_inA", [AGROWS, TOK // 2], BF16)
    ag_inB = nc.dram_tensor("ag_inB", [AGROWS, TOK // 2], BF16)
    ag_outA = nc.dram_tensor("ag_outA", [8 * AGROWS, TOK // 2], BF16, addr_space="Shared")
    ag_outB = nc.dram_tensor("ag_outB", [8 * AGROWS, TOK // 2], BF16, addr_space="Shared")
    agy_inA = nc.dram_tensor("agy_inA", [DH // 2, L], BF16)
    agy_inB = nc.dram_tensor("agy_inB", [DH // 2, L], BF16)
    agy_outA = nc.dram_tensor("agy_outA", [4 * DH, L], BF16, addr_space="Shared")
    agy_outB = nc.dram_tensor("agy_outB", [4 * DH, L], BF16, addr_space="Shared")
    zs = nc.dram_tensor("zs", [DI, TOK], BF16)
    qs = nc.dram_tensor("qs", [DM, TOK], BF16)
    ssw_d = nc.dram_tensor("ssw_d", [1, 2], F32)
    xcp = nc.dram_tensor("xcp", [DH, L], BF16)
    projp = nc.dram_tensor("projp", [33, L], BF16)
    yp = nc.dram_tensor("yp", [DI, TOK], BF16)
    dregs = nc.dram_tensor("dregs", [1, 8], I32, kind="ExternalOutput") if debug else None
    if debug:
        d_y2 = nc.dram_tensor("d_y2", [DI, TOK], BF16, kind="ExternalOutput")
        d_x3d = nc.dram_tensor("d_x3d", [DM, TOK], F32, kind="ExternalOutput")
        d_e1 = nc.dram_tensor("d_e1", [DM, TOK], F32, kind="ExternalOutput")
        d_e2 = nc.dram_tensor("d_e2", [DM, TOK], F32, kind="ExternalOutput")
    dbg = {}
    if debug:
        for nm, t in [("d_aginA", ag_inA), ("d_aginB", ag_inB),
                      ("d_agyinA", agy_inA), ("d_agyinB", agy_inB), ("d_zs", zs),
                      ("d_qs", qs), ("d_ssw", ssw_d), ("d_xcp", xcp),
                      ("d_projp", projp), ("d_yp", yp)]:
            dbg[nm] = (nc.dram_tensor(nm, list(t.shape), t.dtype,
                                      kind="ExternalOutput"), t)

    pl = Plan()
    sb = {}
    REG = {}

    es = contextlib.ExitStack()
    with es:
        def ST(name, shape, dtype=F32, stack=es):
            sb[name] = stack.enter_context(nc.sbuf_tensor(name, shape, dtype))

        for n_ in ["eps", "mhalf", "one", "invs", "sswb"]:
            ST(n_, [128, 1])
        ST("sinb", [1, 1]); ST("swb", [1, 1])
        ST("s_ridx", [1, 4], I32)
        ST("identf", [128, 128])
        for d in range(8):
            ST(f"dtw{d}", [128, 1]); ST(f"dtb{d}", [128, 1])
            ST(f"negA{d}", [128, N]); ST(f"carry{d}", [128, N])
        psum = [es.enter_context(nc.psum_tensor(f"pb{i}", [128, 512], F32))
                for i in range(8)]

        def EMIT(eng, fn, reads=(), writes=(), dma=False, fam=None):
            pl.emit(eng, fn, reads, writes, dma, fam)

        EMIT("q", lambda h: h.dma_start(out=sb["s_ridx"][:], in_=ins["ridx"].rearrange("a b -> b a")),
             writes=["s_ridx"], dma=True, fam="ridx")
        EMIT("q", lambda h: h.dma_start(out=sb["identf"][:], in_=ins["identin"][:, :]),
             writes=["identf"], dma=True, fam="identf")
        EMIT("v", lambda h: h.memset(sb["eps"][:], 1e-5), writes=["eps"])
        EMIT("v", lambda h: h.memset(sb["mhalf"][:], -0.5), writes=["mhalf"])
        EMIT("v", lambda h: h.memset(sb["one"][:], 1.0), writes=["one"])
        EMIT("q", lambda h: h.dma_start(out=sb["sinb"][:], in_=ins["sin"][:, :]),
             writes=["sinb"], dma=True, fam="scales")
        EMIT("q", lambda h: h.dma_start(out=sb["swb"][:], in_=ins["sw"][:, :]),
             writes=["swb"], dma=True, fam="scales")
        EMIT("v", lambda h: h.tensor_tensor(out=sb["swb"][:], in0=sb["sinb"][:],
             in1=sb["swb"][:], op=ALU.mult), reads=["sinb", "swb"], writes=["swb"])
        EMIT("v", lambda h: h.reciprocal(out=sb["sinb"][:], in_=sb["sinb"][:]),
             reads=["sinb"], writes=["sinb"])
        EMIT("g", lambda h: h.dma_start(out=ssw_d.ap()[0:1, 0:1], in_=sb["sinb"][:]),
             reads=["sinb"], writes=["ssw_d"], dma=True, fam="sswd")
        EMIT("g", lambda h: h.dma_start(out=ssw_d.ap()[0:1, 1:2], in_=sb["swb"][:]),
             reads=["ssw_d", "swb"], writes=["ssw_d"], dma=True, fam="sswd")
        EMIT("q", lambda h: h.dma_start(
            out=sb["invs"][:], in_=bass.AP(ssw_d, 0, [[0, 128], [1, 1]])),
            reads=["ssw_d"], writes=["invs"], dma=True, fam="consts2")
        EMIT("q", lambda h: h.dma_start(
            out=sb["sswb"][:], in_=bass.AP(ssw_d, 1, [[0, 128], [1, 1]])),
            reads=["ssw_d"], writes=["sswb"], dma=True, fam="consts2")

        def init_regs(h):
            insts = []
            for i, nm in enumerate(["r_xc", "r_pj", "r_y", "r_ow"]):
                REG[nm] = h.alloc_register(nm)
                insts.append(h.reg_load(REG[nm], sb["s_ridx"][0:1, i:i + 1]))
            REG["tmp"] = h.alloc_register("tmp")
            if debug:
                for i, nm in enumerate(["r_xc", "r_pj", "r_y", "r_ow"]):
                    insts.append(h.reg_save(dregs.ap()[0:1, i:i + 1], REG[nm]))
            insts.append(h.nop())
            return insts
        pl.emit_group("q", init_regs, reads=["s_ridx"])

        def dyn_copy(dst_ap, src_tensor, base, static_off, pattern, reads, writes,
                     fam="dyncp"):
            def fn(h):
                h.reg_mov(REG["tmp"], int(static_off))
                h.reg_add(REG["tmp"], REG["tmp"], REG[base])
                return h.dma_start(out=dst_ap,
                                   in_=bass.AP(src_tensor, REG["tmp"], pattern))
            EMIT("q", fn, reads=reads, writes=writes, dma=True, fam=fam)

        ST("alod", [128, N])
        ST("ident16", [128, 128], BF16)
        EMIT("v", lambda h: h.tensor_copy(out=sb["ident16"][:], in_=sb["identf"][:]),
             reads=["identf"], writes=["ident16"])

        for d in range(8):
            EMIT("q", (lambda dd=d: (lambda h: h.dma_start(
                out=sb["alod"][:],
                in_=ins["alog_s"][dd * 128:(dd + 1) * 128, :])))(),
                writes=["alod"], dma=True, fam="alod")
            EMIT("s", (lambda dd=d: (lambda h: h.activation(
                out=sb[f"negA{dd}"][:], in_=sb["alod"][:], func=AF.Exp)))(),
                reads=["alod"], writes=[f"negA{d}"])
            EMIT("v", (lambda dd=d: (lambda h: h.tensor_scalar(
                out=sb[f"negA{dd}"][:], in0=sb[f"negA{dd}"][:], scalar1=-1.0,
                scalar2=None, op0=ALU.mult)))(),
                reads=[f"negA{d}"], writes=[f"negA{d}"])
            EMIT("q", (lambda dd=d: (lambda h: h.dma_start(
                out=sb[f"dtw{dd}"][:],
                in_=ins["dtw_s"][dd * 128:(dd + 1) * 128, :])))(),
                writes=[f"dtw{d}"], dma=True, fam="consts3")
            EMIT("q", (lambda dd=d: (lambda h: h.dma_start(
                out=sb[f"dtb{dd}"][:],
                in_=ins["dtb_s"][dd * 128:(dd + 1) * 128, :])))(),
                writes=[f"dtb{d}"], dma=True, fam="consts3")
            EMIT("v", (lambda dd=d: (lambda h: h.memset(sb[f"carry{dd}"][:], 0.0)))(),
                 writes=[f"carry{d}"])

        p1 = contextlib.ExitStack()
        with p1:
            def S1(name, shape, dtype=F32):
                ST(name, shape, dtype, stack=p1)

            for i in range(2):
                for sp in range(2):
                    S1(f"xT{i}_{sp}", [128, DM])
                    S1(f"mean{i}_{sp}", [128, 1]); S1(f"ex2{i}_{sp}", [128, 1])
                    S1(f"var{i}_{sp}", [128, 1]); S1(f"rstd{i}_{sp}", [128, 1])
                    S1(f"nmr{i}_{sp}", [128, 1])
                S1(f"sq{i}", [128, DM])
            for k in range(8):
                S1(f"winT{k}", [128, E], BF16)
                S1(f"hxnd16_{k}", [128, 3], BF16)
            for sp in range(2):
                S1(f"xndC_{sp}", [128, 8 * T1]); S1(f"xndC16_{sp}", [128, 8 * T1], BF16)
                S1(f"q16C_{sp}", [128, 8 * T1], BF16)
            for d in range(16):
                S1(f"wxT{d}", [128, 33], BF16)
                S1(f"xss{d}", [128, T1 + 3], BF16)
            S1("xcw", [128, 16 * T1], BF16)
            S1("z16w", [128, 16 * T1], BF16)
            S1("wld", [128, 1024])
            S1("cacc_0", [128, 8 * T1]); S1("cacc_1", [128, 8 * T1])
            S1("proj_0", [33, T1], BF16); S1("proj_1", [33, T1], BF16)
            S1("qv", [128, 8 * T1]); S1("qc", [128, 8 * T1]); S1("qw", [128, 8 * T1])
            S1("qhi", [128, 8 * T1])
            S1("qm", [128, 8 * T1], U8)
            S1("c2p", [128, 1]); S1("c2n", [128, 1])
            for d in range(16):
                S1(f"cw{d}", [128, 4]); S1(f"cb{d}", [128, 1])

            EMIT("v", lambda h: h.memset(sb["c2p"][:], C2C), writes=["c2p"])
            EMIT("v", lambda h: h.memset(sb["c2n"][:], -C2C), writes=["c2n"])
            for d in range(16):
                EMIT("q", (lambda dd=d: (lambda h: h.dma_start(
                    out=sb[f"cw{dd}"][:], in_=ins["cw"][dd * 128:(dd + 1) * 128, :])))(),
                    writes=[f"cw{d}"], dma=True, fam="consts")
                EMIT("q", (lambda dd=d: (lambda h: h.dma_start(
                    out=sb[f"cb{dd}"][:], in_=ins["cb"][dd * 128:(dd + 1) * 128, :])))(),
                    writes=[f"cb{d}"], dma=True, fam="consts")
            for k in range(8):
                for c4 in range(4):
                    EMIT("q", (lambda kk=k, cc=c4: (lambda h: h.dma_start(
                        out=sb["wld"][:],
                        in_=ins["winT"].ap()[kk * 128:(kk + 1) * 128,
                                             cc * 1024:(cc + 1) * 1024])))(),
                        writes=["wld"], dma=True, fam="wld")
                    EMIT("v", (lambda kk=k, cc=c4: (lambda h: h.tensor_copy(
                        out=sb[f"winT{kk}"][:, cc * 1024:(cc + 1) * 1024],
                        in_=sb["wld"][:])))(),
                        reads=["wld"], writes=[f"winT{k}"])
            for d in range(16):
                EMIT("q", (lambda dd=d: (lambda h: h.dma_start(
                    out=sb["wld"][:, 0:33],
                    in_=ins["wxT"][dd * 128:(dd + 1) * 128, :])))(),
                    writes=["wld"], dma=True, fam="wld")
                EMIT("v", (lambda dd=d: (lambda h: h.tensor_copy(
                    out=sb[f"wxT{dd}"][:], in_=sb["wld"][:, 0:33])))(),
                    reads=["wld"], writes=[f"wxT{d}"])

            def ln_apply(kx, ksq, km, ke, kv, kr, kn):
                kxn = kx
                xT, sq = sb[kx], sb[ksq]
                mean, ex2, var, rstd, nmr = sb[km], sb[ke], sb[kv], sb[kr], sb[kn]
                EMIT("v", lambda h: h.tensor_tensor(out=sq[:], in0=xT[:], in1=xT[:],
                     op=ALU.mult), reads=[kx], writes=[ksq])
                EMIT("v", lambda h: h.reduce_sum(out=mean[:], in_=xT[:], axis=AXX),
                     reads=[kx], writes=[km])
                EMIT("v", lambda h: h.reduce_sum(out=ex2[:], in_=sq[:], axis=AXX),
                     reads=[ksq], writes=[ke])
                EMIT("v", lambda h: h.tensor_scalar(out=mean[:], in0=mean[:],
                     scalar1=1.0 / DM, scalar2=None, op0=ALU.mult),
                     reads=[km], writes=[km])
                EMIT("v", lambda h: h.tensor_scalar(out=ex2[:], in0=ex2[:],
                     scalar1=1.0 / DM, scalar2=None, op0=ALU.mult),
                     reads=[ke], writes=[ke])
                EMIT("v", lambda h: h.scalar_tensor_tensor(out=var[:], in0=mean[:],
                     scalar=mean[:], in1=ex2[:], op0=ALU.mult, op1=ALU.subtract),
                     reads=[km, ke], writes=[kv])
                EMIT("v", lambda h: h.tensor_scalar(out=var[:], in0=var[:],
                     scalar1=-1.0, scalar2=None, op0=ALU.mult),
                     reads=[kv], writes=[kv])
                EMIT("s", lambda h: h.activation(out=var[:], in_=var[:], func=AF.Ln,
                     bias=sb["eps"][:]), reads=[kv, "eps"], writes=[kv])
                EMIT("s", lambda h: h.activation(out=rstd[:], in_=var[:], func=AF.Exp,
                     scale=sb["mhalf"][:]), reads=[kv, "mhalf"], writes=[kr])
                EMIT("v", lambda h: h.scalar_tensor_tensor(out=nmr[:], in0=mean[:],
                     scalar=-1.0, in1=rstd[:], op0=ALU.mult, op1=ALU.mult),
                     reads=[km, kr], writes=[kn])
                EMIT("s", lambda h: h.activation(out=sb[kxn][:], in_=xT[:],
                     func=AF.Identity, scale=rstd[:], bias=nmr[:]),
                     reads=[kx, kr, kn], writes=[kxn])

            EMIT("v", lambda h: h.memset(sb["xT0_0"][:], 0.0), writes=["xT0_0"])
            EMIT("q", lambda h: h.dma_start(out=sb["xT0_0"][0:3, :], in_=ins["xh"][:, :]),
                 reads=["xT0_0"], writes=["xT0_0"], dma=True, fam="xh")
            ln_apply("xT0_0", "sq0", "mean0_0", "ex20_0", "var0_0", "rstd0_0", "nmr0_0")
            for k in range(8):
                pk = k % 2
                EMIT("t", (lambda kk=k, p=pk: (lambda h: h.transpose(
                    out=psum[p][:, 0:128],
                    in_=sb["xT0_0"][:, kk * 128:(kk + 1) * 128],
                    identity=sb["identf"][:])))(),
                    reads=["xT0_0", "identf"], writes=[f"pb{pk}"])
                EMIT("s", (lambda kk=k, p=pk: (lambda h: h.activation(
                    out=sb[f"hxnd16_{kk}"][:], in_=psum[p][:, 0:3],
                    func=AF.Identity)))(),
                    reads=[f"pb{pk}"], writes=[f"hxnd16_{k}"])
            for e in range(16):
                pe_ = 2 + e % 2

                def mmgroup(h, ee=e, p=pe_):
                    return [h.matmul(out=psum[p][:, 0:3],
                                     lhsT=sb[f"winT{k}"][:, ee * 128:(ee + 1) * 128],
                                     rhs=sb[f"hxnd16_{k}"][:],
                                     start=(k == 0), stop=(k == 7))
                            for k in range(8)]
                pl.emit_group("t", mmgroup,
                              reads=[f"hxnd16_{k}" for k in range(8)]
                              + [f"winT{k}" for k in range(8)],
                              writes=[f"pb{pe_}"])
                EMIT("s", (lambda ee=e, p=pe_: (lambda h: h.activation(
                    out=sb[f"xss{ee}"][:, 0:3], in_=psum[p][:, 0:3],
                    func=AF.Identity)))(),
                    reads=[f"pb{pe_}"], writes=[f"xss{e}"])

            def issue_agA():
                aginA_keys = ([f"agin_{s}" for s in range(NS1 // 2)]
                              + [f"agin_{s}_p" for s in range(NS1 // 2)])
                pl.coll(lambda h: h.collective_compute(
                    "AllGather", ALU.bypass, replica_groups=[list(range(8))],
                    ins=[ag_inA.ap().opt()], outs=[ag_outA.ap().opt()]),
                    reads=aginA_keys, writes=["ag_outA"])

            for s1 in range(NS1):
                if s1 == NS1 // 2:
                    issue_agA()
                c0 = s1 * T1
                sp = s1 % 2
                xTn = [f"xT{i}_{sp}" for i in range(2)]
                xndC = f"xndC_{sp}"
                xndC16 = f"xndC16_{sp}"
                q16C = f"q16C_{sp}"
                xcwn = "xcw"
                caccn = ("cacc_0", "cacc_1")
                projn = f"proj_{sp}"
                for i in range(2):
                    EMIT("q", (lambda ii=i, cc=c0, bx=xTn[i]: (lambda h: h.dma_start(
                        out=sb[bx][:],
                        in_=ins["xs"][cc + ii * 128: cc + (ii + 1) * 128, :])))(),
                        writes=[xTn[i]], dma=True, fam=f"xT{sp}")
                for i in range(2):
                    ln_apply(xTn[i], f"sq{i}", f"mean{i}_{sp}", f"ex2{i}_{sp}",
                             f"var{i}_{sp}", f"rstd{i}_{sp}", f"nmr{i}_{sp}")
                for kc in range(8):
                    p = kc % 2
                    for i in range(2):
                        EMIT("t", (lambda kk=kc, ii=i, pp=p, bx=xTn[i]: (lambda h: h.transpose(
                            out=psum[pp][:, ii * 128:(ii + 1) * 128],
                            in_=sb[bx][:, kk * 128:(kk + 1) * 128],
                            identity=sb["identf"][:])))(),
                            reads=[xTn[i], "identf"], writes=[f"pb{p}"])
                    EMIT("s", (lambda kk=kc, pp=p, bx=xndC: (lambda h: h.activation(
                        out=sb[bx][:, kk * T1:(kk + 1) * T1], in_=psum[pp][:, 0:T1],
                        func=AF.Identity)))(),
                        reads=[f"pb{p}"], writes=[f"{xndC}:{kc}"])
                EMIT("v", (lambda b1=xndC, b2=xndC16: (lambda h: h.tensor_copy(
                    out=sb[b2][:], in_=sb[b1][:])))(),
                    reads=[f"{xndC}:{kc2}" for kc2 in range(8)], writes=[xndC16])
                def conv_d(d):
                    ca = caccn[d % 2]
                    sl = (d // 2) * T1
                    EMIT("v", (lambda dd=d, c_=ca, s_=sl: (lambda h: h.tensor_scalar(
                        out=sb[c_][:, s_:s_ + T1], in0=sb[f"xss{dd}"][:, 0:T1],
                        scalar1=sb[f"cw{dd}"][:, 0:1], scalar2=None, op0=ALU.mult)))(),
                        reads=[f"xss{d}", f"cw{d}"], writes=[f"{ca}:{d}"])
                    for kk in range(1, 4):
                        EMIT("v", (lambda dd=d, k2=kk, c_=ca, s_=sl: (lambda h: h.scalar_tensor_tensor(
                            out=sb[c_][:, s_:s_ + T1], in0=sb[f"xss{dd}"][:, k2:T1 + k2],
                            scalar=sb[f"cw{dd}"][:, k2:k2 + 1], in1=sb[c_][:, s_:s_ + T1],
                            op0=ALU.mult, op1=ALU.add)))(),
                            reads=[f"xss{d}", f"cw{d}", f"{ca}:{d}"], writes=[f"{ca}:{d}"])
                    EMIT("v", (lambda dd=d: (lambda h: h.tensor_copy(
                        out=sb[f"xss{dd}"][:, 0:3],
                        in_=sb[f"xss{dd}"][:, T1:T1 + 3])))(),
                        reads=[f"xss{d}"], writes=[f"xss{d}"])

                def conv_tail(d):
                    ca = caccn[d % 2]
                    sl = (d // 2) * T1
                    EMIT("s", (lambda dd=d, c_=ca, s_=sl, bx=xcwn: (lambda h: h.activation(
                        out=sb[bx][:, dd * T1:(dd + 1) * T1],
                        in_=sb[c_][:, s_:s_ + T1], func=AF.Silu,
                        bias=sb[f"cb{dd}"][:])))(),
                        reads=[f"{ca}:{d}", f"cb{d}"], writes=[f"{xcwn}:{d}"])

                for e in range(32):
                    p = (2, 3, 5, 6)[e % 4]

                    def mmg(h, ee=e, pp=p, bx=xndC16):
                        return [h.matmul(out=psum[pp][:, 0:T1],
                                         lhsT=sb[f"winT{k}"][:, ee * 128:(ee + 1) * 128],
                                         rhs=sb[bx][:, k * T1:(k + 1) * T1],
                                         start=(k == 0), stop=(k == 7))
                                for k in range(8)]
                    pl.emit_group("t", mmg,
                                  reads=[xndC16]
                                  + [f"winT{k}" for k in range(8)],
                                  writes=[f"pb{p}"])
                    if e < 16:
                        EMIT("s", (lambda ee=e, pp=p: (lambda h: h.activation(
                            out=sb[f"xss{ee}"][:, 3:T1 + 3], in_=psum[pp][:, 0:T1],
                            func=AF.Identity)))(),
                            reads=[f"pb{p}"], writes=[f"xss{e}"])
                        conv_d(e)
                    else:
                        EMIT("s", (lambda ee=e, pp=p: (lambda h: h.activation(
                            out=sb["z16w"][:, (ee - 16) * T1:(ee - 15) * T1],
                            in_=psum[pp][:, 0:T1],
                            func=AF.Identity)))(),
                            reads=[f"pb{p}"], writes=[f"z16w:{e - 16}"])
                EMIT("g", (lambda cc=c0: (lambda h: h.dma_start(
                    out=bass.AP(zs, cc, [[TOK, 128], [128 * TOK, 16], [1, T1]]),
                    in_=sb["z16w"][:].rearrange("p (k j) -> p k j", k=16))))(),
                    reads=[f"z16w:{j}" for j in range(16)],
                    writes=[f"zs_{s1}"], dma=True, fam="zs")

                for d in range(16):
                    conv_tail(d)
                agt = ag_inA if s1 < NS1 // 2 else ag_inB
                cl = c0 % (TOK // 2)
                EMIT("g", (lambda cc=cl, at=agt, bx=xcwn: (lambda h: h.dma_start(
                    out=bass.AP(at, cc,
                                [[TOK // 2, 128], [128 * (TOK // 2), 16], [1, T1]]),
                    in_=sb[bx][:].rearrange("p (k j) -> p k j", k=16))))(),
                    reads=[f"{xcwn}:{d}" for d in range(16)],
                    writes=[f"agin_{s1}"], dma=True, fam="agin")
                def wxg(h, bx=xcwn):
                    return [h.matmul(out=psum[4][0:33, 0:T1], lhsT=sb[f"wxT{d}"][:],
                                     rhs=sb[bx][:, d * T1:(d + 1) * T1],
                                     start=(d == 0), stop=(d == 15))
                            for d in range(16)]
                pl.emit_group("t", wxg,
                              reads=[f"{xcwn}:{d}" for d in range(16)]
                              + [f"wxT{d}" for d in range(16)],
                              writes=["pb4"])
                EMIT("v", (lambda pj=projn: (lambda h: h.tensor_copy(out=sb[pj][:],
                     in_=psum[4][0:33, 0:T1])))(),
                     reads=["pb4"], writes=[projn])
                agt2 = ag_inA if s1 < NS1 // 2 else ag_inB
                cl2 = c0 % (TOK // 2)
                EMIT("g", (lambda cc=cl2, pj=projn, at=agt2: (lambda h: h.dma_start(
                    out=at.ap()[DI:DI + 33, cc:cc + T1], in_=sb[pj][:])))(),
                    reads=[projn], writes=[f"agin_{s1}_p"], dma=True, fam="agin")

                EMIT("v", (lambda bx=xndC: (lambda h: h.tensor_scalar(
                    out=sb["qv"][:], in0=sb[bx][:], scalar1=sb["invs"][:],
                    scalar2=-448.0, op0=ALU.mult, op1=ALU.max)))(),
                    reads=[f"{xndC}:{kc2}" for kc2 in range(8)] + ["invs"],
                    writes=["qv"])
                EMIT("v", lambda h: h.tensor_scalar(out=sb["qv"][:], in0=sb["qv"][:],
                     scalar1=448.0, scalar2=None, op0=ALU.min),
                     reads=["qv"], writes=["qv"])
                EMIT("v", lambda h: h.tensor_scalar(out=sb["qc"][:], in0=sb["qv"][:],
                     scalar1=K1C, scalar2=None, op0=ALU.mult),
                     reads=["qv"], writes=["qc"])
                EMIT("v", lambda h: h.scalar_tensor_tensor(out=sb["qw"][:],
                     in0=sb["qv"][:], scalar=K1C, in1=sb["qv"][:],
                     op0=ALU.mult, op1=ALU.subtract),
                     reads=["qv"], writes=["qw"])
                EMIT("v", lambda h: h.tensor_tensor(out=sb["qhi"][:], in0=sb["qc"][:],
                     in1=sb["qw"][:], op=ALU.subtract),
                     reads=["qc", "qw"], writes=["qhi"])
                EMIT("v", lambda h: h.tensor_scalar(out=sb["qc"][:], in0=sb["qv"][:],
                     scalar1=C2C, scalar2=None, op0=ALU.add),
                     reads=["qv"], writes=["qc"])
                EMIT("v", lambda h: h.tensor_scalar(out=sb["qw"][:], in0=sb["qc"][:],
                     scalar1=C2C, scalar2=None, op0=ALU.subtract),
                     reads=["qc"], writes=["qw"])
                EMIT("v", lambda h: h.tensor_tensor(out=sb["qc"][:], in0=sb["qv"][:],
                     in1=sb["qv"][:], op=ALU.mult),
                     reads=["qv"], writes=["qc"])
                EMIT("v", lambda h: h.tensor_scalar(out=sb["qm"][:], in0=sb["qc"][:],
                     scalar1=float(2.0 ** -12), scalar2=None, op0=ALU.is_lt),
                     reads=["qc"], writes=["qm"])
                EMIT("v", lambda h: h.select(out=sb["qc"][:], mask=sb["qm"][:],
                     on_true=sb["qw"][:], on_false=sb["qhi"][:]),
                     reads=["qm", "qw", "qhi"], writes=["qc"])
                EMIT("v", (lambda q=q16C: (lambda h: h.tensor_copy(
                    out=sb[q][:], in_=sb["qc"][:])))(),
                    reads=["qc"], writes=[q16C])
                EMIT("g", (lambda q=q16C, cc=c0: (lambda h: h.dma_start(
                    out=bass.AP(qs, cc, [[TOK, 128], [128 * TOK, 8], [1, T1]]),
                    in_=bass.AP(sb[q].tensor if hasattr(sb[q], "tensor") else None, 0,
                                [[8 * T1, 128], [T1, 8], [1, T1]])
                    if False else sb[q][:].rearrange("p (k j) -> p k j", k=8))))(),
                    reads=[q16C], writes=[f"qs_{s1}"], dma=True, fam="qs")

        pl.fence()
        aginB_keys = ([f"agin_{s}" for s in range(NS1 // 2, NS1)]
                      + [f"agin_{s}_p" for s in range(NS1 // 2, NS1)])
        pl.coll(lambda h: h.collective_compute(
            "AllGather", ALU.bypass, replica_groups=[list(range(8))],
            ins=[ag_inB.ap().opt()], outs=[ag_outB.ap().opt()]),
            reads=aginB_keys, writes=["ag_outB"])
        HT = TOK // 2
        for half in range(2):
            q_ = 2 * half
            dyn_copy(xcp.ap()[:, q_ * HT:(q_ + 1) * HT], ag_outA, "r_xc",
                     half * AGROWS * HT, [[HT, DH], [1, HT]],
                     reads=["ag_outA"], writes=[f"xcp_q{q_}"], fam=f"xcq{q_}")
            dyn_copy(projp.ap()[:, q_ * HT:(q_ + 1) * HT], ag_outA, "r_pj",
                     half * AGROWS * HT, [[HT, 33], [1, HT]],
                     reads=["ag_outA"], writes=[f"projp_q{q_}"], fam=f"pjq{q_}")
        for half in range(2):
            q_ = 2 * half + 1
            dyn_copy(xcp.ap()[:, q_ * HT:(q_ + 1) * HT], ag_outB, "r_xc",
                     half * AGROWS * HT, [[HT, DH], [1, HT]],
                     reads=["ag_outB"], writes=[f"xcp_q{q_}"], fam=f"xcq{q_}")
            dyn_copy(projp.ap()[:, q_ * HT:(q_ + 1) * HT], ag_outB, "r_pj",
                     half * AGROWS * HT, [[HT, 33], [1, HT]],
                     reads=["ag_outB"], writes=[f"projp_q{q_}"], fam=f"pjq{q_}")
        p23 = contextlib.ExitStack()
        with p23:
            def S23(name, shape, dtype=F32):
                ST(name, shape, dtype, stack=p23)

            S23("wld3a", [128, DM]); S23("wld3b", [128, DM])
            for d in range(16):
                S23(f"woT{d}", [128, DM], BF16)
                S23(f"dp{d}", [128, 1])
            for k in range(8):
                S23(f"wfT{k}", [128, DM], BF16)
            for m in range(8):
                S23(f"fb{m}", [128, 1])

            p2 = contextlib.ExitStack()
            with p2:
                def S2(name, shape, dtype=F32):
                    ST(name, shape, dtype, stack=p2)

                for d in range(8):
                    S2(f"dt16_{d}", [128, Q2], BF16)
                    S2(f"u16_{d}", [128, Q2], BF16)
                    S2(f"xc2_{d}", [128, Q2], BF16)
                S2("dtbc", [128, Q2], BF16)
                for i in range(2):
                    S2(f"zp{i}", [128, Q2]); S2(f"rel{i}", [128, Q2])
                    S2(f"nab{i}", [128, Q2]); S2(f"lnb{i}", [128, Q2])
                for p in range(2):
                    S2(f"Bb{p}", [128, Q2], BF16); S2(f"Cb{p}", [128, Q2], BF16)
                    S2(f"dA{p}", [128, Q2]); S2(f"dBx{p}", [128, Q2], BF16)
                    S2(f"hh{p}", [128, Q2], BF16); S2(f"gy{p}", [128, Q2], BF16)
                    S2(f"y16w_{p}", [128, 1024], BF16)
                for d in range(16):
                    EMIT("q", (lambda dd=d: (lambda h: h.dma_start(
                        out=sb[f"dp{dd}"][:],
                        in_=ins["dparam"][dd * 128:(dd + 1) * 128, :])))(),
                        writes=[f"dp{d}"], dma=True, fam="consts4")
                for m in range(8):
                    EMIT("q", (lambda mm=m: (lambda h: h.dma_start(
                        out=sb[f"fb{mm}"][:],
                        in_=ins["fbias"][mm * 128:(mm + 1) * 128, :])))(),
                        writes=[f"fb{m}"], dma=True, fam="consts4")

                for c in range(NC2):
                    EMIT("q", (lambda cc=c: (lambda h: h.dma_start(out=sb["dtbc"][:],
                         in_=bass.AP(projp, cc * Q2, [[0, 128], [1, Q2]]))))(),
                         reads=[f"projp_q{c // 2}"], writes=["dtbc"], dma=True, fam="dtbc")
                    for d in range(8):
                        EMIT("q", (lambda dd=d, cc=c: (lambda h: h.dma_start(
                            out=sb[f"xc2_{dd}"][:],
                            in_=xcp.ap()[dd * 128:(dd + 1) * 128,
                                         cc * Q2:(cc + 1) * Q2])))(),
                            reads=[f"xcp_q{c // 2}"], writes=[f"xc2_{d}"], dma=True, fam="xc2")
                    for j in range(3):
                        w = 3 * c + j
                        stg = "wld3a" if w % 2 == 0 else "wld3b"
                        famw = "wl3a" if w % 2 == 0 else "wl3b"
                        if w < 16:
                            srct, row, dstb = ins["woutT"], w, f"woT{w}"
                        else:
                            srct, row, dstb = ins["wfT"], w - 16, f"wfT{w - 16}"
                        EMIT("q", (lambda s_=srct, r_=row, g_=stg: (lambda h: h.dma_start(
                            out=sb[g_][:], in_=s_[r_ * 128:(r_ + 1) * 128, :])))(),
                            writes=[stg], dma=True, fam=famw)
                        EMIT("g", (lambda g_=stg, b_=dstb: (lambda h: h.tensor_copy(
                            out=sb[b_][:], in_=sb[g_][:])))(),
                            reads=[stg], writes=[dstb])
                    for d in range(8):
                        i2 = d % 2
                        zp, rel = f"zp{i2}", f"rel{i2}"
                        nab, lnb = f"nab{i2}", f"lnb{i2}"
                        EMIT("s", (lambda dd=d, z=zp: (lambda h: h.activation(
                            out=sb[z][:], in_=sb["dtbc"][:], func=AF.Identity,
                            scale=sb[f"dtw{dd}"][:], bias=sb[f"dtb{dd}"][:])))(),
                            reads=["dtbc", f"dtw{d}", f"dtb{d}"], writes=[zp])
                        EMIT("v", (lambda z=zp, r=rel: (lambda h: h.tensor_scalar(
                            out=sb[r][:], in0=sb[z][:],
                            scalar1=0.0, scalar2=None, op0=ALU.max)))(),
                            reads=[zp], writes=[rel])
                        EMIT("v", (lambda z=zp, nb=nab: (lambda h: h.scalar_tensor_tensor(
                            out=sb[nb][:],
                            in0=sb[z][:], scalar=-1.0, in1=sb[z][:],
                            op0=ALU.mult, op1=ALU.min)))(),
                            reads=[zp], writes=[nab])
                        EMIT("s", (lambda nb=nab: (lambda h: h.activation(
                            out=sb[nb][:], in_=sb[nb][:],
                            func=AF.Exp)))(), reads=[nab], writes=[nab])
                        EMIT("s", (lambda nb=nab, lb=lnb: (lambda h: h.activation(
                            out=sb[lb][:], in_=sb[nb][:],
                            func=AF.Ln, bias=sb["one"][:])))(),
                            reads=[nab, "one"], writes=[lnb])
                        EMIT("v", (lambda dd=d, r=rel, lb=lnb: (lambda h: h.tensor_tensor(
                            out=sb[f"dt16_{dd}"][:], in0=sb[r][:], in1=sb[lb][:],
                            op=ALU.add)))(),
                            reads=[rel, lnb], writes=[f"dt16_{d}"])
                        EMIT("v", (lambda dd=d: (lambda h: h.tensor_tensor(
                            out=sb[f"u16_{dd}"][:], in0=sb[f"dt16_{dd}"][:],
                            in1=sb[f"xc2_{dd}"][:], op=ALU.mult)))(),
                            reads=[f"dt16_{d}", f"xc2_{d}"], writes=[f"u16_{d}"])
                    for dgrp in range(2):
                        d4 = dgrp * 4
                        for n in range(N):
                            p = n % 2
                            EMIT("q", (lambda nn=n, pp=p, cc=c: (lambda h: h.dma_start(
                                out=sb[f"Bb{pp}"][:],
                                in_=bass.AP(projp, (1 + nn) * L + cc * Q2,
                                            [[0, 128], [1, Q2]]))))(),
                                reads=[f"projp_q{c // 2}"], writes=[f"Bb{p}"], dma=True, fam=f"bb{p}")
                            EMIT("q", (lambda nn=n, pp=p, cc=c: (lambda h: h.dma_start(
                                out=sb[f"Cb{pp}"][:],
                                in_=bass.AP(projp, (17 + nn) * L + cc * Q2,
                                            [[0, 128], [1, Q2]]))))(),
                                reads=[f"projp_q{c // 2}"], writes=[f"Cb{p}"], dma=True, fam=f"cb2{p}")
                            for d in range(d4, d4 + 4):
                                q = (n * 4 + d) % 2
                                EMIT("s", (lambda dd=d, nn=n, qq=q: (lambda h: h.activation(
                                    out=sb[f"dA{qq}"][:], in_=sb[f"dt16_{dd}"][:],
                                    func=AF.Exp,
                                    scale=sb[f"negA{dd}"][:, nn:nn + 1])))(),
                                    reads=[f"dt16_{d}", f"negA{d}"], writes=[f"dA{q}"])
                                EMIT("v", (lambda dd=d, pp=p, qq=q: (lambda h: h.tensor_tensor(
                                    out=sb[f"dBx{qq}"][:], in0=sb[f"u16_{dd}"][:],
                                    in1=sb[f"Bb{pp}"][:], op=ALU.mult)))(),
                                    reads=[f"u16_{d}", f"Bb{p}"], writes=[f"dBx{q}"])
                                EMIT("v", (lambda dd=d, nn=n, qq=q: (lambda h: h.tensor_tensor_scan(
                                    out=sb[f"hh{qq}"][:], data0=sb[f"dA{qq}"][:],
                                    data1=sb[f"dBx{qq}"][:],
                                    initial=sb[f"carry{dd}"][:, nn:nn + 1],
                                    op0=ALU.mult, op1=ALU.add)))(),
                                    reads=[f"dA{q}", f"dBx{q}", f"carry{d}"],
                                    writes=[f"hh{q}"])
                                EMIT("v", (lambda dd=d, nn=n, qq=q: (lambda h: h.tensor_copy(
                                    out=sb[f"carry{dd}"][:, nn:nn + 1],
                                    in_=sb[f"hh{qq}"][:, Q2 - 1:Q2])))(),
                                    reads=[f"hh{q}"], writes=[f"carry{d}"])
                                EMIT("v", (lambda dd=d, pp=p, qq=q: (lambda h: h.tensor_tensor(
                                    out=sb[f"gy{qq}"][:], in0=sb[f"hh{qq}"][:],
                                    in1=sb[f"Cb{pp}"][:], op=ALU.mult)))(),
                                    reads=[f"hh{q}", f"Cb{p}"], writes=[f"gy{q}"])
                                for hf in range(2):
                                    bk = 2 * (d - d4) + hf
                                    EMIT("t", (lambda dd=d, nn=n, qq=q, hh2=hf, bb=bk:
                                         (lambda h: h.matmul(
                                            out=psum[bb][:],
                                            lhsT=sb["ident16"][:],
                                            rhs=sb[f"gy{qq}"][:, hh2 * 512:(hh2 + 1) * 512],
                                            start=(nn == 0), stop=(nn == N - 1),
                                            skip_group_check=True)))(),
                                        reads=[f"gy{q}", "ident16"], writes=[f"pb{bk}"])
                        for d in range(d4, d4 + 4):
                            yb = f"y16w_{d % 2}"
                            for hf in range(2):
                                bk = 2 * (d - d4) + hf
                                EMIT("s", (lambda bb=bk, y_=yb, hh2=hf: (lambda h: h.activation(
                                    out=sb[y_][:, hh2 * 512:(hh2 + 1) * 512],
                                    in_=psum[bb][:],
                                    func=AF.Identity)))(),
                                    reads=[f"pb{bk}"], writes=[f"{yb}:{hf}"])
                            agt3 = agy_inA if d < 4 else agy_inB
                            EMIT("g", (lambda dd=d, y_=yb, cc=c, at=agt3: (lambda h: h.dma_start(
                                out=at.ap()[(dd % 4) * 128:(dd % 4 + 1) * 128,
                                            cc * Q2:(cc + 1) * Q2],
                                in_=sb[y_][:])))(),
                                reads=[f"{yb}:0", f"{yb}:1"], writes=[f"agy_{c}_{d}"],
                                dma=True, fam=f"agy{d % 2}")
                        if c == NC2 - 1 and dgrp == 0:
                            agyA_keys = [f"agy_{cc}_{dd}" for cc in range(NC2)
                                         for dd in range(4)]
                            pl.coll(lambda h: h.collective_compute(
                                "AllGather", ALU.bypass, replica_groups=[list(range(8))],
                                ins=[agy_inA.ap().opt()], outs=[agy_outA.ap().opt()]),
                                reads=agyA_keys, writes=["agy_outA"])

            pl.fence()
            agyB_keys = [f"agy_{c}_{d}" for c in range(NC2) for d in range(4, 8)]
            pl.coll(lambda h: h.collective_compute(
                "AllGather", ALU.bypass, replica_groups=[list(range(8))],
                ins=[agy_inB.ap().opt()], outs=[agy_outB.ap().opt()]),
                reads=agyB_keys, writes=["agy_outB"])
            for half in range(2):
                dyn_copy(yp.ap()[half * DH:half * DH + 512, :], agy_outA, "r_y",
                         half * 512 * L, [[L, 512], [1, TOK]],
                         reads=["agy_outA"], writes=[f"yp_q{2 * half}"],
                         fam=f"ypq{2 * half}")
            for half in range(2):
                dyn_copy(yp.ap()[half * DH + 512:(half + 1) * DH, :], agy_outB, "r_y",
                         half * 512 * L, [[L, 512], [1, TOK]],
                         reads=["agy_outB"], writes=[f"yp_q{2 * half + 1}"],
                         fam=f"ypq{2 * half + 1}")
            p3 = contextlib.ExitStack()
            with p3:
                def S3(name, shape, dtype=F32):
                    ST(name, shape, dtype, stack=p3)

                for sp in range(2):
                    S3(f"y3w_{sp}", [128, 16 * T3], BF16)
                    S3(f"xc3w_{sp}", [128, 16 * T3], BF16)
                    S3(f"z3w_{sp}", [128, 16 * T3], BF16)
                    S3(f"q3w_{sp}", [128, 8 * T3], BF16)
                    S3(f"x3w_{sp}", [128, 8 * T3])
                for p in range(8):
                    S3(f"e1_{p}", [128, T3])
                for p in range(2):
                    S3(f"e2_{p}", [128, T3])
                    S3(f"odw_{p}", [128, 8 * T3])

                for s3 in range(NS3):
                    c0 = s3 * T3
                    sp = s3 % 2
                    y3w, xc3w, z3w = f"y3w_{sp}", f"xc3w_{sp}", f"z3w_{sp}"
                    q3w, x3w = f"q3w_{sp}", f"x3w_{sp}"
                    EMIT("q", (lambda cc=c0, b=q3w: (lambda h: h.dma_start(
                        out=sb[b][:].rearrange("p (k j) -> p k j", k=8),
                        in_=bass.AP(qs, cc, [[TOK, 128], [128 * TOK, 8], [1, T3]]))))(),
                        reads=["qsall"], writes=[f"{q3w}:{k}" for k in range(8)],
                        dma=True, fam=f"q3{sp}")
                    EMIT("q", (lambda cc=c0, b=x3w: (lambda h: h.dma_start(
                        out=sb[b][:].rearrange("p (k j) -> p k j", k=8),
                        in_=bass.AP(ins["xsT"], cc,
                                    [[TOK, 128], [128 * TOK, 8], [1, T3]]))))(),
                        writes=[f"{x3w}:{m}" for m in range(8)],
                        dma=True, fam=f"x3{sp}")
                    EMIT("q", (lambda cc=c0, b=y3w: (lambda h: h.dma_start(
                        out=sb[b][:].rearrange("p (k j) -> p k j", k=16),
                        in_=bass.AP(yp, cc, [[TOK, 128], [128 * TOK, 16], [1, T3]]))))(),
                        reads=[f"yp_q{qq}" for qq in range(4)],
                        writes=[f"{y3w}:{d}" for d in range(16)],
                        dma=True, fam=f"y3{sp}")
                    ag3 = ag_inA if s3 < NS3 // 2 else ag_inB
                    cl3 = c0 % (TOK // 2)
                    EMIT("q", (lambda cc=cl3, b=xc3w, at=ag3: (lambda h: h.dma_start(
                        out=sb[b][:].rearrange("p (k j) -> p k j", k=16),
                        in_=bass.AP(at, cc,
                                    [[TOK // 2, 128], [128 * (TOK // 2), 16], [1, T3]]))))(),
                        reads=["againall"], writes=[f"{xc3w}:{d}" for d in range(16)],
                        dma=True, fam=f"xc3{sp}")
                    EMIT("q", (lambda cc=c0, b=z3w: (lambda h: h.dma_start(
                        out=sb[b][:].rearrange("p (k j) -> p k j", k=16),
                        in_=bass.AP(zs, cc, [[TOK, 128], [128 * TOK, 16], [1, T3]]))))(),
                        reads=["zsall"], writes=[f"{z3w}:{d}" for d in range(16)],
                        dma=True, fam=f"z3{sp}")
                    for m in range(8):
                        pf = (4, 7)[m % 2]
                        pe1 = m

                        def ffg(h, mm=m, pp=pf, b=q3w):
                            return [h.matmul(out=psum[pp][:, 0:T3],
                                             lhsT=sb[f"wfT{k}"][:, mm * 128:(mm + 1) * 128],
                                             rhs=sb[b][:, k * T3:(k + 1) * T3],
                                             start=(k == 0), stop=(k == 7))
                                    for k in range(8)]
                        pl.emit_group("t", ffg,
                                      reads=[f"{q3w}:{k}" for k in range(8)]
                                      + [f"wfT{k}" for k in range(8)],
                                      writes=[f"pb{pf}"])
                        EMIT("s", (lambda mm=m, pf_=pf, p_=pe1: (lambda h: h.activation(
                            out=sb[f"e1_{p_}"][:], in_=psum[pf_][:, 0:T3],
                            func=AF.Identity, scale=sb["sswb"][:],
                            bias=sb[f"fb{mm}"][:])))(),
                            reads=[f"pb{pf}", "sswb", f"fb{m}"], writes=[f"e1_{pe1}"])
                    for d in range(16):
                        sl = slice(d * T3, (d + 1) * T3)
                        EMIT("s", (lambda b=z3w, s_=sl: (lambda h: h.activation(
                            out=sb[b][:, s_], in_=sb[b][:, s_], func=AF.Silu)))(),
                            reads=[f"{z3w}:{d}"], writes=[f"{z3w}:{d}"])
                        EMIT("v", (lambda dd=d, by=y3w, bx=xc3w, s_=sl: (lambda h: h.scalar_tensor_tensor(
                            out=sb[by][:, s_], in0=sb[bx][:, s_],
                            scalar=sb[f"dp{dd}"][:], in1=sb[by][:, s_],
                            op0=ALU.mult, op1=ALU.add)))(),
                            reads=[f"{xc3w}:{d}", f"dp{d}", f"{y3w}:{d}"],
                            writes=[f"{y3w}:{d}"])
                        EMIT("v", (lambda by=y3w, bz=z3w, s_=sl: (lambda h: h.tensor_tensor(
                            out=sb[by][:, s_], in0=sb[by][:, s_],
                            in1=sb[bz][:, s_], op=ALU.mult)))(),
                            reads=[f"{y3w}:{d}", f"{z3w}:{d}"], writes=[f"{y3w}:{d}"])
                    for m in range(8):
                        pw = (2, 3, 5, 6)[m % 4]
                        pe1 = m
                        pp = m % 2

                        def wog(h, mm=m, pp=pw, b=y3w):
                            return [h.matmul(out=psum[pp][:, 0:T3],
                                             lhsT=sb[f"woT{d}"][:, mm * 128:(mm + 1) * 128],
                                             rhs=sb[b][:, d * T3:(d + 1) * T3],
                                             start=(d == 0), stop=(d == 15))
                                    for d in range(16)]
                        pl.emit_group("t", wog,
                                      reads=[f"{y3w}:{d}" for d in range(16)]
                                      + [f"woT{d}" for d in range(16)],
                                      writes=[f"pb{pw}"])
                        EMIT("v", (lambda mm=m, pw_=pw, p_=pp, pe=pe1: (lambda h: h.tensor_tensor(
                            out=sb[f"e2_{p_}"][:], in0=psum[pw_][:, 0:T3],
                            in1=sb[f"e1_{pe}"][:], op=ALU.add)))(),
                            reads=[f"pb{pw}", f"e1_{pe1}"], writes=[f"e2_{pp}"])
                        EMIT("v", (lambda mm=m, p_=sp, bx=x3w, pe=pp: (lambda h: h.tensor_tensor(
                            out=sb[f"odw_{p_}"][:, mm * T3:(mm + 1) * T3],
                            in0=sb[f"e2_{pe}"][:],
                            in1=sb[bx][:, mm * T3:(mm + 1) * T3], op=ALU.add)))(),
                            reads=[f"e2_{pp}", f"{x3w}:{m}"], writes=[f"odw_{sp}:{m}"])
                    EMIT("g", (lambda p_=sp, cc=c0: (lambda h: h.dma_start(
                        out=bass.AP(out_d, cc, [[TOK, 128], [128 * TOK, 8], [1, T3]]),
                        in_=sb[f"odw_{p_}"][:].rearrange("p (k j) -> p k j", k=8))))(),
                        reads=[f"odw_{sp}:{m}" for m in range(8)],
                        writes=[f"out_{s3}"], dma=True, fam="outd")

        if debug:
            pl.fence(("q",))
            for nm, (dst, srct) in dbg.items():
                pl.emit("q", (lambda d_=dst, s_=srct: (lambda h: h.dma_start(
                    out=d_.ap()[:, :], in_=s_.ap()[:, :])))(),
                    reads=[], writes=[nm], dma=True, fam="outd")
        pl.fence(("g",))
        pl.emit("g", lambda h: h.nop())

        global _LAST_PLAN
        _LAST_PLAN = pl
        with nc.Block() as block:
            pl.replay(nc, block, es)
    return nc


_NC_CACHE = None
_LAST_PLAN = None


def kernel(L0_attn_x, L0_attn_W_in, L0_attn_conv_w, L0_attn_conv_b, L0_attn_W_x,
           L0_attn_dt_w, L0_attn_dt_b, L0_attn_A_log, L0_attn_D_param,
           L0_attn_W_out, L0_ffn_weight_fp8, L0_ffn_bias, L0_ffn_input_scale,
           L0_ffn_weight_scale, _debug=False):
    f32 = np.float32
    x = np.asarray(L0_attn_x, f32)
    winT = np.ascontiguousarray(np.asarray(L0_attn_W_in, f32).T)
    wxT = np.ascontiguousarray(np.asarray(L0_attn_W_x, f32).T)
    woutT = np.ascontiguousarray(np.asarray(L0_attn_W_out, f32).T)
    wfT = np.ascontiguousarray(np.asarray(L0_ffn_weight_fp8, f32).T)
    cw = np.ascontiguousarray(np.asarray(L0_attn_conv_w, f32))
    cb = np.asarray(L0_attn_conv_b, f32).reshape(DI, 1)
    dtw = np.asarray(L0_attn_dt_w, f32).reshape(DI, 1)
    dtb = np.asarray(L0_attn_dt_b, f32).reshape(DI, 1)
    alog = np.ascontiguousarray(np.asarray(L0_attn_A_log, f32))
    dparam = np.asarray(L0_attn_D_param, f32).reshape(DI, 1)
    fbias = np.asarray(L0_ffn_bias, f32).reshape(DM, 1)
    sin = np.asarray(L0_ffn_input_scale, f32).reshape(1, 1)
    sw = np.asarray(L0_ffn_weight_scale, f32).reshape(1, 1)
    ident = np.eye(128, dtype=f32)

    global _NC_CACHE
    if _debug:
        nc = build(debug=True)
    else:
        if _NC_CACHE is None:
            _NC_CACHE = build()
        nc = _NC_CACHE

    in_maps = []
    for k in range(8):
        b, h = k // 2, k % 2
        t0 = h * TOK
        xs_ = np.ascontiguousarray(x[b, t0:t0 + TOK, :])
        xh_ = (np.zeros((3, DM), f32) if h == 0
               else np.ascontiguousarray(x[b, t0 - 3:t0, :]))
        dh0 = h * DH
        ridx_ = np.array([
            [(2 * b * AGROWS + h * DH) * (TOK // 2)],
            [(2 * b * AGROWS + DI) * (TOK // 2)],
            [2 * b * 512 * L + h * TOK],
            [k * AGROWS * TOK]], dtype=np.int32)
        in_maps.append(dict(
            xs=xs_, xsT=np.ascontiguousarray(xs_.T), xh=xh_, identin=ident,
            winT=winT, wxT=wxT, woutT=woutT,
            wfT=wfT, cw=cw, cb=cb,
            dtw_s=np.ascontiguousarray(dtw[dh0:dh0 + DH]),
            dtb_s=np.ascontiguousarray(dtb[dh0:dh0 + DH]),
            alog_s=np.ascontiguousarray(alog[dh0:dh0 + DH]),
            dparam=dparam, fbias=fbias, sin=sin, sw=sw, ridx=ridx_))
    res = run_bass_kernel_spmd(nc, in_maps, core_ids=list(range(8)))
    out = np.empty((B, L, DM), f32)
    for k in range(8):
        b, h = k // 2, k % 2
        out[b, h * TOK:(h + 1) * TOK, :] = np.asarray(res.results[k]["out_d"]).T
    if _debug:
        return out, res
    return out


# version 26
# speedup vs baseline: 1.0269x; 1.0111x over previous
import contextlib
import numpy as np
import concourse.bass as bass
import concourse.mybir as mybir
from concourse.bass_utils import run_bass_kernel_spmd

F32 = mybir.dt.float32
BF16 = mybir.dt.bfloat16
U8 = mybir.dt.uint8
I32 = mybir.dt.int32
AF = mybir.ActivationFunctionType
ALU = mybir.AluOpType
AXX = mybir.AxisListType.X

B, L, DM, E, DI, N = 4, 8192, 1024, 4096, 2048, 16
TOK = 4096
T1 = 256
NS1 = TOK // T1
Q2 = 1024
NC2 = L // Q2
T3 = 256
NS3 = TOK // T3
DH = DI // 2
AGROWS = 2084
K1C = float(2 ** 20 + 1)
C2C = float(3 * 2 ** 13)

SECTIONS = ("v", "s", "t", "g", "q")
SECNAME = {"v": "vector", "s": "scalar", "t": "tensor", "g": "gpsimd", "q": "sync"}


class Op:
    __slots__ = ("eng", "fn", "waits", "sem_key", "inc", "group")

    def __init__(self, eng, fn, waits, sem_key, inc, group=False):
        self.eng, self.fn, self.waits = eng, fn, waits
        self.sem_key, self.inc, self.group = sem_key, inc, group


class Plan:
    def __init__(self):
        self.ops = []
        self.count = {}
        self.waited = {}
        self.writer = {}
        self.readers = {}
        self.n_colls = 0
        self._pending = {}

    def _need(self, sec, waits, stream, value):
        if stream is None or value <= 0:
            return
        if stream.startswith("D:"):
            value = self.count[stream]
        key = (sec, stream)
        if self.waited.get(key, 0) >= value:
            return
        waits.append((stream, value))
        self.waited[key] = value

    def _dep_waits(self, sec, reads, writes):
        waits = []
        pend = self._pending.pop(sec, None)
        if pend:
            waits.extend(pend)
        for b in reads:
            w = self.writer.get(b)
            if w is not None:
                self._need(sec, waits, w[0], w[1])
        for b in writes:
            w = self.writer.get(b)
            if w is not None:
                self._need(sec, waits, w[0], w[1])
            for pe, pc in self.readers.get(b, {}).items():
                self._need(sec, waits, pe, pc)
        if len(waits) > 1:
            best = {}
            for stream, value in waits:
                if best.get(stream, -1) < value:
                    best[stream] = value
            waits = list(best.items())
        return waits

    def _commit(self, stream, inc, reads, writes):
        self.count[stream] = self.count.get(stream, 0) + inc
        c = self.count[stream]
        for b in writes:
            self.writer[b] = (stream, c)
            self.readers[b] = {}
        for b in reads:
            self.readers.setdefault(b, {})[stream] = c

    def emit(self, eng, fn, reads=(), writes=(), dma=False, fam=None):
        if dma:
            assert fam is not None, "dma emit requires fam"
            stream = "D:" + fam
        else:
            stream = eng
        waits = self._dep_waits(eng, reads, writes)
        self._commit(stream, 16 if dma else 1, reads, writes)
        self.ops.append(Op(eng, fn, waits, stream, 16 if dma else 1))

    def emit_group(self, eng, fn, reads=(), writes=()):
        waits = self._dep_waits(eng, reads, writes)
        self._commit(eng, 1, reads, writes)
        self.ops.append(Op(eng, fn, waits, eng, 1, group=True))

    def coll(self, fn, reads=(), writes=()):
        name = f"coll{self.n_colls}"
        self.n_colls += 1
        waits = self._dep_waits("g", reads, writes)
        self.count[name] = 1
        for b in writes:
            self.writer[b] = (name, 1)
            self.readers[b] = {}
        for b in reads:
            self.readers.setdefault(b, {})[name] = 1
        self.ops.append(Op("g", fn, waits, name, 1))

    def fence(self, secs=SECTIONS):
        snapshot = dict(self.count)
        for sec in secs:
            waits = []
            for stream, value in snapshot.items():
                self._need(sec, waits, stream, value)
            self._pending.setdefault(sec, []).extend(waits)

    def replay(self, nc, block, es):
        sems = {s: es.enter_context(nc.semaphore(f"S_{s}")) for s in self.count}
        by_sec = {s: [] for s in SECTIONS}
        for op in self.ops:
            by_sec[op.eng].append(op)

        def body_for(sec):
            ops = by_sec[sec]

            def body(h):
                for op in ops:
                    for stream, value in op.waits:
                        h.wait_ge(sems[stream], value)
                    if op.group:
                        op.fn(h)[-1].then_inc(sems[op.sem_key], op.inc)
                    else:
                        op.fn(h).then_inc(sems[op.sem_key], op.inc)
            return body

        for sec in SECTIONS:
            if by_sec[sec]:
                getattr(block, SECNAME[sec])(body_for(sec))


def build(debug=False):
    nc = bass.Bass("TRN2", target_bir_lowering=False, debug=False)
    ins = {}

    def I(name, shape, dtype=F32):
        ins[name] = nc.dram_tensor(name, shape, dtype, kind="ExternalInput")

    I("xs", [TOK, DM]); I("xsT", [DM, TOK]); I("xh", [3, DM])
    I("identin", [128, 128])
    I("winT", [DM, E]); I("wxT", [DI, 33]); I("woutT", [DI, DM]); I("wfT", [DM, DM])
    I("cw", [DI, 4]); I("cb", [DI, 1])
    I("dtw_s", [DH, 1]); I("dtb_s", [DH, 1]); I("alog_s", [DH, N])
    I("dparam", [DI, 1]); I("fbias", [DM, 1]); I("sin", [1, 1]); I("sw", [1, 1])
    I("ridx", [4, 1], I32)
    out_d = nc.dram_tensor("out_d", [DM, TOK], F32, kind="ExternalOutput")

    ag_inA = nc.dram_tensor("ag_inA", [AGROWS, TOK // 2], BF16)
    ag_inB = nc.dram_tensor("ag_inB", [AGROWS, TOK // 2], BF16)
    ag_outA = nc.dram_tensor("ag_outA", [8 * AGROWS, TOK // 2], BF16, addr_space="Shared")
    ag_outB = nc.dram_tensor("ag_outB", [8 * AGROWS, TOK // 2], BF16, addr_space="Shared")
    agy_inA = nc.dram_tensor("agy_inA", [DH // 2, L], BF16)
    agy_inB = nc.dram_tensor("agy_inB", [DH // 2, L], BF16)
    agy_outA = nc.dram_tensor("agy_outA", [4 * DH, L], BF16, addr_space="Shared")
    agy_outB = nc.dram_tensor("agy_outB", [4 * DH, L], BF16, addr_space="Shared")
    zs = nc.dram_tensor("zs", [DI, TOK], BF16)
    qs = nc.dram_tensor("qs", [DM, TOK], BF16)
    ssw_d = nc.dram_tensor("ssw_d", [1, 2], F32)
    xcp = nc.dram_tensor("xcp", [DH, L], BF16)
    projp = nc.dram_tensor("projp", [33, L], BF16)
    yp = nc.dram_tensor("yp", [DI, TOK], BF16)
    dregs = nc.dram_tensor("dregs", [1, 8], I32, kind="ExternalOutput") if debug else None
    if debug:
        d_y2 = nc.dram_tensor("d_y2", [DI, TOK], BF16, kind="ExternalOutput")
        d_x3d = nc.dram_tensor("d_x3d", [DM, TOK], F32, kind="ExternalOutput")
        d_e1 = nc.dram_tensor("d_e1", [DM, TOK], F32, kind="ExternalOutput")
        d_e2 = nc.dram_tensor("d_e2", [DM, TOK], F32, kind="ExternalOutput")
    dbg = {}
    if debug:
        for nm, t in [("d_aginA", ag_inA), ("d_aginB", ag_inB),
                      ("d_agyinA", agy_inA), ("d_agyinB", agy_inB), ("d_zs", zs),
                      ("d_qs", qs), ("d_ssw", ssw_d), ("d_xcp", xcp),
                      ("d_projp", projp), ("d_yp", yp)]:
            dbg[nm] = (nc.dram_tensor(nm, list(t.shape), t.dtype,
                                      kind="ExternalOutput"), t)

    pl = Plan()
    sb = {}
    REG = {}

    es = contextlib.ExitStack()
    with es:
        def ST(name, shape, dtype=F32, stack=es):
            sb[name] = stack.enter_context(nc.sbuf_tensor(name, shape, dtype))

        for n_ in ["eps", "mhalf", "one", "invs", "sswb"]:
            ST(n_, [128, 1])
        ST("sinb", [1, 1]); ST("swb", [1, 1])
        ST("s_ridx", [1, 4], I32)
        ST("identf", [128, 128])
        for d in range(8):
            ST(f"dtw{d}", [128, 1]); ST(f"dtb{d}", [128, 1])
            ST(f"negA{d}", [128, N]); ST(f"carry{d}", [128, N])
        psum = [es.enter_context(nc.psum_tensor(f"pb{i}", [128, 512], F32))
                for i in range(8)]

        def EMIT(eng, fn, reads=(), writes=(), dma=False, fam=None):
            pl.emit(eng, fn, reads, writes, dma, fam)

        EMIT("q", lambda h: h.dma_start(out=sb["s_ridx"][:], in_=ins["ridx"].rearrange("a b -> b a")),
             writes=["s_ridx"], dma=True, fam="ridx")
        EMIT("q", lambda h: h.dma_start(out=sb["identf"][:], in_=ins["identin"][:, :]),
             writes=["identf"], dma=True, fam="identf")
        EMIT("v", lambda h: h.memset(sb["eps"][:], 1e-5), writes=["eps"])
        EMIT("v", lambda h: h.memset(sb["mhalf"][:], -0.5), writes=["mhalf"])
        EMIT("v", lambda h: h.memset(sb["one"][:], 1.0), writes=["one"])
        EMIT("q", lambda h: h.dma_start(out=sb["sinb"][:], in_=ins["sin"][:, :]),
             writes=["sinb"], dma=True, fam="scales")
        EMIT("q", lambda h: h.dma_start(out=sb["swb"][:], in_=ins["sw"][:, :]),
             writes=["swb"], dma=True, fam="scales")
        EMIT("v", lambda h: h.tensor_tensor(out=sb["swb"][:], in0=sb["sinb"][:],
             in1=sb["swb"][:], op=ALU.mult), reads=["sinb", "swb"], writes=["swb"])
        EMIT("v", lambda h: h.reciprocal(out=sb["sinb"][:], in_=sb["sinb"][:]),
             reads=["sinb"], writes=["sinb"])
        EMIT("g", lambda h: h.dma_start(out=ssw_d.ap()[0:1, 0:1], in_=sb["sinb"][:]),
             reads=["sinb"], writes=["ssw_d"], dma=True, fam="sswd")
        EMIT("g", lambda h: h.dma_start(out=ssw_d.ap()[0:1, 1:2], in_=sb["swb"][:]),
             reads=["ssw_d", "swb"], writes=["ssw_d"], dma=True, fam="sswd")
        EMIT("q", lambda h: h.dma_start(
            out=sb["invs"][:], in_=bass.AP(ssw_d, 0, [[0, 128], [1, 1]])),
            reads=["ssw_d"], writes=["invs"], dma=True, fam="consts2")
        EMIT("q", lambda h: h.dma_start(
            out=sb["sswb"][:], in_=bass.AP(ssw_d, 1, [[0, 128], [1, 1]])),
            reads=["ssw_d"], writes=["sswb"], dma=True, fam="consts2")

        def init_regs(h):
            insts = []
            for i, nm in enumerate(["r_xc", "r_pj", "r_y", "r_ow"]):
                REG[nm] = h.alloc_register(nm)
                insts.append(h.reg_load(REG[nm], sb["s_ridx"][0:1, i:i + 1]))
            REG["tmp"] = h.alloc_register("tmp")
            if debug:
                for i, nm in enumerate(["r_xc", "r_pj", "r_y", "r_ow"]):
                    insts.append(h.reg_save(dregs.ap()[0:1, i:i + 1], REG[nm]))
            insts.append(h.nop())
            return insts
        pl.emit_group("q", init_regs, reads=["s_ridx"])

        def dyn_copy(dst_ap, src_tensor, base, static_off, pattern, reads, writes,
                     fam="dyncp"):
            def fn(h):
                h.reg_mov(REG["tmp"], int(static_off))
                h.reg_add(REG["tmp"], REG["tmp"], REG[base])
                return h.dma_start(out=dst_ap,
                                   in_=bass.AP(src_tensor, REG["tmp"], pattern))
            EMIT("q", fn, reads=reads, writes=writes, dma=True, fam=fam)

        ST("alod", [128, N])
        ST("ident16", [128, 128], BF16)
        EMIT("v", lambda h: h.tensor_copy(out=sb["ident16"][:], in_=sb["identf"][:]),
             reads=["identf"], writes=["ident16"])

        for d in range(8):
            EMIT("q", (lambda dd=d: (lambda h: h.dma_start(
                out=sb["alod"][:],
                in_=ins["alog_s"][dd * 128:(dd + 1) * 128, :])))(),
                writes=["alod"], dma=True, fam="alod")
            EMIT("s", (lambda dd=d: (lambda h: h.activation(
                out=sb[f"negA{dd}"][:], in_=sb["alod"][:], func=AF.Exp)))(),
                reads=["alod"], writes=[f"negA{d}"])
            EMIT("v", (lambda dd=d: (lambda h: h.tensor_scalar(
                out=sb[f"negA{dd}"][:], in0=sb[f"negA{dd}"][:], scalar1=-1.0,
                scalar2=None, op0=ALU.mult)))(),
                reads=[f"negA{d}"], writes=[f"negA{d}"])
            EMIT("q", (lambda dd=d: (lambda h: h.dma_start(
                out=sb[f"dtw{dd}"][:],
                in_=ins["dtw_s"][dd * 128:(dd + 1) * 128, :])))(),
                writes=[f"dtw{d}"], dma=True, fam="consts3")
            EMIT("q", (lambda dd=d: (lambda h: h.dma_start(
                out=sb[f"dtb{dd}"][:],
                in_=ins["dtb_s"][dd * 128:(dd + 1) * 128, :])))(),
                writes=[f"dtb{d}"], dma=True, fam="consts3")
            EMIT("v", (lambda dd=d: (lambda h: h.memset(sb[f"carry{dd}"][:], 0.0)))(),
                 writes=[f"carry{d}"])

        p1 = contextlib.ExitStack()
        with p1:
            def S1(name, shape, dtype=F32):
                ST(name, shape, dtype, stack=p1)

            for i in range(2):
                for sp in range(2):
                    S1(f"xT{i}_{sp}", [128, DM])
                    S1(f"mean{i}_{sp}", [128, 1]); S1(f"ex2{i}_{sp}", [128, 1])
                    S1(f"var{i}_{sp}", [128, 1]); S1(f"rstd{i}_{sp}", [128, 1])
                    S1(f"nmr{i}_{sp}", [128, 1])
                S1(f"sq{i}", [128, DM])
            for k in range(8):
                S1(f"winT{k}", [128, E], BF16)
                S1(f"hxnd16_{k}", [128, 3], BF16)
            for sp in range(2):
                S1(f"xndC_{sp}", [128, 8 * T1]); S1(f"xndC16_{sp}", [128, 8 * T1], BF16)
                S1(f"q16C_{sp}", [128, 8 * T1], BF16)
            for d in range(16):
                S1(f"wxT{d}", [128, 33], BF16)
                S1(f"xss{d}", [128, T1 + 3], BF16)
            S1("xcw", [128, 16 * T1], BF16)
            S1("z16w", [128, 16 * T1], BF16)
            S1("wld", [128, 1024])
            S1("cacc_0", [128, 8 * T1]); S1("cacc_1", [128, 8 * T1])
            S1("proj_0", [33, T1], BF16); S1("proj_1", [33, T1], BF16)
            S1("qv", [128, 8 * T1]); S1("qc", [128, 8 * T1]); S1("qw", [128, 8 * T1])
            S1("qhi", [128, 8 * T1])
            S1("qm", [128, 8 * T1], U8)
            S1("c2p", [128, 1]); S1("c2n", [128, 1])
            for d in range(16):
                S1(f"cw{d}", [128, 4]); S1(f"cb{d}", [128, 1])

            EMIT("v", lambda h: h.memset(sb["c2p"][:], C2C), writes=["c2p"])
            EMIT("v", lambda h: h.memset(sb["c2n"][:], -C2C), writes=["c2n"])
            for d in range(16):
                EMIT("q", (lambda dd=d: (lambda h: h.dma_start(
                    out=sb[f"cw{dd}"][:], in_=ins["cw"][dd * 128:(dd + 1) * 128, :])))(),
                    writes=[f"cw{d}"], dma=True, fam="consts")
                EMIT("q", (lambda dd=d: (lambda h: h.dma_start(
                    out=sb[f"cb{dd}"][:], in_=ins["cb"][dd * 128:(dd + 1) * 128, :])))(),
                    writes=[f"cb{d}"], dma=True, fam="consts")
            for k in range(8):
                for c4 in range(4):
                    EMIT("q", (lambda kk=k, cc=c4: (lambda h: h.dma_start(
                        out=sb["wld"][:],
                        in_=ins["winT"].ap()[kk * 128:(kk + 1) * 128,
                                             cc * 1024:(cc + 1) * 1024])))(),
                        writes=["wld"], dma=True, fam="wld")
                    EMIT("v", (lambda kk=k, cc=c4: (lambda h: h.tensor_copy(
                        out=sb[f"winT{kk}"][:, cc * 1024:(cc + 1) * 1024],
                        in_=sb["wld"][:])))(),
                        reads=["wld"], writes=[f"winT{k}"])
            for d in range(16):
                EMIT("q", (lambda dd=d: (lambda h: h.dma_start(
                    out=sb["wld"][:, 0:33],
                    in_=ins["wxT"][dd * 128:(dd + 1) * 128, :])))(),
                    writes=["wld"], dma=True, fam="wld")
                EMIT("v", (lambda dd=d: (lambda h: h.tensor_copy(
                    out=sb[f"wxT{dd}"][:], in_=sb["wld"][:, 0:33])))(),
                    reads=["wld"], writes=[f"wxT{d}"])

            def ln_apply(kx, ksq, km, ke, kv, kr, kn):
                kxn = kx
                xT, sq = sb[kx], sb[ksq]
                mean, ex2, var, rstd, nmr = sb[km], sb[ke], sb[kv], sb[kr], sb[kn]
                EMIT("v", lambda h: h.tensor_tensor(out=sq[:], in0=xT[:], in1=xT[:],
                     op=ALU.mult), reads=[kx], writes=[ksq])
                EMIT("v", lambda h: h.reduce_sum(out=mean[:], in_=xT[:], axis=AXX),
                     reads=[kx], writes=[km])
                EMIT("v", lambda h: h.reduce_sum(out=ex2[:], in_=sq[:], axis=AXX),
                     reads=[ksq], writes=[ke])
                EMIT("v", lambda h: h.tensor_scalar(out=mean[:], in0=mean[:],
                     scalar1=1.0 / DM, scalar2=None, op0=ALU.mult),
                     reads=[km], writes=[km])
                EMIT("v", lambda h: h.tensor_scalar(out=ex2[:], in0=ex2[:],
                     scalar1=1.0 / DM, scalar2=None, op0=ALU.mult),
                     reads=[ke], writes=[ke])
                EMIT("v", lambda h: h.scalar_tensor_tensor(out=var[:], in0=mean[:],
                     scalar=mean[:], in1=ex2[:], op0=ALU.mult, op1=ALU.subtract),
                     reads=[km, ke], writes=[kv])
                EMIT("v", lambda h: h.tensor_scalar(out=var[:], in0=var[:],
                     scalar1=-1.0, scalar2=None, op0=ALU.mult),
                     reads=[kv], writes=[kv])
                EMIT("s", lambda h: h.activation(out=var[:], in_=var[:], func=AF.Ln,
                     bias=sb["eps"][:]), reads=[kv, "eps"], writes=[kv])
                EMIT("s", lambda h: h.activation(out=rstd[:], in_=var[:], func=AF.Exp,
                     scale=sb["mhalf"][:]), reads=[kv, "mhalf"], writes=[kr])
                EMIT("v", lambda h: h.scalar_tensor_tensor(out=nmr[:], in0=mean[:],
                     scalar=-1.0, in1=rstd[:], op0=ALU.mult, op1=ALU.mult),
                     reads=[km, kr], writes=[kn])
                EMIT("s", lambda h: h.activation(out=sb[kxn][:], in_=xT[:],
                     func=AF.Identity, scale=rstd[:], bias=nmr[:]),
                     reads=[kx, kr, kn], writes=[kxn])

            EMIT("v", lambda h: h.memset(sb["xT0_0"][:], 0.0), writes=["xT0_0"])
            EMIT("q", lambda h: h.dma_start(out=sb["xT0_0"][0:3, :], in_=ins["xh"][:, :]),
                 reads=["xT0_0"], writes=["xT0_0"], dma=True, fam="xh")
            ln_apply("xT0_0", "sq0", "mean0_0", "ex20_0", "var0_0", "rstd0_0", "nmr0_0")
            for k in range(8):
                pk = k % 2
                EMIT("t", (lambda kk=k, p=pk: (lambda h: h.transpose(
                    out=psum[p][:, 0:128],
                    in_=sb["xT0_0"][:, kk * 128:(kk + 1) * 128],
                    identity=sb["identf"][:])))(),
                    reads=["xT0_0", "identf"], writes=[f"pb{pk}"])
                EMIT("s", (lambda kk=k, p=pk: (lambda h: h.activation(
                    out=sb[f"hxnd16_{kk}"][:], in_=psum[p][:, 0:3],
                    func=AF.Identity)))(),
                    reads=[f"pb{pk}"], writes=[f"hxnd16_{k}"])
            for e in range(16):
                pe_ = 2 + e % 2

                def mmgroup(h, ee=e, p=pe_):
                    return [h.matmul(out=psum[p][:, 0:3],
                                     lhsT=sb[f"winT{k}"][:, ee * 128:(ee + 1) * 128],
                                     rhs=sb[f"hxnd16_{k}"][:],
                                     start=(k == 0), stop=(k == 7))
                            for k in range(8)]
                pl.emit_group("t", mmgroup,
                              reads=[f"hxnd16_{k}" for k in range(8)]
                              + [f"winT{k}" for k in range(8)],
                              writes=[f"pb{pe_}"])
                EMIT("s", (lambda ee=e, p=pe_: (lambda h: h.activation(
                    out=sb[f"xss{ee}"][:, 0:3], in_=psum[p][:, 0:3],
                    func=AF.Identity)))(),
                    reads=[f"pb{pe_}"], writes=[f"xss{e}"])

            def emit_quant(xndCv, q16Cv, c0v, s1v):
                EMIT("v", (lambda bx=xndCv: (lambda h: h.tensor_scalar(
                    out=sb["qv"][:], in0=sb[bx][:], scalar1=sb["invs"][:],
                    scalar2=-448.0, op0=ALU.mult, op1=ALU.max)))(),
                    reads=[f"{xndCv}:{kc2}" for kc2 in range(8)] + ["invs"],
                    writes=["qv"])
                EMIT("v", lambda h: h.tensor_scalar(out=sb["qv"][:], in0=sb["qv"][:],
                     scalar1=448.0, scalar2=None, op0=ALU.min),
                     reads=["qv"], writes=["qv"])
                EMIT("v", lambda h: h.tensor_scalar(out=sb["qc"][:], in0=sb["qv"][:],
                     scalar1=K1C, scalar2=None, op0=ALU.mult),
                     reads=["qv"], writes=["qc"])
                EMIT("v", lambda h: h.scalar_tensor_tensor(out=sb["qw"][:],
                     in0=sb["qv"][:], scalar=K1C, in1=sb["qv"][:],
                     op0=ALU.mult, op1=ALU.subtract),
                     reads=["qv"], writes=["qw"])
                EMIT("v", lambda h: h.tensor_tensor(out=sb["qhi"][:], in0=sb["qc"][:],
                     in1=sb["qw"][:], op=ALU.subtract),
                     reads=["qc", "qw"], writes=["qhi"])
                EMIT("v", lambda h: h.tensor_scalar(out=sb["qc"][:], in0=sb["qv"][:],
                     scalar1=C2C, scalar2=None, op0=ALU.add),
                     reads=["qv"], writes=["qc"])
                EMIT("v", lambda h: h.tensor_scalar(out=sb["qw"][:], in0=sb["qc"][:],
                     scalar1=C2C, scalar2=None, op0=ALU.subtract),
                     reads=["qc"], writes=["qw"])
                EMIT("v", lambda h: h.tensor_tensor(out=sb["qc"][:], in0=sb["qv"][:],
                     in1=sb["qv"][:], op=ALU.mult),
                     reads=["qv"], writes=["qc"])
                EMIT("v", lambda h: h.tensor_scalar(out=sb["qm"][:], in0=sb["qc"][:],
                     scalar1=float(2.0 ** -12), scalar2=None, op0=ALU.is_lt),
                     reads=["qc"], writes=["qm"])
                EMIT("v", lambda h: h.select(out=sb["qc"][:], mask=sb["qm"][:],
                     on_true=sb["qw"][:], on_false=sb["qhi"][:]),
                     reads=["qm", "qw", "qhi"], writes=["qc"])
                EMIT("v", (lambda q=q16Cv: (lambda h: h.tensor_copy(
                    out=sb[q][:], in_=sb["qc"][:])))(),
                    reads=["qc"], writes=[q16Cv])
                EMIT("g", (lambda q=q16Cv, cc=c0v: (lambda h: h.dma_start(
                    out=bass.AP(qs, cc, [[TOK, 128], [128 * TOK, 8], [1, T1]]),
                    in_=bass.AP(sb[q].tensor if hasattr(sb[q], "tensor") else None, 0,
                                [[8 * T1, 128], [T1, 8], [1, T1]])
                    if False else sb[q][:].rearrange("p (k j) -> p k j", k=8))))(),
                    reads=[q16Cv], writes=[f"qs_{s1v}"], dma=True, fam="qs")

            pending_q = [None]

            def issue_agA():
                aginA_keys = ([f"agin_{s}" for s in range(NS1 // 2)]
                              + [f"agin_{s}_p" for s in range(NS1 // 2)])
                pl.coll(lambda h: h.collective_compute(
                    "AllGather", ALU.bypass, replica_groups=[list(range(8))],
                    ins=[ag_inA.ap().opt()], outs=[ag_outA.ap().opt()]),
                    reads=aginA_keys, writes=["ag_outA"])

            for s1 in range(NS1):
                if s1 == NS1 // 2:
                    issue_agA()
                c0 = s1 * T1
                sp = s1 % 2
                xTn = [f"xT{i}_{sp}" for i in range(2)]
                xndC = f"xndC_{sp}"
                xndC16 = f"xndC16_{sp}"
                q16C = f"q16C_{sp}"
                xcwn = "xcw"
                caccn = ("cacc_0", "cacc_1")
                projn = f"proj_{sp}"
                for i in range(2):
                    EMIT("q", (lambda ii=i, cc=c0, bx=xTn[i]: (lambda h: h.dma_start(
                        out=sb[bx][:],
                        in_=ins["xs"][cc + ii * 128: cc + (ii + 1) * 128, :])))(),
                        writes=[xTn[i]], dma=True, fam=f"xT{sp}")
                for i in range(2):
                    ln_apply(xTn[i], f"sq{i}", f"mean{i}_{sp}", f"ex2{i}_{sp}",
                             f"var{i}_{sp}", f"rstd{i}_{sp}", f"nmr{i}_{sp}")
                if pending_q[0] is not None:
                    emit_quant(*pending_q[0])
                    pending_q[0] = None
                for kc in range(8):
                    p = kc % 2
                    for i in range(2):
                        EMIT("t", (lambda kk=kc, ii=i, pp=p, bx=xTn[i]: (lambda h: h.transpose(
                            out=psum[pp][:, ii * 128:(ii + 1) * 128],
                            in_=sb[bx][:, kk * 128:(kk + 1) * 128],
                            identity=sb["identf"][:])))(),
                            reads=[xTn[i], "identf"], writes=[f"pb{p}"])
                    EMIT("s", (lambda kk=kc, pp=p, bx=xndC: (lambda h: h.activation(
                        out=sb[bx][:, kk * T1:(kk + 1) * T1], in_=psum[pp][:, 0:T1],
                        func=AF.Identity)))(),
                        reads=[f"pb{p}"], writes=[f"{xndC}:{kc}"])
                EMIT("v", (lambda b1=xndC, b2=xndC16: (lambda h: h.tensor_copy(
                    out=sb[b2][:], in_=sb[b1][:])))(),
                    reads=[f"{xndC}:{kc2}" for kc2 in range(8)], writes=[xndC16])
                def conv_d(d):
                    ca = caccn[d % 2]
                    sl = (d // 2) * T1
                    EMIT("v", (lambda dd=d, c_=ca, s_=sl: (lambda h: h.tensor_scalar(
                        out=sb[c_][:, s_:s_ + T1], in0=sb[f"xss{dd}"][:, 0:T1],
                        scalar1=sb[f"cw{dd}"][:, 0:1], scalar2=None, op0=ALU.mult)))(),
                        reads=[f"xss{d}", f"cw{d}"], writes=[f"{ca}:{d}"])
                    for kk in range(1, 4):
                        EMIT("v", (lambda dd=d, k2=kk, c_=ca, s_=sl: (lambda h: h.scalar_tensor_tensor(
                            out=sb[c_][:, s_:s_ + T1], in0=sb[f"xss{dd}"][:, k2:T1 + k2],
                            scalar=sb[f"cw{dd}"][:, k2:k2 + 1], in1=sb[c_][:, s_:s_ + T1],
                            op0=ALU.mult, op1=ALU.add)))(),
                            reads=[f"xss{d}", f"cw{d}", f"{ca}:{d}"], writes=[f"{ca}:{d}"])
                    EMIT("v", (lambda dd=d: (lambda h: h.tensor_copy(
                        out=sb[f"xss{dd}"][:, 0:3],
                        in_=sb[f"xss{dd}"][:, T1:T1 + 3])))(),
                        reads=[f"xss{d}"], writes=[f"xss{d}"])

                def conv_tail(d):
                    ca = caccn[d % 2]
                    sl = (d // 2) * T1
                    EMIT("s", (lambda dd=d, c_=ca, s_=sl, bx=xcwn: (lambda h: h.activation(
                        out=sb[bx][:, dd * T1:(dd + 1) * T1],
                        in_=sb[c_][:, s_:s_ + T1], func=AF.Silu,
                        bias=sb[f"cb{dd}"][:])))(),
                        reads=[f"{ca}:{d}", f"cb{d}"], writes=[f"{xcwn}:{d}"])

                for e in range(32):
                    p = (2, 3, 5, 6)[e % 4]

                    def mmg(h, ee=e, pp=p, bx=xndC16):
                        return [h.matmul(out=psum[pp][:, 0:T1],
                                         lhsT=sb[f"winT{k}"][:, ee * 128:(ee + 1) * 128],
                                         rhs=sb[bx][:, k * T1:(k + 1) * T1],
                                         start=(k == 0), stop=(k == 7))
                                for k in range(8)]
                    pl.emit_group("t", mmg,
                                  reads=[xndC16]
                                  + [f"winT{k}" for k in range(8)],
                                  writes=[f"pb{p}"])
                    if e < 16:
                        EMIT("s", (lambda ee=e, pp=p: (lambda h: h.activation(
                            out=sb[f"xss{ee}"][:, 3:T1 + 3], in_=psum[pp][:, 0:T1],
                            func=AF.Identity)))(),
                            reads=[f"pb{p}"], writes=[f"xss{e}"])
                        conv_d(e)
                    else:
                        EMIT("s", (lambda ee=e, pp=p: (lambda h: h.activation(
                            out=sb["z16w"][:, (ee - 16) * T1:(ee - 15) * T1],
                            in_=psum[pp][:, 0:T1],
                            func=AF.Identity)))(),
                            reads=[f"pb{p}"], writes=[f"z16w:{e - 16}"])
                EMIT("g", (lambda cc=c0: (lambda h: h.dma_start(
                    out=bass.AP(zs, cc, [[TOK, 128], [128 * TOK, 16], [1, T1]]),
                    in_=sb["z16w"][:].rearrange("p (k j) -> p k j", k=16))))(),
                    reads=[f"z16w:{j}" for j in range(16)],
                    writes=[f"zs_{s1}"], dma=True, fam="zs")

                for d in range(16):
                    conv_tail(d)
                agt = ag_inA if s1 < NS1 // 2 else ag_inB
                cl = c0 % (TOK // 2)
                EMIT("g", (lambda cc=cl, at=agt, bx=xcwn: (lambda h: h.dma_start(
                    out=bass.AP(at, cc,
                                [[TOK // 2, 128], [128 * (TOK // 2), 16], [1, T1]]),
                    in_=sb[bx][:].rearrange("p (k j) -> p k j", k=16))))(),
                    reads=[f"{xcwn}:{d}" for d in range(16)],
                    writes=[f"agin_{s1}"], dma=True, fam="agin")
                def wxg(h, bx=xcwn):
                    return [h.matmul(out=psum[4][0:33, 0:T1], lhsT=sb[f"wxT{d}"][:],
                                     rhs=sb[bx][:, d * T1:(d + 1) * T1],
                                     start=(d == 0), stop=(d == 15))
                            for d in range(16)]
                pl.emit_group("t", wxg,
                              reads=[f"{xcwn}:{d}" for d in range(16)]
                              + [f"wxT{d}" for d in range(16)],
                              writes=["pb4"])
                EMIT("v", (lambda pj=projn: (lambda h: h.tensor_copy(out=sb[pj][:],
                     in_=psum[4][0:33, 0:T1])))(),
                     reads=["pb4"], writes=[projn])
                agt2 = ag_inA if s1 < NS1 // 2 else ag_inB
                cl2 = c0 % (TOK // 2)
                EMIT("g", (lambda cc=cl2, pj=projn, at=agt2: (lambda h: h.dma_start(
                    out=at.ap()[DI:DI + 33, cc:cc + T1], in_=sb[pj][:])))(),
                    reads=[projn], writes=[f"agin_{s1}_p"], dma=True, fam="agin")

                pending_q[0] = (xndC, q16C, c0, s1)

            if pending_q[0] is not None:
                emit_quant(*pending_q[0])
                pending_q[0] = None

        pl.fence()
        aginB_keys = ([f"agin_{s}" for s in range(NS1 // 2, NS1)]
                      + [f"agin_{s}_p" for s in range(NS1 // 2, NS1)])
        pl.coll(lambda h: h.collective_compute(
            "AllGather", ALU.bypass, replica_groups=[list(range(8))],
            ins=[ag_inB.ap().opt()], outs=[ag_outB.ap().opt()]),
            reads=aginB_keys, writes=["ag_outB"])
        HT = TOK // 2
        for half in range(2):
            q_ = 2 * half
            dyn_copy(xcp.ap()[:, q_ * HT:(q_ + 1) * HT], ag_outA, "r_xc",
                     half * AGROWS * HT, [[HT, DH], [1, HT]],
                     reads=["ag_outA"], writes=[f"xcp_q{q_}"], fam=f"xcq{q_}")
            dyn_copy(projp.ap()[:, q_ * HT:(q_ + 1) * HT], ag_outA, "r_pj",
                     half * AGROWS * HT, [[HT, 33], [1, HT]],
                     reads=["ag_outA"], writes=[f"projp_q{q_}"], fam=f"pjq{q_}")
        for half in range(2):
            q_ = 2 * half + 1
            dyn_copy(xcp.ap()[:, q_ * HT:(q_ + 1) * HT], ag_outB, "r_xc",
                     half * AGROWS * HT, [[HT, DH], [1, HT]],
                     reads=["ag_outB"], writes=[f"xcp_q{q_}"], fam=f"xcq{q_}")
            dyn_copy(projp.ap()[:, q_ * HT:(q_ + 1) * HT], ag_outB, "r_pj",
                     half * AGROWS * HT, [[HT, 33], [1, HT]],
                     reads=["ag_outB"], writes=[f"projp_q{q_}"], fam=f"pjq{q_}")
        p23 = contextlib.ExitStack()
        with p23:
            def S23(name, shape, dtype=F32):
                ST(name, shape, dtype, stack=p23)

            S23("wld3a", [128, DM]); S23("wld3b", [128, DM])
            for d in range(16):
                S23(f"woT{d}", [128, DM], BF16)
                S23(f"dp{d}", [128, 1])
            for k in range(8):
                S23(f"wfT{k}", [128, DM], BF16)
            for m in range(8):
                S23(f"fb{m}", [128, 1])

            p2 = contextlib.ExitStack()
            with p2:
                def S2(name, shape, dtype=F32):
                    ST(name, shape, dtype, stack=p2)

                for d in range(8):
                    S2(f"dt16_{d}", [128, Q2], BF16)
                    S2(f"u16_{d}", [128, Q2], BF16)
                    S2(f"xc2_{d}", [128, Q2], BF16)
                S2("dtbc", [128, Q2], BF16)
                for i in range(2):
                    S2(f"zp{i}", [128, Q2]); S2(f"rel{i}", [128, Q2])
                    S2(f"nab{i}", [128, Q2]); S2(f"lnb{i}", [128, Q2])
                for p in range(2):
                    S2(f"Bb{p}", [128, Q2], BF16); S2(f"Cb{p}", [128, Q2], BF16)
                    S2(f"dA{p}", [128, Q2]); S2(f"dBx{p}", [128, Q2], BF16)
                    S2(f"hh{p}", [128, Q2], BF16); S2(f"gy{p}", [128, Q2], BF16)
                    S2(f"y16w_{p}", [128, 1024], BF16)
                for d in range(16):
                    EMIT("q", (lambda dd=d: (lambda h: h.dma_start(
                        out=sb[f"dp{dd}"][:],
                        in_=ins["dparam"][dd * 128:(dd + 1) * 128, :])))(),
                        writes=[f"dp{d}"], dma=True, fam="consts4")
                for m in range(8):
                    EMIT("q", (lambda mm=m: (lambda h: h.dma_start(
                        out=sb[f"fb{mm}"][:],
                        in_=ins["fbias"][mm * 128:(mm + 1) * 128, :])))(),
                        writes=[f"fb{m}"], dma=True, fam="consts4")

                for c in range(NC2):
                    EMIT("q", (lambda cc=c: (lambda h: h.dma_start(out=sb["dtbc"][:],
                         in_=bass.AP(projp, cc * Q2, [[0, 128], [1, Q2]]))))(),
                         reads=[f"projp_q{c // 2}"], writes=["dtbc"], dma=True, fam="dtbc")
                    for d in range(8):
                        EMIT("q", (lambda dd=d, cc=c: (lambda h: h.dma_start(
                            out=sb[f"xc2_{dd}"][:],
                            in_=xcp.ap()[dd * 128:(dd + 1) * 128,
                                         cc * Q2:(cc + 1) * Q2])))(),
                            reads=[f"xcp_q{c // 2}"], writes=[f"xc2_{d}"], dma=True, fam="xc2")
                    for j in range(3):
                        w = 3 * c + j
                        stg = "wld3a" if w % 2 == 0 else "wld3b"
                        famw = "wl3a" if w % 2 == 0 else "wl3b"
                        if w < 16:
                            srct, row, dstb = ins["woutT"], w, f"woT{w}"
                        else:
                            srct, row, dstb = ins["wfT"], w - 16, f"wfT{w - 16}"
                        EMIT("q", (lambda s_=srct, r_=row, g_=stg: (lambda h: h.dma_start(
                            out=sb[g_][:], in_=s_[r_ * 128:(r_ + 1) * 128, :])))(),
                            writes=[stg], dma=True, fam=famw)
                        EMIT("g", (lambda g_=stg, b_=dstb: (lambda h: h.tensor_copy(
                            out=sb[b_][:], in_=sb[g_][:])))(),
                            reads=[stg], writes=[dstb])
                    for d in range(8):
                        i2 = d % 2
                        zp, rel = f"zp{i2}", f"rel{i2}"
                        nab, lnb = f"nab{i2}", f"lnb{i2}"
                        EMIT("s", (lambda dd=d, z=zp: (lambda h: h.activation(
                            out=sb[z][:], in_=sb["dtbc"][:], func=AF.Identity,
                            scale=sb[f"dtw{dd}"][:], bias=sb[f"dtb{dd}"][:])))(),
                            reads=["dtbc", f"dtw{d}", f"dtb{d}"], writes=[zp])
                        EMIT("v", (lambda z=zp, r=rel: (lambda h: h.tensor_scalar(
                            out=sb[r][:], in0=sb[z][:],
                            scalar1=0.0, scalar2=None, op0=ALU.max)))(),
                            reads=[zp], writes=[rel])
                        EMIT("v", (lambda z=zp, nb=nab: (lambda h: h.scalar_tensor_tensor(
                            out=sb[nb][:],
                            in0=sb[z][:], scalar=-1.0, in1=sb[z][:],
                            op0=ALU.mult, op1=ALU.min)))(),
                            reads=[zp], writes=[nab])
                        EMIT("s", (lambda nb=nab: (lambda h: h.activation(
                            out=sb[nb][:], in_=sb[nb][:],
                            func=AF.Exp)))(), reads=[nab], writes=[nab])
                        EMIT("s", (lambda nb=nab, lb=lnb: (lambda h: h.activation(
                            out=sb[lb][:], in_=sb[nb][:],
                            func=AF.Ln, bias=sb["one"][:])))(),
                            reads=[nab, "one"], writes=[lnb])
                        EMIT("v", (lambda dd=d, r=rel, lb=lnb: (lambda h: h.tensor_tensor(
                            out=sb[f"dt16_{dd}"][:], in0=sb[r][:], in1=sb[lb][:],
                            op=ALU.add)))(),
                            reads=[rel, lnb], writes=[f"dt16_{d}"])
                        EMIT("v", (lambda dd=d: (lambda h: h.tensor_tensor(
                            out=sb[f"u16_{dd}"][:], in0=sb[f"dt16_{dd}"][:],
                            in1=sb[f"xc2_{dd}"][:], op=ALU.mult)))(),
                            reads=[f"dt16_{d}", f"xc2_{d}"], writes=[f"u16_{d}"])
                    for dgrp in range(2):
                        d4 = dgrp * 4
                        for n in range(N):
                            p = n % 2
                            EMIT("q", (lambda nn=n, pp=p, cc=c: (lambda h: h.dma_start(
                                out=sb[f"Bb{pp}"][:],
                                in_=bass.AP(projp, (1 + nn) * L + cc * Q2,
                                            [[0, 128], [1, Q2]]))))(),
                                reads=[f"projp_q{c // 2}"], writes=[f"Bb{p}"], dma=True, fam=f"bb{p}")
                            EMIT("q", (lambda nn=n, pp=p, cc=c: (lambda h: h.dma_start(
                                out=sb[f"Cb{pp}"][:],
                                in_=bass.AP(projp, (17 + nn) * L + cc * Q2,
                                            [[0, 128], [1, Q2]]))))(),
                                reads=[f"projp_q{c // 2}"], writes=[f"Cb{p}"], dma=True, fam=f"cb2{p}")
                            for d in range(d4, d4 + 4):
                                q = (n * 4 + d) % 2
                                EMIT("s", (lambda dd=d, nn=n, qq=q: (lambda h: h.activation(
                                    out=sb[f"dA{qq}"][:], in_=sb[f"dt16_{dd}"][:],
                                    func=AF.Exp,
                                    scale=sb[f"negA{dd}"][:, nn:nn + 1])))(),
                                    reads=[f"dt16_{d}", f"negA{d}"], writes=[f"dA{q}"])
                                EMIT("v", (lambda dd=d, pp=p, qq=q: (lambda h: h.tensor_tensor(
                                    out=sb[f"dBx{qq}"][:], in0=sb[f"u16_{dd}"][:],
                                    in1=sb[f"Bb{pp}"][:], op=ALU.mult)))(),
                                    reads=[f"u16_{d}", f"Bb{p}"], writes=[f"dBx{q}"])
                                EMIT("v", (lambda dd=d, nn=n, qq=q: (lambda h: h.tensor_tensor_scan(
                                    out=sb[f"hh{qq}"][:], data0=sb[f"dA{qq}"][:],
                                    data1=sb[f"dBx{qq}"][:],
                                    initial=sb[f"carry{dd}"][:, nn:nn + 1],
                                    op0=ALU.mult, op1=ALU.add)))(),
                                    reads=[f"dA{q}", f"dBx{q}", f"carry{d}"],
                                    writes=[f"hh{q}"])
                                EMIT("v", (lambda dd=d, nn=n, qq=q: (lambda h: h.tensor_copy(
                                    out=sb[f"carry{dd}"][:, nn:nn + 1],
                                    in_=sb[f"hh{qq}"][:, Q2 - 1:Q2])))(),
                                    reads=[f"hh{q}"], writes=[f"carry{d}"])
                                EMIT("v", (lambda dd=d, pp=p, qq=q: (lambda h: h.tensor_tensor(
                                    out=sb[f"gy{qq}"][:], in0=sb[f"hh{qq}"][:],
                                    in1=sb[f"Cb{pp}"][:], op=ALU.mult)))(),
                                    reads=[f"hh{q}", f"Cb{p}"], writes=[f"gy{q}"])
                                for hf in range(2):
                                    bk = 2 * (d - d4) + hf
                                    EMIT("t", (lambda dd=d, nn=n, qq=q, hh2=hf, bb=bk:
                                         (lambda h: h.matmul(
                                            out=psum[bb][:],
                                            lhsT=sb["ident16"][:],
                                            rhs=sb[f"gy{qq}"][:, hh2 * 512:(hh2 + 1) * 512],
                                            start=(nn == 0), stop=(nn == N - 1),
                                            skip_group_check=True)))(),
                                        reads=[f"gy{q}", "ident16"], writes=[f"pb{bk}"])
                        for d in range(d4, d4 + 4):
                            yb = f"y16w_{d % 2}"
                            for hf in range(2):
                                bk = 2 * (d - d4) + hf
                                EMIT("s", (lambda bb=bk, y_=yb, hh2=hf: (lambda h: h.activation(
                                    out=sb[y_][:, hh2 * 512:(hh2 + 1) * 512],
                                    in_=psum[bb][:],
                                    func=AF.Identity)))(),
                                    reads=[f"pb{bk}"], writes=[f"{yb}:{hf}"])
                            agt3 = agy_inA if d < 4 else agy_inB
                            EMIT("g", (lambda dd=d, y_=yb, cc=c, at=agt3: (lambda h: h.dma_start(
                                out=at.ap()[(dd % 4) * 128:(dd % 4 + 1) * 128,
                                            cc * Q2:(cc + 1) * Q2],
                                in_=sb[y_][:])))(),
                                reads=[f"{yb}:0", f"{yb}:1"], writes=[f"agy_{c}_{d}"],
                                dma=True, fam=f"agy{d % 2}")
                        if c == NC2 - 1 and dgrp == 0:
                            agyA_keys = [f"agy_{cc}_{dd}" for cc in range(NC2)
                                         for dd in range(4)]
                            pl.coll(lambda h: h.collective_compute(
                                "AllGather", ALU.bypass, replica_groups=[list(range(8))],
                                ins=[agy_inA.ap().opt()], outs=[agy_outA.ap().opt()]),
                                reads=agyA_keys, writes=["agy_outA"])

            pl.fence()
            agyB_keys = [f"agy_{c}_{d}" for c in range(NC2) for d in range(4, 8)]
            pl.coll(lambda h: h.collective_compute(
                "AllGather", ALU.bypass, replica_groups=[list(range(8))],
                ins=[agy_inB.ap().opt()], outs=[agy_outB.ap().opt()]),
                reads=agyB_keys, writes=["agy_outB"])
            for half in range(2):
                dyn_copy(yp.ap()[half * DH:half * DH + 512, :], agy_outA, "r_y",
                         half * 512 * L, [[L, 512], [1, TOK]],
                         reads=["agy_outA"], writes=[f"yp_q{2 * half}"],
                         fam=f"ypq{2 * half}")
            for half in range(2):
                dyn_copy(yp.ap()[half * DH + 512:(half + 1) * DH, :], agy_outB, "r_y",
                         half * 512 * L, [[L, 512], [1, TOK]],
                         reads=["agy_outB"], writes=[f"yp_q{2 * half + 1}"],
                         fam=f"ypq{2 * half + 1}")
            p3 = contextlib.ExitStack()
            with p3:
                def S3(name, shape, dtype=F32):
                    ST(name, shape, dtype, stack=p3)

                for sp in range(2):
                    S3(f"y3w_{sp}", [128, 16 * T3], BF16)
                    S3(f"xc3w_{sp}", [128, 16 * T3], BF16)
                    S3(f"z3w_{sp}", [128, 16 * T3], BF16)
                    S3(f"q3w_{sp}", [128, 8 * T3], BF16)
                    S3(f"x3w_{sp}", [128, 8 * T3])
                for p in range(8):
                    S3(f"e1_{p}", [128, T3])
                for p in range(2):
                    S3(f"e2_{p}", [128, T3])
                    S3(f"odw_{p}", [128, 8 * T3])

                for s3 in range(NS3):
                    c0 = s3 * T3
                    sp = s3 % 2
                    y3w, xc3w, z3w = f"y3w_{sp}", f"xc3w_{sp}", f"z3w_{sp}"
                    q3w, x3w = f"q3w_{sp}", f"x3w_{sp}"
                    EMIT("q", (lambda cc=c0, b=q3w: (lambda h: h.dma_start(
                        out=sb[b][:].rearrange("p (k j) -> p k j", k=8),
                        in_=bass.AP(qs, cc, [[TOK, 128], [128 * TOK, 8], [1, T3]]))))(),
                        reads=["qsall"], writes=[f"{q3w}:{k}" for k in range(8)],
                        dma=True, fam=f"q3{sp}")
                    EMIT("q", (lambda cc=c0, b=x3w: (lambda h: h.dma_start(
                        out=sb[b][:].rearrange("p (k j) -> p k j", k=8),
                        in_=bass.AP(ins["xsT"], cc,
                                    [[TOK, 128], [128 * TOK, 8], [1, T3]]))))(),
                        writes=[f"{x3w}:{m}" for m in range(8)],
                        dma=True, fam=f"x3{sp}")
                    EMIT("q", (lambda cc=c0, b=y3w: (lambda h: h.dma_start(
                        out=sb[b][:].rearrange("p (k j) -> p k j", k=16),
                        in_=bass.AP(yp, cc, [[TOK, 128], [128 * TOK, 16], [1, T3]]))))(),
                        reads=[f"yp_q{qq}" for qq in range(4)],
                        writes=[f"{y3w}:{d}" for d in range(16)],
                        dma=True, fam=f"y3{sp}")
                    ag3 = ag_inA if s3 < NS3 // 2 else ag_inB
                    cl3 = c0 % (TOK // 2)
                    EMIT("q", (lambda cc=cl3, b=xc3w, at=ag3: (lambda h: h.dma_start(
                        out=sb[b][:].rearrange("p (k j) -> p k j", k=16),
                        in_=bass.AP(at, cc,
                                    [[TOK // 2, 128], [128 * (TOK // 2), 16], [1, T3]]))))(),
                        reads=["againall"], writes=[f"{xc3w}:{d}" for d in range(16)],
                        dma=True, fam=f"xc3{sp}")
                    EMIT("q", (lambda cc=c0, b=z3w: (lambda h: h.dma_start(
                        out=sb[b][:].rearrange("p (k j) -> p k j", k=16),
                        in_=bass.AP(zs, cc, [[TOK, 128], [128 * TOK, 16], [1, T3]]))))(),
                        reads=["zsall"], writes=[f"{z3w}:{d}" for d in range(16)],
                        dma=True, fam=f"z3{sp}")
                    for m in range(8):
                        pf = (4, 7)[m % 2]
                        pe1 = m

                        def ffg(h, mm=m, pp=pf, b=q3w):
                            return [h.matmul(out=psum[pp][:, 0:T3],
                                             lhsT=sb[f"wfT{k}"][:, mm * 128:(mm + 1) * 128],
                                             rhs=sb[b][:, k * T3:(k + 1) * T3],
                                             start=(k == 0), stop=(k == 7))
                                    for k in range(8)]
                        pl.emit_group("t", ffg,
                                      reads=[f"{q3w}:{k}" for k in range(8)]
                                      + [f"wfT{k}" for k in range(8)],
                                      writes=[f"pb{pf}"])
                        EMIT("s", (lambda mm=m, pf_=pf, p_=pe1: (lambda h: h.activation(
                            out=sb[f"e1_{p_}"][:], in_=psum[pf_][:, 0:T3],
                            func=AF.Identity, scale=sb["sswb"][:],
                            bias=sb[f"fb{mm}"][:])))(),
                            reads=[f"pb{pf}", "sswb", f"fb{m}"], writes=[f"e1_{pe1}"])
                    for d in range(16):
                        sl = slice(d * T3, (d + 1) * T3)
                        EMIT("s", (lambda b=z3w, s_=sl: (lambda h: h.activation(
                            out=sb[b][:, s_], in_=sb[b][:, s_], func=AF.Silu)))(),
                            reads=[f"{z3w}:{d}"], writes=[f"{z3w}:{d}"])
                        EMIT("v", (lambda dd=d, by=y3w, bx=xc3w, s_=sl: (lambda h: h.scalar_tensor_tensor(
                            out=sb[by][:, s_], in0=sb[bx][:, s_],
                            scalar=sb[f"dp{dd}"][:], in1=sb[by][:, s_],
                            op0=ALU.mult, op1=ALU.add)))(),
                            reads=[f"{xc3w}:{d}", f"dp{d}", f"{y3w}:{d}"],
                            writes=[f"{y3w}:{d}"])
                        EMIT("v", (lambda by=y3w, bz=z3w, s_=sl: (lambda h: h.tensor_tensor(
                            out=sb[by][:, s_], in0=sb[by][:, s_],
                            in1=sb[bz][:, s_], op=ALU.mult)))(),
                            reads=[f"{y3w}:{d}", f"{z3w}:{d}"], writes=[f"{y3w}:{d}"])
                    for m in range(8):
                        pw = (2, 3, 5, 6)[m % 4]
                        pe1 = m
                        pp = m % 2

                        def wog(h, mm=m, pp=pw, b=y3w):
                            return [h.matmul(out=psum[pp][:, 0:T3],
                                             lhsT=sb[f"woT{d}"][:, mm * 128:(mm + 1) * 128],
                                             rhs=sb[b][:, d * T3:(d + 1) * T3],
                                             start=(d == 0), stop=(d == 15))
                                    for d in range(16)]
                        pl.emit_group("t", wog,
                                      reads=[f"{y3w}:{d}" for d in range(16)]
                                      + [f"woT{d}" for d in range(16)],
                                      writes=[f"pb{pw}"])
                        EMIT("v", (lambda mm=m, pw_=pw, p_=pp, pe=pe1: (lambda h: h.tensor_tensor(
                            out=sb[f"e2_{p_}"][:], in0=psum[pw_][:, 0:T3],
                            in1=sb[f"e1_{pe}"][:], op=ALU.add)))(),
                            reads=[f"pb{pw}", f"e1_{pe1}"], writes=[f"e2_{pp}"])
                        EMIT("v", (lambda mm=m, p_=sp, bx=x3w, pe=pp: (lambda h: h.tensor_tensor(
                            out=sb[f"odw_{p_}"][:, mm * T3:(mm + 1) * T3],
                            in0=sb[f"e2_{pe}"][:],
                            in1=sb[bx][:, mm * T3:(mm + 1) * T3], op=ALU.add)))(),
                            reads=[f"e2_{pp}", f"{x3w}:{m}"], writes=[f"odw_{sp}:{m}"])
                    EMIT("g", (lambda p_=sp, cc=c0: (lambda h: h.dma_start(
                        out=bass.AP(out_d, cc, [[TOK, 128], [128 * TOK, 8], [1, T3]]),
                        in_=sb[f"odw_{p_}"][:].rearrange("p (k j) -> p k j", k=8))))(),
                        reads=[f"odw_{sp}:{m}" for m in range(8)],
                        writes=[f"out_{s3}"], dma=True, fam="outd")

        if debug:
            pl.fence(("q",))
            for nm, (dst, srct) in dbg.items():
                pl.emit("q", (lambda d_=dst, s_=srct: (lambda h: h.dma_start(
                    out=d_.ap()[:, :], in_=s_.ap()[:, :])))(),
                    reads=[], writes=[nm], dma=True, fam="outd")
        pl.fence(("g",))
        pl.emit("g", lambda h: h.nop())

        global _LAST_PLAN
        _LAST_PLAN = pl
        with nc.Block() as block:
            pl.replay(nc, block, es)
    return nc


_NC_CACHE = None
_LAST_PLAN = None


def kernel(L0_attn_x, L0_attn_W_in, L0_attn_conv_w, L0_attn_conv_b, L0_attn_W_x,
           L0_attn_dt_w, L0_attn_dt_b, L0_attn_A_log, L0_attn_D_param,
           L0_attn_W_out, L0_ffn_weight_fp8, L0_ffn_bias, L0_ffn_input_scale,
           L0_ffn_weight_scale, _debug=False):
    f32 = np.float32
    x = np.asarray(L0_attn_x, f32)
    winT = np.ascontiguousarray(np.asarray(L0_attn_W_in, f32).T)
    wxT = np.ascontiguousarray(np.asarray(L0_attn_W_x, f32).T)
    woutT = np.ascontiguousarray(np.asarray(L0_attn_W_out, f32).T)
    wfT = np.ascontiguousarray(np.asarray(L0_ffn_weight_fp8, f32).T)
    cw = np.ascontiguousarray(np.asarray(L0_attn_conv_w, f32))
    cb = np.asarray(L0_attn_conv_b, f32).reshape(DI, 1)
    dtw = np.asarray(L0_attn_dt_w, f32).reshape(DI, 1)
    dtb = np.asarray(L0_attn_dt_b, f32).reshape(DI, 1)
    alog = np.ascontiguousarray(np.asarray(L0_attn_A_log, f32))
    dparam = np.asarray(L0_attn_D_param, f32).reshape(DI, 1)
    fbias = np.asarray(L0_ffn_bias, f32).reshape(DM, 1)
    sin = np.asarray(L0_ffn_input_scale, f32).reshape(1, 1)
    sw = np.asarray(L0_ffn_weight_scale, f32).reshape(1, 1)
    ident = np.eye(128, dtype=f32)

    global _NC_CACHE
    if _debug:
        nc = build(debug=True)
    else:
        if _NC_CACHE is None:
            _NC_CACHE = build()
        nc = _NC_CACHE

    in_maps = []
    for k in range(8):
        b, h = k // 2, k % 2
        t0 = h * TOK
        xs_ = np.ascontiguousarray(x[b, t0:t0 + TOK, :])
        xh_ = (np.zeros((3, DM), f32) if h == 0
               else np.ascontiguousarray(x[b, t0 - 3:t0, :]))
        dh0 = h * DH
        ridx_ = np.array([
            [(2 * b * AGROWS + h * DH) * (TOK // 2)],
            [(2 * b * AGROWS + DI) * (TOK // 2)],
            [2 * b * 512 * L + h * TOK],
            [k * AGROWS * TOK]], dtype=np.int32)
        in_maps.append(dict(
            xs=xs_, xsT=np.ascontiguousarray(xs_.T), xh=xh_, identin=ident,
            winT=winT, wxT=wxT, woutT=woutT,
            wfT=wfT, cw=cw, cb=cb,
            dtw_s=np.ascontiguousarray(dtw[dh0:dh0 + DH]),
            dtb_s=np.ascontiguousarray(dtb[dh0:dh0 + DH]),
            alog_s=np.ascontiguousarray(alog[dh0:dh0 + DH]),
            dparam=dparam, fbias=fbias, sin=sin, sw=sw, ridx=ridx_))
    res = run_bass_kernel_spmd(nc, in_maps, core_ids=list(range(8)))
    out = np.empty((B, L, DM), f32)
    for k in range(8):
        b, h = k // 2, k % 2
        out[b, h * TOK:(h + 1) * TOK, :] = np.asarray(res.results[k]["out_d"]).T
    if _debug:
        return out, res
    return out


# version 27
# speedup vs baseline: 1.0310x; 1.0039x over previous
import contextlib
import numpy as np
import concourse.bass as bass
import concourse.mybir as mybir
from concourse.bass_utils import run_bass_kernel_spmd

F32 = mybir.dt.float32
BF16 = mybir.dt.bfloat16
U8 = mybir.dt.uint8
I32 = mybir.dt.int32
AF = mybir.ActivationFunctionType
ALU = mybir.AluOpType
AXX = mybir.AxisListType.X

B, L, DM, E, DI, N = 4, 8192, 1024, 4096, 2048, 16
TOK = 4096
T1 = 256
NS1 = TOK // T1
Q2 = 1024
NC2 = L // Q2
T3 = 256
NS3 = TOK // T3
DH = DI // 2
AGROWS = 2084
K1C = float(2 ** 20 + 1)
C2C = float(3 * 2 ** 13)

SECTIONS = ("v", "s", "t", "g", "q")
SECNAME = {"v": "vector", "s": "scalar", "t": "tensor", "g": "gpsimd", "q": "sync"}


class Op:
    __slots__ = ("eng", "fn", "waits", "sem_key", "inc", "group")

    def __init__(self, eng, fn, waits, sem_key, inc, group=False):
        self.eng, self.fn, self.waits = eng, fn, waits
        self.sem_key, self.inc, self.group = sem_key, inc, group


class Plan:
    def __init__(self):
        self.ops = []
        self.count = {}
        self.waited = {}
        self.writer = {}
        self.readers = {}
        self.n_colls = 0
        self._pending = {}

    def _need(self, sec, waits, stream, value):
        if stream is None or value <= 0:
            return
        if stream.startswith("D:"):
            value = self.count[stream]
        key = (sec, stream)
        if self.waited.get(key, 0) >= value:
            return
        waits.append((stream, value))
        self.waited[key] = value

    def _dep_waits(self, sec, reads, writes):
        waits = []
        pend = self._pending.pop(sec, None)
        if pend:
            waits.extend(pend)
        for b in reads:
            w = self.writer.get(b)
            if w is not None:
                self._need(sec, waits, w[0], w[1])
        for b in writes:
            w = self.writer.get(b)
            if w is not None:
                self._need(sec, waits, w[0], w[1])
            for pe, pc in self.readers.get(b, {}).items():
                self._need(sec, waits, pe, pc)
        if len(waits) > 1:
            best = {}
            for stream, value in waits:
                if best.get(stream, -1) < value:
                    best[stream] = value
            waits = list(best.items())
        return waits

    def _commit(self, stream, inc, reads, writes):
        self.count[stream] = self.count.get(stream, 0) + inc
        c = self.count[stream]
        for b in writes:
            self.writer[b] = (stream, c)
            self.readers[b] = {}
        for b in reads:
            self.readers.setdefault(b, {})[stream] = c

    def emit(self, eng, fn, reads=(), writes=(), dma=False, fam=None):
        if dma:
            assert fam is not None, "dma emit requires fam"
            stream = "D:" + fam
        else:
            stream = eng
        waits = self._dep_waits(eng, reads, writes)
        self._commit(stream, 16 if dma else 1, reads, writes)
        self.ops.append(Op(eng, fn, waits, stream, 16 if dma else 1))

    def emit_group(self, eng, fn, reads=(), writes=()):
        waits = self._dep_waits(eng, reads, writes)
        self._commit(eng, 1, reads, writes)
        self.ops.append(Op(eng, fn, waits, eng, 1, group=True))

    def coll(self, fn, reads=(), writes=()):
        name = f"coll{self.n_colls}"
        self.n_colls += 1
        waits = self._dep_waits("g", reads, writes)
        self.count[name] = 1
        for b in writes:
            self.writer[b] = (name, 1)
            self.readers[b] = {}
        for b in reads:
            self.readers.setdefault(b, {})[name] = 1
        self.ops.append(Op("g", fn, waits, name, 1))

    def fence(self, secs=SECTIONS):
        snapshot = dict(self.count)
        for sec in secs:
            waits = []
            for stream, value in snapshot.items():
                self._need(sec, waits, stream, value)
            self._pending.setdefault(sec, []).extend(waits)

    def replay(self, nc, block, es):
        sems = {s: es.enter_context(nc.semaphore(f"S_{s}")) for s in self.count}
        by_sec = {s: [] for s in SECTIONS}
        for op in self.ops:
            by_sec[op.eng].append(op)

        def body_for(sec):
            ops = by_sec[sec]

            def body(h):
                for op in ops:
                    for stream, value in op.waits:
                        h.wait_ge(sems[stream], value)
                    if op.group:
                        op.fn(h)[-1].then_inc(sems[op.sem_key], op.inc)
                    else:
                        op.fn(h).then_inc(sems[op.sem_key], op.inc)
            return body

        for sec in SECTIONS:
            if by_sec[sec]:
                getattr(block, SECNAME[sec])(body_for(sec))


def build(debug=False):
    nc = bass.Bass("TRN2", target_bir_lowering=False, debug=False)
    ins = {}

    def I(name, shape, dtype=F32):
        ins[name] = nc.dram_tensor(name, shape, dtype, kind="ExternalInput")

    I("xs", [TOK, DM]); I("xsT", [DM, TOK]); I("xh", [3, DM])
    I("identin", [128, 128])
    I("winT", [DM, E]); I("wxT", [DI, 33]); I("woutT", [DI, DM]); I("wfT", [DM, DM])
    I("cw", [DI, 4]); I("cb", [DI, 1])
    I("dtw_s", [DH, 1]); I("dtb_s", [DH, 1]); I("alog_s", [DH, N])
    I("dparam", [DI, 1]); I("fbias", [DM, 1]); I("sin", [1, 1]); I("sw", [1, 1])
    I("ridx", [4, 1], I32)
    out_d = nc.dram_tensor("out_d", [DM, TOK], F32, kind="ExternalOutput")

    ag_inA = nc.dram_tensor("ag_inA", [AGROWS, TOK // 2], BF16)
    ag_inB = nc.dram_tensor("ag_inB", [AGROWS, TOK // 2], BF16)
    ag_outA = nc.dram_tensor("ag_outA", [8 * AGROWS, TOK // 2], BF16, addr_space="Shared")
    ag_outB = nc.dram_tensor("ag_outB", [8 * AGROWS, TOK // 2], BF16, addr_space="Shared")
    agy_inA = nc.dram_tensor("agy_inA", [DH // 2, L], BF16)
    agy_inB = nc.dram_tensor("agy_inB", [DH // 2, L], BF16)
    agy_outA = nc.dram_tensor("agy_outA", [4 * DH, L], BF16, addr_space="Shared")
    agy_outB = nc.dram_tensor("agy_outB", [4 * DH, L], BF16, addr_space="Shared")
    zs = nc.dram_tensor("zs", [DI, TOK], BF16)
    qs = nc.dram_tensor("qs", [DM, TOK], BF16)
    ssw_d = nc.dram_tensor("ssw_d", [1, 2], F32)
    xcp = nc.dram_tensor("xcp", [DH, L], BF16)
    projp = nc.dram_tensor("projp", [33, L], BF16)
    yp = nc.dram_tensor("yp", [DI, TOK], BF16)
    dregs = nc.dram_tensor("dregs", [1, 8], I32, kind="ExternalOutput") if debug else None
    if debug:
        d_y2 = nc.dram_tensor("d_y2", [DI, TOK], BF16, kind="ExternalOutput")
        d_x3d = nc.dram_tensor("d_x3d", [DM, TOK], F32, kind="ExternalOutput")
        d_e1 = nc.dram_tensor("d_e1", [DM, TOK], F32, kind="ExternalOutput")
        d_e2 = nc.dram_tensor("d_e2", [DM, TOK], F32, kind="ExternalOutput")
    dbg = {}
    if debug:
        for nm, t in [("d_aginA", ag_inA), ("d_aginB", ag_inB),
                      ("d_agyinA", agy_inA), ("d_agyinB", agy_inB), ("d_zs", zs),
                      ("d_qs", qs), ("d_ssw", ssw_d), ("d_xcp", xcp),
                      ("d_projp", projp), ("d_yp", yp)]:
            dbg[nm] = (nc.dram_tensor(nm, list(t.shape), t.dtype,
                                      kind="ExternalOutput"), t)

    pl = Plan()
    sb = {}
    REG = {}

    es = contextlib.ExitStack()
    with es:
        def ST(name, shape, dtype=F32, stack=es):
            sb[name] = stack.enter_context(nc.sbuf_tensor(name, shape, dtype))

        for n_ in ["eps", "mhalf", "one", "invs", "sswb"]:
            ST(n_, [128, 1])
        ST("sinb", [1, 1]); ST("swb", [1, 1])
        ST("s_ridx", [1, 4], I32)
        ST("identf", [128, 128])
        for d in range(8):
            ST(f"dtw{d}", [128, 1]); ST(f"dtb{d}", [128, 1])
            ST(f"negA{d}", [128, N]); ST(f"carry{d}", [128, N])
        psum = [es.enter_context(nc.psum_tensor(f"pb{i}", [128, 512], F32))
                for i in range(8)]

        def EMIT(eng, fn, reads=(), writes=(), dma=False, fam=None):
            pl.emit(eng, fn, reads, writes, dma, fam)

        EMIT("q", lambda h: h.dma_start(out=sb["s_ridx"][:], in_=ins["ridx"].rearrange("a b -> b a")),
             writes=["s_ridx"], dma=True, fam="ridx")
        EMIT("q", lambda h: h.dma_start(out=sb["identf"][:], in_=ins["identin"][:, :]),
             writes=["identf"], dma=True, fam="identf")
        EMIT("v", lambda h: h.memset(sb["eps"][:], 1e-5), writes=["eps"])
        EMIT("v", lambda h: h.memset(sb["mhalf"][:], -0.5), writes=["mhalf"])
        EMIT("v", lambda h: h.memset(sb["one"][:], 1.0), writes=["one"])
        EMIT("q", lambda h: h.dma_start(out=sb["sinb"][:], in_=ins["sin"][:, :]),
             writes=["sinb"], dma=True, fam="scales")
        EMIT("q", lambda h: h.dma_start(out=sb["swb"][:], in_=ins["sw"][:, :]),
             writes=["swb"], dma=True, fam="scales")
        EMIT("v", lambda h: h.tensor_tensor(out=sb["swb"][:], in0=sb["sinb"][:],
             in1=sb["swb"][:], op=ALU.mult), reads=["sinb", "swb"], writes=["swb"])
        EMIT("v", lambda h: h.reciprocal(out=sb["sinb"][:], in_=sb["sinb"][:]),
             reads=["sinb"], writes=["sinb"])
        EMIT("g", lambda h: h.dma_start(out=ssw_d.ap()[0:1, 0:1], in_=sb["sinb"][:]),
             reads=["sinb"], writes=["ssw_d"], dma=True, fam="sswd")
        EMIT("g", lambda h: h.dma_start(out=ssw_d.ap()[0:1, 1:2], in_=sb["swb"][:]),
             reads=["ssw_d", "swb"], writes=["ssw_d"], dma=True, fam="sswd")
        EMIT("q", lambda h: h.dma_start(
            out=sb["invs"][:], in_=bass.AP(ssw_d, 0, [[0, 128], [1, 1]])),
            reads=["ssw_d"], writes=["invs"], dma=True, fam="consts2")
        EMIT("q", lambda h: h.dma_start(
            out=sb["sswb"][:], in_=bass.AP(ssw_d, 1, [[0, 128], [1, 1]])),
            reads=["ssw_d"], writes=["sswb"], dma=True, fam="consts2")

        def init_regs(h):
            insts = []
            for i, nm in enumerate(["r_xc", "r_pj", "r_y", "r_ow"]):
                REG[nm] = h.alloc_register(nm)
                insts.append(h.reg_load(REG[nm], sb["s_ridx"][0:1, i:i + 1]))
            REG["tmp"] = h.alloc_register("tmp")
            if debug:
                for i, nm in enumerate(["r_xc", "r_pj", "r_y", "r_ow"]):
                    insts.append(h.reg_save(dregs.ap()[0:1, i:i + 1], REG[nm]))
            insts.append(h.nop())
            return insts
        pl.emit_group("q", init_regs, reads=["s_ridx"])

        def dyn_copy(dst_ap, src_tensor, base, static_off, pattern, reads, writes,
                     fam="dyncp"):
            def fn(h):
                h.reg_mov(REG["tmp"], int(static_off))
                h.reg_add(REG["tmp"], REG["tmp"], REG[base])
                return h.dma_start(out=dst_ap,
                                   in_=bass.AP(src_tensor, REG["tmp"], pattern))
            EMIT("q", fn, reads=reads, writes=writes, dma=True, fam=fam)

        ST("alod", [128, N])
        ST("ident16", [128, 128], BF16)
        EMIT("v", lambda h: h.tensor_copy(out=sb["ident16"][:], in_=sb["identf"][:]),
             reads=["identf"], writes=["ident16"])

        for d in range(8):
            EMIT("q", (lambda dd=d: (lambda h: h.dma_start(
                out=sb["alod"][:],
                in_=ins["alog_s"][dd * 128:(dd + 1) * 128, :])))(),
                writes=["alod"], dma=True, fam="alod")
            EMIT("s", (lambda dd=d: (lambda h: h.activation(
                out=sb[f"negA{dd}"][:], in_=sb["alod"][:], func=AF.Exp)))(),
                reads=["alod"], writes=[f"negA{d}"])
            EMIT("v", (lambda dd=d: (lambda h: h.tensor_scalar(
                out=sb[f"negA{dd}"][:], in0=sb[f"negA{dd}"][:], scalar1=-1.0,
                scalar2=None, op0=ALU.mult)))(),
                reads=[f"negA{d}"], writes=[f"negA{d}"])
            EMIT("q", (lambda dd=d: (lambda h: h.dma_start(
                out=sb[f"dtw{dd}"][:],
                in_=ins["dtw_s"][dd * 128:(dd + 1) * 128, :])))(),
                writes=[f"dtw{d}"], dma=True, fam="consts3")
            EMIT("q", (lambda dd=d: (lambda h: h.dma_start(
                out=sb[f"dtb{dd}"][:],
                in_=ins["dtb_s"][dd * 128:(dd + 1) * 128, :])))(),
                writes=[f"dtb{d}"], dma=True, fam="consts3")
            EMIT("v", (lambda dd=d: (lambda h: h.memset(sb[f"carry{dd}"][:], 0.0)))(),
                 writes=[f"carry{d}"])

        p1 = contextlib.ExitStack()
        with p1:
            def S1(name, shape, dtype=F32):
                ST(name, shape, dtype, stack=p1)

            for i in range(2):
                for sp in range(2):
                    S1(f"xT{i}_{sp}", [128, DM])
                    S1(f"mean{i}_{sp}", [128, 1]); S1(f"ex2{i}_{sp}", [128, 1])
                    S1(f"var{i}_{sp}", [128, 1]); S1(f"rstd{i}_{sp}", [128, 1])
                    S1(f"nmr{i}_{sp}", [128, 1])
                S1(f"sq{i}", [128, DM])
            for k in range(8):
                S1(f"winT{k}", [128, E], BF16)
                S1(f"hxnd16_{k}", [128, 3], BF16)
            for sp in range(2):
                S1(f"xndC_{sp}", [128, 8 * T1]); S1(f"xndC16_{sp}", [128, 8 * T1], BF16)
                S1(f"q16C_{sp}", [128, 8 * T1], BF16)
            for d in range(16):
                S1(f"wxT{d}", [128, 33], BF16)
                S1(f"xss{d}", [128, T1 + 3], BF16)
            S1("xcw", [128, 16 * T1], BF16)
            S1("z16w", [128, 16 * T1], BF16)
            S1("wld", [128, 1024])
            S1("cacc_0", [128, 8 * T1]); S1("cacc_1", [128, 8 * T1])
            S1("proj_0", [33, T1], BF16); S1("proj_1", [33, T1], BF16)
            S1("qv", [128, 8 * T1]); S1("qc", [128, 8 * T1]); S1("qw", [128, 8 * T1])
            S1("qhi", [128, 8 * T1])
            S1("qm", [128, 8 * T1], U8)
            S1("c2p", [128, 1]); S1("c2n", [128, 1])
            for d in range(16):
                S1(f"cw{d}", [128, 4]); S1(f"cb{d}", [128, 1])

            EMIT("v", lambda h: h.memset(sb["c2p"][:], C2C), writes=["c2p"])
            EMIT("v", lambda h: h.memset(sb["c2n"][:], -C2C), writes=["c2n"])
            for d in range(16):
                EMIT("q", (lambda dd=d: (lambda h: h.dma_start(
                    out=sb[f"cw{dd}"][:], in_=ins["cw"][dd * 128:(dd + 1) * 128, :])))(),
                    writes=[f"cw{d}"], dma=True, fam="consts")
                EMIT("q", (lambda dd=d: (lambda h: h.dma_start(
                    out=sb[f"cb{dd}"][:], in_=ins["cb"][dd * 128:(dd + 1) * 128, :])))(),
                    writes=[f"cb{d}"], dma=True, fam="consts")
            for k in range(8):
                for c4 in range(4):
                    EMIT("q", (lambda kk=k, cc=c4: (lambda h: h.dma_start(
                        out=sb["wld"][:],
                        in_=ins["winT"].ap()[kk * 128:(kk + 1) * 128,
                                             cc * 1024:(cc + 1) * 1024])))(),
                        writes=["wld"], dma=True, fam="wld")
                    EMIT("v", (lambda kk=k, cc=c4: (lambda h: h.tensor_copy(
                        out=sb[f"winT{kk}"][:, cc * 1024:(cc + 1) * 1024],
                        in_=sb["wld"][:])))(),
                        reads=["wld"], writes=[f"winT{k}"])
            for d in range(16):
                EMIT("q", (lambda dd=d: (lambda h: h.dma_start(
                    out=sb["wld"][:, 0:33],
                    in_=ins["wxT"][dd * 128:(dd + 1) * 128, :])))(),
                    writes=["wld"], dma=True, fam="wld")
                EMIT("v", (lambda dd=d: (lambda h: h.tensor_copy(
                    out=sb[f"wxT{dd}"][:], in_=sb["wld"][:, 0:33])))(),
                    reads=["wld"], writes=[f"wxT{d}"])

            def ln_apply(kx, ksq, km, ke, kv, kr, kn):
                kxn = kx
                xT, sq = sb[kx], sb[ksq]
                mean, ex2, var, rstd, nmr = sb[km], sb[ke], sb[kv], sb[kr], sb[kn]
                EMIT("v", lambda h: h.tensor_tensor(out=sq[:], in0=xT[:], in1=xT[:],
                     op=ALU.mult), reads=[kx], writes=[ksq])
                EMIT("v", lambda h: h.reduce_sum(out=mean[:], in_=xT[:], axis=AXX),
                     reads=[kx], writes=[km])
                EMIT("v", lambda h: h.reduce_sum(out=ex2[:], in_=sq[:], axis=AXX),
                     reads=[ksq], writes=[ke])
                EMIT("v", lambda h: h.tensor_scalar(out=mean[:], in0=mean[:],
                     scalar1=1.0 / DM, scalar2=None, op0=ALU.mult),
                     reads=[km], writes=[km])
                EMIT("v", lambda h: h.tensor_scalar(out=ex2[:], in0=ex2[:],
                     scalar1=1.0 / DM, scalar2=None, op0=ALU.mult),
                     reads=[ke], writes=[ke])
                EMIT("v", lambda h: h.scalar_tensor_tensor(out=var[:], in0=mean[:],
                     scalar=mean[:], in1=ex2[:], op0=ALU.mult, op1=ALU.subtract),
                     reads=[km, ke], writes=[kv])
                EMIT("v", lambda h: h.tensor_scalar(out=var[:], in0=var[:],
                     scalar1=-1.0, scalar2=None, op0=ALU.mult),
                     reads=[kv], writes=[kv])
                EMIT("s", lambda h: h.activation(out=var[:], in_=var[:], func=AF.Ln,
                     bias=sb["eps"][:]), reads=[kv, "eps"], writes=[kv])
                EMIT("s", lambda h: h.activation(out=rstd[:], in_=var[:], func=AF.Exp,
                     scale=sb["mhalf"][:]), reads=[kv, "mhalf"], writes=[kr])
                EMIT("v", lambda h: h.scalar_tensor_tensor(out=nmr[:], in0=mean[:],
                     scalar=-1.0, in1=rstd[:], op0=ALU.mult, op1=ALU.mult),
                     reads=[km, kr], writes=[kn])
                EMIT("s", lambda h: h.activation(out=sb[kxn][:], in_=xT[:],
                     func=AF.Identity, scale=rstd[:], bias=nmr[:]),
                     reads=[kx, kr, kn], writes=[kxn])

            EMIT("v", lambda h: h.memset(sb["xT0_0"][:], 0.0), writes=["xT0_0"])
            EMIT("q", lambda h: h.dma_start(out=sb["xT0_0"][0:3, :], in_=ins["xh"][:, :]),
                 reads=["xT0_0"], writes=["xT0_0"], dma=True, fam="xh")
            ln_apply("xT0_0", "sq0", "mean0_0", "ex20_0", "var0_0", "rstd0_0", "nmr0_0")
            for k in range(8):
                pk = k % 2
                EMIT("t", (lambda kk=k, p=pk: (lambda h: h.transpose(
                    out=psum[p][:, 0:128],
                    in_=sb["xT0_0"][:, kk * 128:(kk + 1) * 128],
                    identity=sb["identf"][:])))(),
                    reads=["xT0_0", "identf"], writes=[f"pb{pk}"])
                EMIT("s", (lambda kk=k, p=pk: (lambda h: h.activation(
                    out=sb[f"hxnd16_{kk}"][:], in_=psum[p][:, 0:3],
                    func=AF.Identity)))(),
                    reads=[f"pb{pk}"], writes=[f"hxnd16_{k}"])
            for e in range(16):
                pe_ = 2 + e % 2

                def mmgroup(h, ee=e, p=pe_):
                    return [h.matmul(out=psum[p][:, 0:3],
                                     lhsT=sb[f"winT{k}"][:, ee * 128:(ee + 1) * 128],
                                     rhs=sb[f"hxnd16_{k}"][:],
                                     start=(k == 0), stop=(k == 7))
                            for k in range(8)]
                pl.emit_group("t", mmgroup,
                              reads=[f"hxnd16_{k}" for k in range(8)]
                              + [f"winT{k}" for k in range(8)],
                              writes=[f"pb{pe_}"])
                EMIT("s", (lambda ee=e, p=pe_: (lambda h: h.activation(
                    out=sb[f"xss{ee}"][:, 0:3], in_=psum[p][:, 0:3],
                    func=AF.Identity)))(),
                    reads=[f"pb{pe_}"], writes=[f"xss{e}"])

            def emit_quant(xndCv, q16Cv, c0v, s1v):
                EMIT("v", (lambda bx=xndCv: (lambda h: h.tensor_scalar(
                    out=sb["qv"][:], in0=sb[bx][:], scalar1=sb["invs"][:],
                    scalar2=-448.0, op0=ALU.mult, op1=ALU.max)))(),
                    reads=[f"{xndCv}:{kc2}" for kc2 in range(8)] + ["invs"],
                    writes=["qv"])
                EMIT("v", lambda h: h.tensor_scalar(out=sb["qv"][:], in0=sb["qv"][:],
                     scalar1=448.0, scalar2=None, op0=ALU.min),
                     reads=["qv"], writes=["qv"])
                EMIT("v", lambda h: h.tensor_scalar(out=sb["qc"][:], in0=sb["qv"][:],
                     scalar1=K1C, scalar2=None, op0=ALU.mult),
                     reads=["qv"], writes=["qc"])
                EMIT("v", lambda h: h.scalar_tensor_tensor(out=sb["qw"][:],
                     in0=sb["qv"][:], scalar=K1C, in1=sb["qv"][:],
                     op0=ALU.mult, op1=ALU.subtract),
                     reads=["qv"], writes=["qw"])
                EMIT("v", lambda h: h.tensor_tensor(out=sb["qhi"][:], in0=sb["qc"][:],
                     in1=sb["qw"][:], op=ALU.subtract),
                     reads=["qc", "qw"], writes=["qhi"])
                EMIT("v", lambda h: h.tensor_scalar(out=sb["qc"][:], in0=sb["qv"][:],
                     scalar1=C2C, scalar2=None, op0=ALU.add),
                     reads=["qv"], writes=["qc"])
                EMIT("v", lambda h: h.tensor_scalar(out=sb["qw"][:], in0=sb["qc"][:],
                     scalar1=C2C, scalar2=None, op0=ALU.subtract),
                     reads=["qc"], writes=["qw"])
                EMIT("v", lambda h: h.tensor_tensor(out=sb["qc"][:], in0=sb["qv"][:],
                     in1=sb["qv"][:], op=ALU.mult),
                     reads=["qv"], writes=["qc"])
                EMIT("v", lambda h: h.tensor_scalar(out=sb["qm"][:], in0=sb["qc"][:],
                     scalar1=float(2.0 ** -12), scalar2=None, op0=ALU.is_lt),
                     reads=["qc"], writes=["qm"])
                EMIT("v", lambda h: h.select(out=sb["qc"][:], mask=sb["qm"][:],
                     on_true=sb["qw"][:], on_false=sb["qhi"][:]),
                     reads=["qm", "qw", "qhi"], writes=["qc"])
                EMIT("v", (lambda q=q16Cv: (lambda h: h.tensor_copy(
                    out=sb[q][:], in_=sb["qc"][:])))(),
                    reads=["qc"], writes=[q16Cv])
                EMIT("g", (lambda q=q16Cv, cc=c0v: (lambda h: h.dma_start(
                    out=bass.AP(qs, cc, [[TOK, 128], [128 * TOK, 8], [1, T1]]),
                    in_=bass.AP(sb[q].tensor if hasattr(sb[q], "tensor") else None, 0,
                                [[8 * T1, 128], [T1, 8], [1, T1]])
                    if False else sb[q][:].rearrange("p (k j) -> p k j", k=8))))(),
                    reads=[q16Cv], writes=[f"qs_{s1v}"], dma=True, fam="qs")

            pending_q = [None]

            def issue_agA():
                aginA_keys = ([f"agin_{s}" for s in range(NS1 // 2)]
                              + [f"agin_{s}_p" for s in range(NS1 // 2)])
                pl.coll(lambda h: h.collective_compute(
                    "AllGather", ALU.bypass, replica_groups=[list(range(8))],
                    ins=[ag_inA.ap().opt()], outs=[ag_outA.ap().opt()]),
                    reads=aginA_keys, writes=["ag_outA"])

            for s1 in range(NS1):
                if s1 == NS1 // 2:
                    issue_agA()
                c0 = s1 * T1
                sp = s1 % 2
                xTn = [f"xT{i}_{sp}" for i in range(2)]
                xndC = f"xndC_{sp}"
                xndC16 = f"xndC16_{sp}"
                q16C = f"q16C_{sp}"
                xcwn = "xcw"
                caccn = ("cacc_0", "cacc_1")
                projn = f"proj_{sp}"
                for i in range(2):
                    EMIT("q", (lambda ii=i, cc=c0, bx=xTn[i]: (lambda h: h.dma_start(
                        out=sb[bx][:],
                        in_=ins["xs"][cc + ii * 128: cc + (ii + 1) * 128, :])))(),
                        writes=[xTn[i]], dma=True, fam=f"xT{sp}")
                for i in range(2):
                    ln_apply(xTn[i], f"sq{i}", f"mean{i}_{sp}", f"ex2{i}_{sp}",
                             f"var{i}_{sp}", f"rstd{i}_{sp}", f"nmr{i}_{sp}")
                if pending_q[0] is not None:
                    emit_quant(*pending_q[0])
                    pending_q[0] = None
                for kc in range(8):
                    p = kc % 2
                    for i in range(2):
                        EMIT("t", (lambda kk=kc, ii=i, pp=p, bx=xTn[i]: (lambda h: h.transpose(
                            out=psum[pp][:, ii * 128:(ii + 1) * 128],
                            in_=sb[bx][:, kk * 128:(kk + 1) * 128],
                            identity=sb["identf"][:])))(),
                            reads=[xTn[i], "identf"], writes=[f"pb{p}"])
                    EMIT("s", (lambda kk=kc, pp=p, bx=xndC: (lambda h: h.activation(
                        out=sb[bx][:, kk * T1:(kk + 1) * T1], in_=psum[pp][:, 0:T1],
                        func=AF.Identity)))(),
                        reads=[f"pb{p}"], writes=[f"{xndC}:{kc}"])
                EMIT("v", (lambda b1=xndC, b2=xndC16: (lambda h: h.tensor_copy(
                    out=sb[b2][:], in_=sb[b1][:])))(),
                    reads=[f"{xndC}:{kc2}" for kc2 in range(8)], writes=[xndC16])
                def conv_d(d):
                    ca = caccn[d % 2]
                    sl = (d // 2) * T1
                    EMIT("v", (lambda dd=d, c_=ca, s_=sl: (lambda h: h.tensor_scalar(
                        out=sb[c_][:, s_:s_ + T1], in0=sb[f"xss{dd}"][:, 0:T1],
                        scalar1=sb[f"cw{dd}"][:, 0:1], scalar2=None, op0=ALU.mult)))(),
                        reads=[f"xss{d}", f"cw{d}"], writes=[f"{ca}:{d}"])
                    for kk in range(1, 4):
                        EMIT("v", (lambda dd=d, k2=kk, c_=ca, s_=sl: (lambda h: h.scalar_tensor_tensor(
                            out=sb[c_][:, s_:s_ + T1], in0=sb[f"xss{dd}"][:, k2:T1 + k2],
                            scalar=sb[f"cw{dd}"][:, k2:k2 + 1], in1=sb[c_][:, s_:s_ + T1],
                            op0=ALU.mult, op1=ALU.add)))(),
                            reads=[f"xss{d}", f"cw{d}", f"{ca}:{d}"], writes=[f"{ca}:{d}"])
                    EMIT("v", (lambda dd=d: (lambda h: h.tensor_copy(
                        out=sb[f"xss{dd}"][:, 0:3],
                        in_=sb[f"xss{dd}"][:, T1:T1 + 3])))(),
                        reads=[f"xss{d}"], writes=[f"xss{d}"])

                def conv_tail(d):
                    ca = caccn[d % 2]
                    sl = (d // 2) * T1
                    EMIT("s", (lambda dd=d, c_=ca, s_=sl, bx=xcwn: (lambda h: h.activation(
                        out=sb[bx][:, dd * T1:(dd + 1) * T1],
                        in_=sb[c_][:, s_:s_ + T1], func=AF.Silu,
                        bias=sb[f"cb{dd}"][:])))(),
                        reads=[f"{ca}:{d}", f"cb{d}"], writes=[f"{xcwn}:{d}"])

                for e in range(32):
                    p = (2, 3, 5, 6)[e % 4]

                    def mmg(h, ee=e, pp=p, bx=xndC16):
                        return [h.matmul(out=psum[pp][:, 0:T1],
                                         lhsT=sb[f"winT{k}"][:, ee * 128:(ee + 1) * 128],
                                         rhs=sb[bx][:, k * T1:(k + 1) * T1],
                                         start=(k == 0), stop=(k == 7))
                                for k in range(8)]
                    pl.emit_group("t", mmg,
                                  reads=[xndC16]
                                  + [f"winT{k}" for k in range(8)],
                                  writes=[f"pb{p}"])
                    if e < 16:
                        EMIT("s", (lambda ee=e, pp=p: (lambda h: h.activation(
                            out=sb[f"xss{ee}"][:, 3:T1 + 3], in_=psum[pp][:, 0:T1],
                            func=AF.Identity)))(),
                            reads=[f"pb{p}"], writes=[f"xss{e}"])
                        conv_d(e)
                    else:
                        EMIT("s", (lambda ee=e, pp=p: (lambda h: h.activation(
                            out=sb["z16w"][:, (ee - 16) * T1:(ee - 15) * T1],
                            in_=psum[pp][:, 0:T1],
                            func=AF.Identity)))(),
                            reads=[f"pb{p}"], writes=[f"z16w:{e - 16}"])
                EMIT("g", (lambda cc=c0: (lambda h: h.dma_start(
                    out=bass.AP(zs, cc, [[TOK, 128], [128 * TOK, 16], [1, T1]]),
                    in_=sb["z16w"][:].rearrange("p (k j) -> p k j", k=16))))(),
                    reads=[f"z16w:{j}" for j in range(16)],
                    writes=[f"zs_{s1}"], dma=True, fam="zs")

                for d in range(16):
                    conv_tail(d)
                agt = ag_inA if s1 < NS1 // 2 else ag_inB
                cl = c0 % (TOK // 2)
                EMIT("g", (lambda cc=cl, at=agt, bx=xcwn: (lambda h: h.dma_start(
                    out=bass.AP(at, cc,
                                [[TOK // 2, 128], [128 * (TOK // 2), 16], [1, T1]]),
                    in_=sb[bx][:].rearrange("p (k j) -> p k j", k=16))))(),
                    reads=[f"{xcwn}:{d}" for d in range(16)],
                    writes=[f"agin_{s1}"], dma=True, fam="agin")
                def wxg(h, bx=xcwn):
                    return [h.matmul(out=psum[4][0:33, 0:T1], lhsT=sb[f"wxT{d}"][:],
                                     rhs=sb[bx][:, d * T1:(d + 1) * T1],
                                     start=(d == 0), stop=(d == 15))
                            for d in range(16)]
                pl.emit_group("t", wxg,
                              reads=[f"{xcwn}:{d}" for d in range(16)]
                              + [f"wxT{d}" for d in range(16)],
                              writes=["pb4"])
                EMIT("v", (lambda pj=projn: (lambda h: h.tensor_copy(out=sb[pj][:],
                     in_=psum[4][0:33, 0:T1])))(),
                     reads=["pb4"], writes=[projn])
                agt2 = ag_inA if s1 < NS1 // 2 else ag_inB
                cl2 = c0 % (TOK // 2)
                EMIT("g", (lambda cc=cl2, pj=projn, at=agt2: (lambda h: h.dma_start(
                    out=at.ap()[DI:DI + 33, cc:cc + T1], in_=sb[pj][:])))(),
                    reads=[projn], writes=[f"agin_{s1}_p"], dma=True, fam="agin")

                pending_q[0] = (xndC, q16C, c0, s1)

            if pending_q[0] is not None:
                emit_quant(*pending_q[0])
                pending_q[0] = None

        pl.fence()
        aginB_keys = ([f"agin_{s}" for s in range(NS1 // 2, NS1)]
                      + [f"agin_{s}_p" for s in range(NS1 // 2, NS1)])
        pl.coll(lambda h: h.collective_compute(
            "AllGather", ALU.bypass, replica_groups=[list(range(8))],
            ins=[ag_inB.ap().opt()], outs=[ag_outB.ap().opt()]),
            reads=aginB_keys, writes=["ag_outB"])
        HT = TOK // 2
        for half in range(2):
            q_ = 2 * half
            dyn_copy(xcp.ap()[:, q_ * HT:(q_ + 1) * HT], ag_outA, "r_xc",
                     half * AGROWS * HT, [[HT, DH], [1, HT]],
                     reads=["ag_outA"], writes=[f"xcp_q{q_}"], fam=f"xcq{q_}")
            dyn_copy(projp.ap()[:, q_ * HT:(q_ + 1) * HT], ag_outA, "r_pj",
                     half * AGROWS * HT, [[HT, 33], [1, HT]],
                     reads=["ag_outA"], writes=[f"projp_q{q_}"], fam=f"pjq{q_}")
        for half in range(2):
            q_ = 2 * half + 1
            dyn_copy(xcp.ap()[:, q_ * HT:(q_ + 1) * HT], ag_outB, "r_xc",
                     half * AGROWS * HT, [[HT, DH], [1, HT]],
                     reads=["ag_outB"], writes=[f"xcp_q{q_}"], fam=f"xcq{q_}")
            dyn_copy(projp.ap()[:, q_ * HT:(q_ + 1) * HT], ag_outB, "r_pj",
                     half * AGROWS * HT, [[HT, 33], [1, HT]],
                     reads=["ag_outB"], writes=[f"projp_q{q_}"], fam=f"pjq{q_}")
        p23 = contextlib.ExitStack()
        with p23:
            def S23(name, shape, dtype=F32):
                ST(name, shape, dtype, stack=p23)

            S23("wld3a", [128, DM]); S23("wld3b", [128, DM])
            for d in range(16):
                S23(f"woT{d}", [128, DM], BF16)
                S23(f"dp{d}", [128, 1])
            for k in range(8):
                S23(f"wfT{k}", [128, DM], BF16)
            for m in range(8):
                S23(f"fb{m}", [128, 1])

            p2 = contextlib.ExitStack()
            with p2:
                def S2(name, shape, dtype=F32):
                    ST(name, shape, dtype, stack=p2)

                for d in range(8):
                    S2(f"dt16_{d}", [128, Q2], BF16)
                    S2(f"u16_{d}", [128, Q2], BF16)
                    S2(f"xc2_{d}", [128, Q2], BF16)
                S2("dtbc", [128, Q2], BF16)
                for i in range(2):
                    S2(f"zp{i}", [128, Q2]); S2(f"rel{i}", [128, Q2])
                    S2(f"nab{i}", [128, Q2]); S2(f"lnb{i}", [128, Q2])
                for p in range(2):
                    S2(f"Bb{p}", [128, Q2], BF16); S2(f"Cb{p}", [128, Q2], BF16)
                    S2(f"dA{p}", [128, Q2]); S2(f"dBx{p}", [128, Q2], BF16)
                    S2(f"hh{p}", [128, Q2], BF16); S2(f"gy{p}", [128, Q2], BF16)
                    S2(f"y16w_{p}", [128, 1024], BF16)
                for d in range(16):
                    EMIT("q", (lambda dd=d: (lambda h: h.dma_start(
                        out=sb[f"dp{dd}"][:],
                        in_=ins["dparam"][dd * 128:(dd + 1) * 128, :])))(),
                        writes=[f"dp{d}"], dma=True, fam="consts4")
                for m in range(8):
                    EMIT("q", (lambda mm=m: (lambda h: h.dma_start(
                        out=sb[f"fb{mm}"][:],
                        in_=ins["fbias"][mm * 128:(mm + 1) * 128, :])))(),
                        writes=[f"fb{m}"], dma=True, fam="consts4")

                for dgrp in range(2):
                    d4 = dgrp * 4
                    for c in range(NC2):
                        EMIT("q", (lambda cc=c: (lambda h: h.dma_start(out=sb["dtbc"][:],
                             in_=bass.AP(projp, cc * Q2, [[0, 128], [1, Q2]]))))(),
                             reads=[f"projp_q{c // 2}"], writes=["dtbc"], dma=True, fam="dtbc")
                        for d in range(d4, d4 + 4):
                            EMIT("q", (lambda dd=d, cc=c: (lambda h: h.dma_start(
                                out=sb[f"xc2_{dd}"][:],
                                in_=xcp.ap()[dd * 128:(dd + 1) * 128,
                                             cc * Q2:(cc + 1) * Q2])))(),
                                reads=[f"xcp_q{c // 2}"], writes=[f"xc2_{d}"], dma=True, fam="xc2")
                        for j in range(3):
                            w = 3 * (dgrp * NC2 + c) + j
                            if w >= 24:
                                continue
                            stg = "wld3a" if w % 2 == 0 else "wld3b"
                            famw = "wl3a" if w % 2 == 0 else "wl3b"
                            if w < 16:
                                srct, row, dstb = ins["woutT"], w, f"woT{w}"
                            else:
                                srct, row, dstb = ins["wfT"], w - 16, f"wfT{w - 16}"
                            EMIT("q", (lambda s_=srct, r_=row, g_=stg: (lambda h: h.dma_start(
                                out=sb[g_][:], in_=s_[r_ * 128:(r_ + 1) * 128, :])))(),
                                writes=[stg], dma=True, fam=famw)
                            EMIT("g", (lambda g_=stg, b_=dstb: (lambda h: h.tensor_copy(
                                out=sb[b_][:], in_=sb[g_][:])))(),
                                reads=[stg], writes=[dstb])
                        for d in range(d4, d4 + 4):
                            i2 = d % 2
                            zp, rel = f"zp{i2}", f"rel{i2}"
                            nab, lnb = f"nab{i2}", f"lnb{i2}"
                            EMIT("s", (lambda dd=d, z=zp: (lambda h: h.activation(
                                out=sb[z][:], in_=sb["dtbc"][:], func=AF.Identity,
                                scale=sb[f"dtw{dd}"][:], bias=sb[f"dtb{dd}"][:])))(),
                                reads=["dtbc", f"dtw{d}", f"dtb{d}"], writes=[zp])
                            EMIT("v", (lambda z=zp, r=rel: (lambda h: h.tensor_scalar(
                                out=sb[r][:], in0=sb[z][:],
                                scalar1=0.0, scalar2=None, op0=ALU.max)))(),
                                reads=[zp], writes=[rel])
                            EMIT("v", (lambda z=zp, nb=nab: (lambda h: h.scalar_tensor_tensor(
                                out=sb[nb][:],
                                in0=sb[z][:], scalar=-1.0, in1=sb[z][:],
                                op0=ALU.mult, op1=ALU.min)))(),
                                reads=[zp], writes=[nab])
                            EMIT("s", (lambda nb=nab: (lambda h: h.activation(
                                out=sb[nb][:], in_=sb[nb][:],
                                func=AF.Exp)))(), reads=[nab], writes=[nab])
                            EMIT("s", (lambda nb=nab, lb=lnb: (lambda h: h.activation(
                                out=sb[lb][:], in_=sb[nb][:],
                                func=AF.Ln, bias=sb["one"][:])))(),
                                reads=[nab, "one"], writes=[lnb])
                            EMIT("v", (lambda dd=d, r=rel, lb=lnb: (lambda h: h.tensor_tensor(
                                out=sb[f"dt16_{dd}"][:], in0=sb[r][:], in1=sb[lb][:],
                                op=ALU.add)))(),
                                reads=[rel, lnb], writes=[f"dt16_{d}"])
                            EMIT("v", (lambda dd=d: (lambda h: h.tensor_tensor(
                                out=sb[f"u16_{dd}"][:], in0=sb[f"dt16_{dd}"][:],
                                in1=sb[f"xc2_{dd}"][:], op=ALU.mult)))(),
                                reads=[f"dt16_{d}", f"xc2_{d}"], writes=[f"u16_{d}"])
                        for n in range(N):
                            p = n % 2
                            EMIT("q", (lambda nn=n, pp=p, cc=c: (lambda h: h.dma_start(
                                out=sb[f"Bb{pp}"][:],
                                in_=bass.AP(projp, (1 + nn) * L + cc * Q2,
                                            [[0, 128], [1, Q2]]))))(),
                                reads=[f"projp_q{c // 2}"], writes=[f"Bb{p}"], dma=True, fam=f"bb{p}")
                            EMIT("q", (lambda nn=n, pp=p, cc=c: (lambda h: h.dma_start(
                                out=sb[f"Cb{pp}"][:],
                                in_=bass.AP(projp, (17 + nn) * L + cc * Q2,
                                            [[0, 128], [1, Q2]]))))(),
                                reads=[f"projp_q{c // 2}"], writes=[f"Cb{p}"], dma=True, fam=f"cb2{p}")
                            for d in range(d4, d4 + 4):
                                q = (n * 4 + d) % 2
                                EMIT("s", (lambda dd=d, nn=n, qq=q: (lambda h: h.activation(
                                    out=sb[f"dA{qq}"][:], in_=sb[f"dt16_{dd}"][:],
                                    func=AF.Exp,
                                    scale=sb[f"negA{dd}"][:, nn:nn + 1])))(),
                                    reads=[f"dt16_{d}", f"negA{d}"], writes=[f"dA{q}"])
                                EMIT("v", (lambda dd=d, pp=p, qq=q: (lambda h: h.tensor_tensor(
                                    out=sb[f"dBx{qq}"][:], in0=sb[f"u16_{dd}"][:],
                                    in1=sb[f"Bb{pp}"][:], op=ALU.mult)))(),
                                    reads=[f"u16_{d}", f"Bb{p}"], writes=[f"dBx{q}"])
                                EMIT("v", (lambda dd=d, nn=n, qq=q: (lambda h: h.tensor_tensor_scan(
                                    out=sb[f"hh{qq}"][:], data0=sb[f"dA{qq}"][:],
                                    data1=sb[f"dBx{qq}"][:],
                                    initial=sb[f"carry{dd}"][:, nn:nn + 1],
                                    op0=ALU.mult, op1=ALU.add)))(),
                                    reads=[f"dA{q}", f"dBx{q}", f"carry{d}"],
                                    writes=[f"hh{q}"])
                                EMIT("v", (lambda dd=d, nn=n, qq=q: (lambda h: h.tensor_copy(
                                    out=sb[f"carry{dd}"][:, nn:nn + 1],
                                    in_=sb[f"hh{qq}"][:, Q2 - 1:Q2])))(),
                                    reads=[f"hh{q}"], writes=[f"carry{d}"])
                                EMIT("v", (lambda dd=d, pp=p, qq=q: (lambda h: h.tensor_tensor(
                                    out=sb[f"gy{qq}"][:], in0=sb[f"hh{qq}"][:],
                                    in1=sb[f"Cb{pp}"][:], op=ALU.mult)))(),
                                    reads=[f"hh{q}", f"Cb{p}"], writes=[f"gy{q}"])
                                for hf in range(2):
                                    bk = 2 * (d - d4) + hf
                                    EMIT("t", (lambda dd=d, nn=n, qq=q, hh2=hf, bb=bk:
                                         (lambda h: h.matmul(
                                            out=psum[bb][:],
                                            lhsT=sb["ident16"][:],
                                            rhs=sb[f"gy{qq}"][:, hh2 * 512:(hh2 + 1) * 512],
                                            start=(nn == 0), stop=(nn == N - 1),
                                            skip_group_check=True)))(),
                                        reads=[f"gy{q}", "ident16"], writes=[f"pb{bk}"])
                        for d in range(d4, d4 + 4):
                            yb = f"y16w_{d % 2}"
                            for hf in range(2):
                                bk = 2 * (d - d4) + hf
                                EMIT("s", (lambda bb=bk, y_=yb, hh2=hf: (lambda h: h.activation(
                                    out=sb[y_][:, hh2 * 512:(hh2 + 1) * 512],
                                    in_=psum[bb][:],
                                    func=AF.Identity)))(),
                                    reads=[f"pb{bk}"], writes=[f"{yb}:{hf}"])
                            agt3 = agy_inA if d < 4 else agy_inB
                            EMIT("g", (lambda dd=d, y_=yb, cc=c, at=agt3: (lambda h: h.dma_start(
                                out=at.ap()[(dd % 4) * 128:(dd % 4 + 1) * 128,
                                            cc * Q2:(cc + 1) * Q2],
                                in_=sb[y_][:])))(),
                                reads=[f"{yb}:0", f"{yb}:1"], writes=[f"agy_{c}_{d}"],
                                dma=True, fam=f"agy{d % 2}")
                    if dgrp == 0:
                        agyA_keys = [f"agy_{cc}_{dd}" for cc in range(NC2)
                                     for dd in range(4)]
                        pl.coll(lambda h: h.collective_compute(
                            "AllGather", ALU.bypass, replica_groups=[list(range(8))],
                            ins=[agy_inA.ap().opt()], outs=[agy_outA.ap().opt()]),
                            reads=agyA_keys, writes=["agy_outA"])

            pl.fence()
            agyB_keys = [f"agy_{c}_{d}" for c in range(NC2) for d in range(4, 8)]
            pl.coll(lambda h: h.collective_compute(
                "AllGather", ALU.bypass, replica_groups=[list(range(8))],
                ins=[agy_inB.ap().opt()], outs=[agy_outB.ap().opt()]),
                reads=agyB_keys, writes=["agy_outB"])
            for half in range(2):
                dyn_copy(yp.ap()[half * DH:half * DH + 512, :], agy_outA, "r_y",
                         half * 512 * L, [[L, 512], [1, TOK]],
                         reads=["agy_outA"], writes=[f"yp_q{2 * half}"],
                         fam=f"ypq{2 * half}")
            for half in range(2):
                dyn_copy(yp.ap()[half * DH + 512:(half + 1) * DH, :], agy_outB, "r_y",
                         half * 512 * L, [[L, 512], [1, TOK]],
                         reads=["agy_outB"], writes=[f"yp_q{2 * half + 1}"],
                         fam=f"ypq{2 * half + 1}")
            p3 = contextlib.ExitStack()
            with p3:
                def S3(name, shape, dtype=F32):
                    ST(name, shape, dtype, stack=p3)

                for sp in range(2):
                    S3(f"y3w_{sp}", [128, 16 * T3], BF16)
                    S3(f"xc3w_{sp}", [128, 16 * T3], BF16)
                    S3(f"z3w_{sp}", [128, 16 * T3], BF16)
                    S3(f"q3w_{sp}", [128, 8 * T3], BF16)
                    S3(f"x3w_{sp}", [128, 8 * T3])
                for p in range(8):
                    S3(f"e1_{p}", [128, T3])
                for p in range(2):
                    S3(f"e2_{p}", [128, T3])
                    S3(f"odw_{p}", [128, 8 * T3])

                for s3 in range(NS3):
                    c0 = s3 * T3
                    sp = s3 % 2
                    y3w, xc3w, z3w = f"y3w_{sp}", f"xc3w_{sp}", f"z3w_{sp}"
                    q3w, x3w = f"q3w_{sp}", f"x3w_{sp}"
                    EMIT("q", (lambda cc=c0, b=q3w: (lambda h: h.dma_start(
                        out=sb[b][:].rearrange("p (k j) -> p k j", k=8),
                        in_=bass.AP(qs, cc, [[TOK, 128], [128 * TOK, 8], [1, T3]]))))(),
                        reads=["qsall"], writes=[f"{q3w}:{k}" for k in range(8)],
                        dma=True, fam=f"q3{sp}")
                    EMIT("q", (lambda cc=c0, b=x3w: (lambda h: h.dma_start(
                        out=sb[b][:].rearrange("p (k j) -> p k j", k=8),
                        in_=bass.AP(ins["xsT"], cc,
                                    [[TOK, 128], [128 * TOK, 8], [1, T3]]))))(),
                        writes=[f"{x3w}:{m}" for m in range(8)],
                        dma=True, fam=f"x3{sp}")
                    EMIT("q", (lambda cc=c0, b=y3w: (lambda h: h.dma_start(
                        out=sb[b][:].rearrange("p (k j) -> p k j", k=16),
                        in_=bass.AP(yp, cc, [[TOK, 128], [128 * TOK, 16], [1, T3]]))))(),
                        reads=[f"yp_q{qq}" for qq in range(4)],
                        writes=[f"{y3w}:{d}" for d in range(16)],
                        dma=True, fam=f"y3{sp}")
                    ag3 = ag_inA if s3 < NS3 // 2 else ag_inB
                    cl3 = c0 % (TOK // 2)
                    EMIT("q", (lambda cc=cl3, b=xc3w, at=ag3: (lambda h: h.dma_start(
                        out=sb[b][:].rearrange("p (k j) -> p k j", k=16),
                        in_=bass.AP(at, cc,
                                    [[TOK // 2, 128], [128 * (TOK // 2), 16], [1, T3]]))))(),
                        reads=["againall"], writes=[f"{xc3w}:{d}" for d in range(16)],
                        dma=True, fam=f"xc3{sp}")
                    EMIT("q", (lambda cc=c0, b=z3w: (lambda h: h.dma_start(
                        out=sb[b][:].rearrange("p (k j) -> p k j", k=16),
                        in_=bass.AP(zs, cc, [[TOK, 128], [128 * TOK, 16], [1, T3]]))))(),
                        reads=["zsall"], writes=[f"{z3w}:{d}" for d in range(16)],
                        dma=True, fam=f"z3{sp}")
                    for m in range(8):
                        pf = (4, 7)[m % 2]
                        pe1 = m

                        def ffg(h, mm=m, pp=pf, b=q3w):
                            return [h.matmul(out=psum[pp][:, 0:T3],
                                             lhsT=sb[f"wfT{k}"][:, mm * 128:(mm + 1) * 128],
                                             rhs=sb[b][:, k * T3:(k + 1) * T3],
                                             start=(k == 0), stop=(k == 7))
                                    for k in range(8)]
                        pl.emit_group("t", ffg,
                                      reads=[f"{q3w}:{k}" for k in range(8)]
                                      + [f"wfT{k}" for k in range(8)],
                                      writes=[f"pb{pf}"])
                        EMIT("s", (lambda mm=m, pf_=pf, p_=pe1: (lambda h: h.activation(
                            out=sb[f"e1_{p_}"][:], in_=psum[pf_][:, 0:T3],
                            func=AF.Identity, scale=sb["sswb"][:],
                            bias=sb[f"fb{mm}"][:])))(),
                            reads=[f"pb{pf}", "sswb", f"fb{m}"], writes=[f"e1_{pe1}"])
                    for d in range(16):
                        sl = slice(d * T3, (d + 1) * T3)
                        EMIT("s", (lambda b=z3w, s_=sl: (lambda h: h.activation(
                            out=sb[b][:, s_], in_=sb[b][:, s_], func=AF.Silu)))(),
                            reads=[f"{z3w}:{d}"], writes=[f"{z3w}:{d}"])
                        EMIT("v", (lambda dd=d, by=y3w, bx=xc3w, s_=sl: (lambda h: h.scalar_tensor_tensor(
                            out=sb[by][:, s_], in0=sb[bx][:, s_],
                            scalar=sb[f"dp{dd}"][:], in1=sb[by][:, s_],
                            op0=ALU.mult, op1=ALU.add)))(),
                            reads=[f"{xc3w}:{d}", f"dp{d}", f"{y3w}:{d}"],
                            writes=[f"{y3w}:{d}"])
                        EMIT("v", (lambda by=y3w, bz=z3w, s_=sl: (lambda h: h.tensor_tensor(
                            out=sb[by][:, s_], in0=sb[by][:, s_],
                            in1=sb[bz][:, s_], op=ALU.mult)))(),
                            reads=[f"{y3w}:{d}", f"{z3w}:{d}"], writes=[f"{y3w}:{d}"])
                    for m in range(8):
                        pw = (2, 3, 5, 6)[m % 4]
                        pe1 = m
                        pp = m % 2

                        def wog(h, mm=m, pp=pw, b=y3w):
                            return [h.matmul(out=psum[pp][:, 0:T3],
                                             lhsT=sb[f"woT{d}"][:, mm * 128:(mm + 1) * 128],
                                             rhs=sb[b][:, d * T3:(d + 1) * T3],
                                             start=(d == 0), stop=(d == 15))
                                    for d in range(16)]
                        pl.emit_group("t", wog,
                                      reads=[f"{y3w}:{d}" for d in range(16)]
                                      + [f"woT{d}" for d in range(16)],
                                      writes=[f"pb{pw}"])
                        EMIT("v", (lambda mm=m, pw_=pw, p_=pp, pe=pe1: (lambda h: h.tensor_tensor(
                            out=sb[f"e2_{p_}"][:], in0=psum[pw_][:, 0:T3],
                            in1=sb[f"e1_{pe}"][:], op=ALU.add)))(),
                            reads=[f"pb{pw}", f"e1_{pe1}"], writes=[f"e2_{pp}"])
                        EMIT("v", (lambda mm=m, p_=sp, bx=x3w, pe=pp: (lambda h: h.tensor_tensor(
                            out=sb[f"odw_{p_}"][:, mm * T3:(mm + 1) * T3],
                            in0=sb[f"e2_{pe}"][:],
                            in1=sb[bx][:, mm * T3:(mm + 1) * T3], op=ALU.add)))(),
                            reads=[f"e2_{pp}", f"{x3w}:{m}"], writes=[f"odw_{sp}:{m}"])
                    EMIT("g", (lambda p_=sp, cc=c0: (lambda h: h.dma_start(
                        out=bass.AP(out_d, cc, [[TOK, 128], [128 * TOK, 8], [1, T3]]),
                        in_=sb[f"odw_{p_}"][:].rearrange("p (k j) -> p k j", k=8))))(),
                        reads=[f"odw_{sp}:{m}" for m in range(8)],
                        writes=[f"out_{s3}"], dma=True, fam="outd")

        if debug:
            pl.fence(("q",))
            for nm, (dst, srct) in dbg.items():
                pl.emit("q", (lambda d_=dst, s_=srct: (lambda h: h.dma_start(
                    out=d_.ap()[:, :], in_=s_.ap()[:, :])))(),
                    reads=[], writes=[nm], dma=True, fam="outd")
        pl.fence(("g",))
        pl.emit("g", lambda h: h.nop())

        global _LAST_PLAN
        _LAST_PLAN = pl
        with nc.Block() as block:
            pl.replay(nc, block, es)
    return nc


_NC_CACHE = None
_LAST_PLAN = None


def kernel(L0_attn_x, L0_attn_W_in, L0_attn_conv_w, L0_attn_conv_b, L0_attn_W_x,
           L0_attn_dt_w, L0_attn_dt_b, L0_attn_A_log, L0_attn_D_param,
           L0_attn_W_out, L0_ffn_weight_fp8, L0_ffn_bias, L0_ffn_input_scale,
           L0_ffn_weight_scale, _debug=False):
    f32 = np.float32
    x = np.asarray(L0_attn_x, f32)
    winT = np.ascontiguousarray(np.asarray(L0_attn_W_in, f32).T)
    wxT = np.ascontiguousarray(np.asarray(L0_attn_W_x, f32).T)
    woutT = np.ascontiguousarray(np.asarray(L0_attn_W_out, f32).T)
    wfT = np.ascontiguousarray(np.asarray(L0_ffn_weight_fp8, f32).T)
    cw = np.ascontiguousarray(np.asarray(L0_attn_conv_w, f32))
    cb = np.asarray(L0_attn_conv_b, f32).reshape(DI, 1)
    dtw = np.asarray(L0_attn_dt_w, f32).reshape(DI, 1)
    dtb = np.asarray(L0_attn_dt_b, f32).reshape(DI, 1)
    alog = np.ascontiguousarray(np.asarray(L0_attn_A_log, f32))
    dparam = np.asarray(L0_attn_D_param, f32).reshape(DI, 1)
    fbias = np.asarray(L0_ffn_bias, f32).reshape(DM, 1)
    sin = np.asarray(L0_ffn_input_scale, f32).reshape(1, 1)
    sw = np.asarray(L0_ffn_weight_scale, f32).reshape(1, 1)
    ident = np.eye(128, dtype=f32)

    global _NC_CACHE
    if _debug:
        nc = build(debug=True)
    else:
        if _NC_CACHE is None:
            _NC_CACHE = build()
        nc = _NC_CACHE

    in_maps = []
    for k in range(8):
        b, h = k // 2, k % 2
        t0 = h * TOK
        xs_ = np.ascontiguousarray(x[b, t0:t0 + TOK, :])
        xh_ = (np.zeros((3, DM), f32) if h == 0
               else np.ascontiguousarray(x[b, t0 - 3:t0, :]))
        dh0 = h * DH
        ridx_ = np.array([
            [(2 * b * AGROWS + h * DH) * (TOK // 2)],
            [(2 * b * AGROWS + DI) * (TOK // 2)],
            [2 * b * 512 * L + h * TOK],
            [k * AGROWS * TOK]], dtype=np.int32)
        in_maps.append(dict(
            xs=xs_, xsT=np.ascontiguousarray(xs_.T), xh=xh_, identin=ident,
            winT=winT, wxT=wxT, woutT=woutT,
            wfT=wfT, cw=cw, cb=cb,
            dtw_s=np.ascontiguousarray(dtw[dh0:dh0 + DH]),
            dtb_s=np.ascontiguousarray(dtb[dh0:dh0 + DH]),
            alog_s=np.ascontiguousarray(alog[dh0:dh0 + DH]),
            dparam=dparam, fbias=fbias, sin=sin, sw=sw, ridx=ridx_))
    res = run_bass_kernel_spmd(nc, in_maps, core_ids=list(range(8)))
    out = np.empty((B, L, DM), f32)
    for k in range(8):
        b, h = k // 2, k % 2
        out[b, h * TOK:(h + 1) * TOK, :] = np.asarray(res.results[k]["out_d"]).T
    if _debug:
        return out, res
    return out


# version 28
# speedup vs baseline: 1.0440x; 1.0127x over previous
import contextlib
import numpy as np
import concourse.bass as bass
import concourse.mybir as mybir
from concourse.bass_utils import run_bass_kernel_spmd

F32 = mybir.dt.float32
BF16 = mybir.dt.bfloat16
U8 = mybir.dt.uint8
I32 = mybir.dt.int32
AF = mybir.ActivationFunctionType
ALU = mybir.AluOpType
AXX = mybir.AxisListType.X

B, L, DM, E, DI, N = 4, 8192, 1024, 4096, 2048, 16
TOK = 4096
T1 = 256
NS1 = TOK // T1
Q2 = 1024
NC2 = L // Q2
T3 = 256
NS3 = TOK // T3
DH = DI // 2
AGROWS = 2084
K1C = float(2 ** 20 + 1)
C2C = float(3 * 2 ** 13)

SECTIONS = ("v", "s", "t", "g", "q")
SECNAME = {"v": "vector", "s": "scalar", "t": "tensor", "g": "gpsimd", "q": "sync"}


class Op:
    __slots__ = ("eng", "fn", "waits", "sem_key", "inc", "group")

    def __init__(self, eng, fn, waits, sem_key, inc, group=False):
        self.eng, self.fn, self.waits = eng, fn, waits
        self.sem_key, self.inc, self.group = sem_key, inc, group


class Plan:
    def __init__(self):
        self.ops = []
        self.count = {}
        self.waited = {}
        self.writer = {}
        self.readers = {}
        self.n_colls = 0
        self._pending = {}

    def _need(self, sec, waits, stream, value):
        if stream is None or value <= 0:
            return
        if stream.startswith("D:"):
            value = self.count[stream]
        key = (sec, stream)
        if self.waited.get(key, 0) >= value:
            return
        waits.append((stream, value))
        self.waited[key] = value

    def _dep_waits(self, sec, reads, writes):
        waits = []
        pend = self._pending.pop(sec, None)
        if pend:
            waits.extend(pend)
        for b in reads:
            w = self.writer.get(b)
            if w is not None:
                self._need(sec, waits, w[0], w[1])
        for b in writes:
            w = self.writer.get(b)
            if w is not None:
                self._need(sec, waits, w[0], w[1])
            for pe, pc in self.readers.get(b, {}).items():
                self._need(sec, waits, pe, pc)
        if len(waits) > 1:
            best = {}
            for stream, value in waits:
                if best.get(stream, -1) < value:
                    best[stream] = value
            waits = list(best.items())
        return waits

    def _commit(self, stream, inc, reads, writes):
        self.count[stream] = self.count.get(stream, 0) + inc
        c = self.count[stream]
        for b in writes:
            self.writer[b] = (stream, c)
            self.readers[b] = {}
        for b in reads:
            self.readers.setdefault(b, {})[stream] = c

    def emit(self, eng, fn, reads=(), writes=(), dma=False, fam=None):
        if dma:
            assert fam is not None, "dma emit requires fam"
            stream = "D:" + fam
        else:
            stream = eng
        waits = self._dep_waits(eng, reads, writes)
        self._commit(stream, 16 if dma else 1, reads, writes)
        self.ops.append(Op(eng, fn, waits, stream, 16 if dma else 1))

    def emit_group(self, eng, fn, reads=(), writes=()):
        waits = self._dep_waits(eng, reads, writes)
        self._commit(eng, 1, reads, writes)
        self.ops.append(Op(eng, fn, waits, eng, 1, group=True))

    def coll(self, fn, reads=(), writes=()):
        name = f"coll{self.n_colls}"
        self.n_colls += 1
        waits = self._dep_waits("g", reads, writes)
        self.count[name] = 1
        for b in writes:
            self.writer[b] = (name, 1)
            self.readers[b] = {}
        for b in reads:
            self.readers.setdefault(b, {})[name] = 1
        self.ops.append(Op("g", fn, waits, name, 1))

    def fence(self, secs=SECTIONS):
        snapshot = dict(self.count)
        for sec in secs:
            waits = []
            for stream, value in snapshot.items():
                self._need(sec, waits, stream, value)
            self._pending.setdefault(sec, []).extend(waits)

    def replay(self, nc, block, es):
        sems = {s: es.enter_context(nc.semaphore(f"S_{s}")) for s in self.count}
        by_sec = {s: [] for s in SECTIONS}
        for op in self.ops:
            by_sec[op.eng].append(op)

        def body_for(sec):
            ops = by_sec[sec]

            def body(h):
                for op in ops:
                    for stream, value in op.waits:
                        h.wait_ge(sems[stream], value)
                    if op.group:
                        op.fn(h)[-1].then_inc(sems[op.sem_key], op.inc)
                    else:
                        op.fn(h).then_inc(sems[op.sem_key], op.inc)
            return body

        for sec in SECTIONS:
            if by_sec[sec]:
                getattr(block, SECNAME[sec])(body_for(sec))


def build(debug=False):
    nc = bass.Bass("TRN2", target_bir_lowering=False, debug=False)
    ins = {}

    def I(name, shape, dtype=F32):
        ins[name] = nc.dram_tensor(name, shape, dtype, kind="ExternalInput")

    I("xs", [TOK, DM]); I("xsT", [DM, TOK]); I("xh", [3, DM])
    I("identin", [128, 128])
    I("winT", [DM, E]); I("wxT", [DI, 33]); I("woutT", [DI, DM]); I("wfT", [DM, DM])
    I("cw", [DI, 4]); I("cb", [DI, 1])
    I("dtw_s", [DH, 1]); I("dtb_s", [DH, 1]); I("alog_s", [DH, N])
    I("dparam", [DI, 1]); I("fbias", [DM, 1]); I("sin", [1, 1]); I("sw", [1, 1])
    I("ridx", [4, 1], I32)
    out_d = nc.dram_tensor("out_d", [DM, TOK], F32, kind="ExternalOutput")

    ag_inQ = [nc.dram_tensor(f"ag_inQ{i}", [AGROWS, TOK // 4], BF16)
              for i in range(4)]
    ag_outQ = [nc.dram_tensor(f"ag_outQ{i}", [8 * AGROWS, TOK // 4], BF16,
                              addr_space="Shared") for i in range(4)]
    agy_inA = nc.dram_tensor("agy_inA", [DH // 2, L], BF16)
    agy_inB = nc.dram_tensor("agy_inB", [DH // 2, L], BF16)
    agy_outA = nc.dram_tensor("agy_outA", [4 * DH, L], BF16, addr_space="Shared")
    agy_outB = nc.dram_tensor("agy_outB", [4 * DH, L], BF16, addr_space="Shared")
    zs = nc.dram_tensor("zs", [DI, TOK], BF16)
    qs = nc.dram_tensor("qs", [DM, TOK], BF16)
    ssw_d = nc.dram_tensor("ssw_d", [1, 2], F32)
    xcp = nc.dram_tensor("xcp", [DH, L], BF16)
    projp = nc.dram_tensor("projp", [33, L], BF16)
    yp = nc.dram_tensor("yp", [DI, TOK], BF16)
    dregs = nc.dram_tensor("dregs", [1, 8], I32, kind="ExternalOutput") if debug else None
    if debug:
        d_y2 = nc.dram_tensor("d_y2", [DI, TOK], BF16, kind="ExternalOutput")
        d_x3d = nc.dram_tensor("d_x3d", [DM, TOK], F32, kind="ExternalOutput")
        d_e1 = nc.dram_tensor("d_e1", [DM, TOK], F32, kind="ExternalOutput")
        d_e2 = nc.dram_tensor("d_e2", [DM, TOK], F32, kind="ExternalOutput")
    dbg = {}
    if debug:
        for nm, t in [("d_aginA", ag_inQ[0]), ("d_aginB", ag_inQ[1]),
                      ("d_agyinA", agy_inA), ("d_agyinB", agy_inB), ("d_zs", zs),
                      ("d_qs", qs), ("d_ssw", ssw_d), ("d_xcp", xcp),
                      ("d_projp", projp), ("d_yp", yp)]:
            dbg[nm] = (nc.dram_tensor(nm, list(t.shape), t.dtype,
                                      kind="ExternalOutput"), t)

    pl = Plan()
    sb = {}
    REG = {}

    es = contextlib.ExitStack()
    with es:
        def ST(name, shape, dtype=F32, stack=es):
            sb[name] = stack.enter_context(nc.sbuf_tensor(name, shape, dtype))

        for n_ in ["eps", "mhalf", "one", "invs", "sswb"]:
            ST(n_, [128, 1])
        ST("sinb", [1, 1]); ST("swb", [1, 1])
        ST("s_ridx", [1, 4], I32)
        ST("identf", [128, 128])
        for d in range(8):
            ST(f"dtw{d}", [128, 1]); ST(f"dtb{d}", [128, 1])
            ST(f"negA{d}", [128, N]); ST(f"carry{d}", [128, N])
        psum = [es.enter_context(nc.psum_tensor(f"pb{i}", [128, 512], F32))
                for i in range(8)]

        def EMIT(eng, fn, reads=(), writes=(), dma=False, fam=None):
            pl.emit(eng, fn, reads, writes, dma, fam)

        EMIT("q", lambda h: h.dma_start(out=sb["s_ridx"][:], in_=ins["ridx"].rearrange("a b -> b a")),
             writes=["s_ridx"], dma=True, fam="ridx")
        EMIT("q", lambda h: h.dma_start(out=sb["identf"][:], in_=ins["identin"][:, :]),
             writes=["identf"], dma=True, fam="identf")
        EMIT("v", lambda h: h.memset(sb["eps"][:], 1e-5), writes=["eps"])
        EMIT("v", lambda h: h.memset(sb["mhalf"][:], -0.5), writes=["mhalf"])
        EMIT("v", lambda h: h.memset(sb["one"][:], 1.0), writes=["one"])
        EMIT("q", lambda h: h.dma_start(out=sb["sinb"][:], in_=ins["sin"][:, :]),
             writes=["sinb"], dma=True, fam="scales")
        EMIT("q", lambda h: h.dma_start(out=sb["swb"][:], in_=ins["sw"][:, :]),
             writes=["swb"], dma=True, fam="scales")
        EMIT("v", lambda h: h.tensor_tensor(out=sb["swb"][:], in0=sb["sinb"][:],
             in1=sb["swb"][:], op=ALU.mult), reads=["sinb", "swb"], writes=["swb"])
        EMIT("v", lambda h: h.reciprocal(out=sb["sinb"][:], in_=sb["sinb"][:]),
             reads=["sinb"], writes=["sinb"])
        EMIT("g", lambda h: h.dma_start(out=ssw_d.ap()[0:1, 0:1], in_=sb["sinb"][:]),
             reads=["sinb"], writes=["ssw_d"], dma=True, fam="sswd")
        EMIT("g", lambda h: h.dma_start(out=ssw_d.ap()[0:1, 1:2], in_=sb["swb"][:]),
             reads=["ssw_d", "swb"], writes=["ssw_d"], dma=True, fam="sswd")
        EMIT("q", lambda h: h.dma_start(
            out=sb["invs"][:], in_=bass.AP(ssw_d, 0, [[0, 128], [1, 1]])),
            reads=["ssw_d"], writes=["invs"], dma=True, fam="consts2")
        EMIT("q", lambda h: h.dma_start(
            out=sb["sswb"][:], in_=bass.AP(ssw_d, 1, [[0, 128], [1, 1]])),
            reads=["ssw_d"], writes=["sswb"], dma=True, fam="consts2")

        def init_regs(h):
            insts = []
            for i, nm in enumerate(["r_xc", "r_pj", "r_y", "r_ow"]):
                REG[nm] = h.alloc_register(nm)
                insts.append(h.reg_load(REG[nm], sb["s_ridx"][0:1, i:i + 1]))
            REG["tmp"] = h.alloc_register("tmp")
            if debug:
                for i, nm in enumerate(["r_xc", "r_pj", "r_y", "r_ow"]):
                    insts.append(h.reg_save(dregs.ap()[0:1, i:i + 1], REG[nm]))
            insts.append(h.nop())
            return insts
        pl.emit_group("q", init_regs, reads=["s_ridx"])

        def dyn_copy(dst_ap, src_tensor, base, static_off, pattern, reads, writes,
                     fam="dyncp"):
            def fn(h):
                h.reg_mov(REG["tmp"], int(static_off))
                h.reg_add(REG["tmp"], REG["tmp"], REG[base])
                return h.dma_start(out=dst_ap,
                                   in_=bass.AP(src_tensor, REG["tmp"], pattern))
            EMIT("q", fn, reads=reads, writes=writes, dma=True, fam=fam)

        ST("alod", [128, N])
        ST("ident16", [128, 128], BF16)
        EMIT("v", lambda h: h.tensor_copy(out=sb["ident16"][:], in_=sb["identf"][:]),
             reads=["identf"], writes=["ident16"])

        for d in range(8):
            EMIT("q", (lambda dd=d: (lambda h: h.dma_start(
                out=sb["alod"][:],
                in_=ins["alog_s"][dd * 128:(dd + 1) * 128, :])))(),
                writes=["alod"], dma=True, fam="alod")
            EMIT("s", (lambda dd=d: (lambda h: h.activation(
                out=sb[f"negA{dd}"][:], in_=sb["alod"][:], func=AF.Exp)))(),
                reads=["alod"], writes=[f"negA{d}"])
            EMIT("v", (lambda dd=d: (lambda h: h.tensor_scalar(
                out=sb[f"negA{dd}"][:], in0=sb[f"negA{dd}"][:], scalar1=-1.0,
                scalar2=None, op0=ALU.mult)))(),
                reads=[f"negA{d}"], writes=[f"negA{d}"])
            EMIT("q", (lambda dd=d: (lambda h: h.dma_start(
                out=sb[f"dtw{dd}"][:],
                in_=ins["dtw_s"][dd * 128:(dd + 1) * 128, :])))(),
                writes=[f"dtw{d}"], dma=True, fam="consts3")
            EMIT("q", (lambda dd=d: (lambda h: h.dma_start(
                out=sb[f"dtb{dd}"][:],
                in_=ins["dtb_s"][dd * 128:(dd + 1) * 128, :])))(),
                writes=[f"dtb{d}"], dma=True, fam="consts3")
            EMIT("v", (lambda dd=d: (lambda h: h.memset(sb[f"carry{dd}"][:], 0.0)))(),
                 writes=[f"carry{d}"])

        p1 = contextlib.ExitStack()
        with p1:
            def S1(name, shape, dtype=F32):
                ST(name, shape, dtype, stack=p1)

            for i in range(2):
                for sp in range(2):
                    S1(f"xT{i}_{sp}", [128, DM])
                    S1(f"mean{i}_{sp}", [128, 1]); S1(f"ex2{i}_{sp}", [128, 1])
                    S1(f"var{i}_{sp}", [128, 1]); S1(f"rstd{i}_{sp}", [128, 1])
                    S1(f"nmr{i}_{sp}", [128, 1])
                S1(f"sq{i}", [128, DM])
            for k in range(8):
                S1(f"winT{k}", [128, E], BF16)
                S1(f"hxnd16_{k}", [128, 3], BF16)
            for sp in range(2):
                S1(f"xndC_{sp}", [128, 8 * T1]); S1(f"xndC16_{sp}", [128, 8 * T1], BF16)
                S1(f"q16C_{sp}", [128, 8 * T1], BF16)
            for d in range(16):
                S1(f"wxT{d}", [128, 33], BF16)
                S1(f"xss{d}", [128, T1 + 3], BF16)
            S1("xcw", [128, 16 * T1], BF16)
            S1("z16w", [128, 16 * T1], BF16)
            S1("wld", [128, 1024])
            S1("cacc_0", [128, 8 * T1]); S1("cacc_1", [128, 8 * T1])
            S1("proj_0", [33, T1], BF16); S1("proj_1", [33, T1], BF16)
            S1("qv", [128, 8 * T1]); S1("qc", [128, 8 * T1]); S1("qw", [128, 8 * T1])
            S1("qhi", [128, 8 * T1])
            S1("qm", [128, 8 * T1], U8)
            S1("c2p", [128, 1]); S1("c2n", [128, 1])
            for d in range(16):
                S1(f"cw{d}", [128, 4]); S1(f"cb{d}", [128, 1])

            EMIT("v", lambda h: h.memset(sb["c2p"][:], C2C), writes=["c2p"])
            EMIT("v", lambda h: h.memset(sb["c2n"][:], -C2C), writes=["c2n"])
            for d in range(16):
                EMIT("q", (lambda dd=d: (lambda h: h.dma_start(
                    out=sb[f"cw{dd}"][:], in_=ins["cw"][dd * 128:(dd + 1) * 128, :])))(),
                    writes=[f"cw{d}"], dma=True, fam="consts")
                EMIT("q", (lambda dd=d: (lambda h: h.dma_start(
                    out=sb[f"cb{dd}"][:], in_=ins["cb"][dd * 128:(dd + 1) * 128, :])))(),
                    writes=[f"cb{d}"], dma=True, fam="consts")
            for k in range(8):
                for c4 in range(4):
                    EMIT("q", (lambda kk=k, cc=c4: (lambda h: h.dma_start(
                        out=sb["wld"][:],
                        in_=ins["winT"].ap()[kk * 128:(kk + 1) * 128,
                                             cc * 1024:(cc + 1) * 1024])))(),
                        writes=["wld"], dma=True, fam="wld")
                    EMIT("v", (lambda kk=k, cc=c4: (lambda h: h.tensor_copy(
                        out=sb[f"winT{kk}"][:, cc * 1024:(cc + 1) * 1024],
                        in_=sb["wld"][:])))(),
                        reads=["wld"], writes=[f"winT{k}"])
            for d in range(16):
                EMIT("q", (lambda dd=d: (lambda h: h.dma_start(
                    out=sb["wld"][:, 0:33],
                    in_=ins["wxT"][dd * 128:(dd + 1) * 128, :])))(),
                    writes=["wld"], dma=True, fam="wld")
                EMIT("v", (lambda dd=d: (lambda h: h.tensor_copy(
                    out=sb[f"wxT{dd}"][:], in_=sb["wld"][:, 0:33])))(),
                    reads=["wld"], writes=[f"wxT{d}"])

            def ln_apply(kx, ksq, km, ke, kv, kr, kn):
                kxn = kx
                xT, sq = sb[kx], sb[ksq]
                mean, ex2, var, rstd, nmr = sb[km], sb[ke], sb[kv], sb[kr], sb[kn]
                EMIT("v", lambda h: h.tensor_tensor(out=sq[:], in0=xT[:], in1=xT[:],
                     op=ALU.mult), reads=[kx], writes=[ksq])
                EMIT("v", lambda h: h.reduce_sum(out=mean[:], in_=xT[:], axis=AXX),
                     reads=[kx], writes=[km])
                EMIT("v", lambda h: h.reduce_sum(out=ex2[:], in_=sq[:], axis=AXX),
                     reads=[ksq], writes=[ke])
                EMIT("v", lambda h: h.tensor_scalar(out=mean[:], in0=mean[:],
                     scalar1=1.0 / DM, scalar2=None, op0=ALU.mult),
                     reads=[km], writes=[km])
                EMIT("v", lambda h: h.tensor_scalar(out=ex2[:], in0=ex2[:],
                     scalar1=1.0 / DM, scalar2=None, op0=ALU.mult),
                     reads=[ke], writes=[ke])
                EMIT("v", lambda h: h.scalar_tensor_tensor(out=var[:], in0=mean[:],
                     scalar=mean[:], in1=ex2[:], op0=ALU.mult, op1=ALU.subtract),
                     reads=[km, ke], writes=[kv])
                EMIT("v", lambda h: h.tensor_scalar(out=var[:], in0=var[:],
                     scalar1=-1.0, scalar2=None, op0=ALU.mult),
                     reads=[kv], writes=[kv])
                EMIT("s", lambda h: h.activation(out=var[:], in_=var[:], func=AF.Ln,
                     bias=sb["eps"][:]), reads=[kv, "eps"], writes=[kv])
                EMIT("s", lambda h: h.activation(out=rstd[:], in_=var[:], func=AF.Exp,
                     scale=sb["mhalf"][:]), reads=[kv, "mhalf"], writes=[kr])
                EMIT("v", lambda h: h.scalar_tensor_tensor(out=nmr[:], in0=mean[:],
                     scalar=-1.0, in1=rstd[:], op0=ALU.mult, op1=ALU.mult),
                     reads=[km, kr], writes=[kn])
                EMIT("s", lambda h: h.activation(out=sb[kxn][:], in_=xT[:],
                     func=AF.Identity, scale=rstd[:], bias=nmr[:]),
                     reads=[kx, kr, kn], writes=[kxn])

            EMIT("v", lambda h: h.memset(sb["xT0_0"][:], 0.0), writes=["xT0_0"])
            EMIT("q", lambda h: h.dma_start(out=sb["xT0_0"][0:3, :], in_=ins["xh"][:, :]),
                 reads=["xT0_0"], writes=["xT0_0"], dma=True, fam="xh")
            ln_apply("xT0_0", "sq0", "mean0_0", "ex20_0", "var0_0", "rstd0_0", "nmr0_0")
            for k in range(8):
                pk = k % 2
                EMIT("t", (lambda kk=k, p=pk: (lambda h: h.transpose(
                    out=psum[p][:, 0:128],
                    in_=sb["xT0_0"][:, kk * 128:(kk + 1) * 128],
                    identity=sb["identf"][:])))(),
                    reads=["xT0_0", "identf"], writes=[f"pb{pk}"])
                EMIT("s", (lambda kk=k, p=pk: (lambda h: h.activation(
                    out=sb[f"hxnd16_{kk}"][:], in_=psum[p][:, 0:3],
                    func=AF.Identity)))(),
                    reads=[f"pb{pk}"], writes=[f"hxnd16_{k}"])
            for e in range(16):
                pe_ = 2 + e % 2

                def mmgroup(h, ee=e, p=pe_):
                    return [h.matmul(out=psum[p][:, 0:3],
                                     lhsT=sb[f"winT{k}"][:, ee * 128:(ee + 1) * 128],
                                     rhs=sb[f"hxnd16_{k}"][:],
                                     start=(k == 0), stop=(k == 7))
                            for k in range(8)]
                pl.emit_group("t", mmgroup,
                              reads=[f"hxnd16_{k}" for k in range(8)]
                              + [f"winT{k}" for k in range(8)],
                              writes=[f"pb{pe_}"])
                EMIT("s", (lambda ee=e, p=pe_: (lambda h: h.activation(
                    out=sb[f"xss{ee}"][:, 0:3], in_=psum[p][:, 0:3],
                    func=AF.Identity)))(),
                    reads=[f"pb{pe_}"], writes=[f"xss{e}"])

            def emit_quant(xndCv, q16Cv, c0v, s1v):
                EMIT("v", (lambda bx=xndCv: (lambda h: h.tensor_scalar(
                    out=sb["qv"][:], in0=sb[bx][:], scalar1=sb["invs"][:],
                    scalar2=-448.0, op0=ALU.mult, op1=ALU.max)))(),
                    reads=[f"{xndCv}:{kc2}" for kc2 in range(8)] + ["invs"],
                    writes=["qv"])
                EMIT("v", lambda h: h.tensor_scalar(out=sb["qv"][:], in0=sb["qv"][:],
                     scalar1=448.0, scalar2=None, op0=ALU.min),
                     reads=["qv"], writes=["qv"])
                EMIT("v", lambda h: h.tensor_scalar(out=sb["qc"][:], in0=sb["qv"][:],
                     scalar1=K1C, scalar2=None, op0=ALU.mult),
                     reads=["qv"], writes=["qc"])
                EMIT("v", lambda h: h.scalar_tensor_tensor(out=sb["qw"][:],
                     in0=sb["qv"][:], scalar=K1C, in1=sb["qv"][:],
                     op0=ALU.mult, op1=ALU.subtract),
                     reads=["qv"], writes=["qw"])
                EMIT("v", lambda h: h.tensor_tensor(out=sb["qhi"][:], in0=sb["qc"][:],
                     in1=sb["qw"][:], op=ALU.subtract),
                     reads=["qc", "qw"], writes=["qhi"])
                EMIT("v", lambda h: h.tensor_scalar(out=sb["qc"][:], in0=sb["qv"][:],
                     scalar1=C2C, scalar2=None, op0=ALU.add),
                     reads=["qv"], writes=["qc"])
                EMIT("v", lambda h: h.tensor_scalar(out=sb["qw"][:], in0=sb["qc"][:],
                     scalar1=C2C, scalar2=None, op0=ALU.subtract),
                     reads=["qc"], writes=["qw"])
                EMIT("v", lambda h: h.tensor_tensor(out=sb["qc"][:], in0=sb["qv"][:],
                     in1=sb["qv"][:], op=ALU.mult),
                     reads=["qv"], writes=["qc"])
                EMIT("v", lambda h: h.tensor_scalar(out=sb["qm"][:], in0=sb["qc"][:],
                     scalar1=float(2.0 ** -12), scalar2=None, op0=ALU.is_lt),
                     reads=["qc"], writes=["qm"])
                EMIT("v", lambda h: h.select(out=sb["qc"][:], mask=sb["qm"][:],
                     on_true=sb["qw"][:], on_false=sb["qhi"][:]),
                     reads=["qm", "qw", "qhi"], writes=["qc"])
                EMIT("v", (lambda q=q16Cv: (lambda h: h.tensor_copy(
                    out=sb[q][:], in_=sb["qc"][:])))(),
                    reads=["qc"], writes=[q16Cv])
                EMIT("g", (lambda q=q16Cv, cc=c0v: (lambda h: h.dma_start(
                    out=bass.AP(qs, cc, [[TOK, 128], [128 * TOK, 8], [1, T1]]),
                    in_=bass.AP(sb[q].tensor if hasattr(sb[q], "tensor") else None, 0,
                                [[8 * T1, 128], [T1, 8], [1, T1]])
                    if False else sb[q][:].rearrange("p (k j) -> p k j", k=8))))(),
                    reads=[q16Cv], writes=[f"qs_{s1v}"], dma=True, fam="qs")

            pending_q = [None]

            def issue_agQ(qi):
                keys = ([f"agin_{s}" for s in range(4 * qi, 4 * qi + 4)]
                        + [f"agin_{s}_p" for s in range(4 * qi, 4 * qi + 4)])
                pl.coll((lambda q_=qi: (lambda h: h.collective_compute(
                    "AllGather", ALU.bypass, replica_groups=[list(range(8))],
                    ins=[ag_inQ[q_].ap().opt()], outs=[ag_outQ[q_].ap().opt()])))(),
                    reads=keys, writes=[f"ag_outQ{qi}"])

            for s1 in range(NS1):
                if s1 in (4, 8, 12):
                    issue_agQ(s1 // 4 - 1)
                c0 = s1 * T1
                sp = s1 % 2
                xTn = [f"xT{i}_{sp}" for i in range(2)]
                xndC = f"xndC_{sp}"
                xndC16 = f"xndC16_{sp}"
                q16C = f"q16C_{sp}"
                xcwn = "xcw"
                caccn = ("cacc_0", "cacc_1")
                projn = f"proj_{sp}"
                for i in range(2):
                    EMIT("q", (lambda ii=i, cc=c0, bx=xTn[i]: (lambda h: h.dma_start(
                        out=sb[bx][:],
                        in_=ins["xs"][cc + ii * 128: cc + (ii + 1) * 128, :])))(),
                        writes=[xTn[i]], dma=True, fam=f"xT{sp}")
                for i in range(2):
                    ln_apply(xTn[i], f"sq{i}", f"mean{i}_{sp}", f"ex2{i}_{sp}",
                             f"var{i}_{sp}", f"rstd{i}_{sp}", f"nmr{i}_{sp}")
                if pending_q[0] is not None:
                    emit_quant(*pending_q[0])
                    pending_q[0] = None
                for kc in range(8):
                    p = kc % 2
                    for i in range(2):
                        EMIT("t", (lambda kk=kc, ii=i, pp=p, bx=xTn[i]: (lambda h: h.transpose(
                            out=psum[pp][:, ii * 128:(ii + 1) * 128],
                            in_=sb[bx][:, kk * 128:(kk + 1) * 128],
                            identity=sb["identf"][:])))(),
                            reads=[xTn[i], "identf"], writes=[f"pb{p}"])
                    EMIT("s", (lambda kk=kc, pp=p, bx=xndC: (lambda h: h.activation(
                        out=sb[bx][:, kk * T1:(kk + 1) * T1], in_=psum[pp][:, 0:T1],
                        func=AF.Identity)))(),
                        reads=[f"pb{p}"], writes=[f"{xndC}:{kc}"])
                EMIT("v", (lambda b1=xndC, b2=xndC16: (lambda h: h.tensor_copy(
                    out=sb[b2][:], in_=sb[b1][:])))(),
                    reads=[f"{xndC}:{kc2}" for kc2 in range(8)], writes=[xndC16])
                def conv_d(d):
                    ca = caccn[d % 2]
                    sl = (d // 2) * T1
                    EMIT("v", (lambda dd=d, c_=ca, s_=sl: (lambda h: h.tensor_scalar(
                        out=sb[c_][:, s_:s_ + T1], in0=sb[f"xss{dd}"][:, 0:T1],
                        scalar1=sb[f"cw{dd}"][:, 0:1], scalar2=None, op0=ALU.mult)))(),
                        reads=[f"xss{d}", f"cw{d}"], writes=[f"{ca}:{d}"])
                    for kk in range(1, 4):
                        EMIT("v", (lambda dd=d, k2=kk, c_=ca, s_=sl: (lambda h: h.scalar_tensor_tensor(
                            out=sb[c_][:, s_:s_ + T1], in0=sb[f"xss{dd}"][:, k2:T1 + k2],
                            scalar=sb[f"cw{dd}"][:, k2:k2 + 1], in1=sb[c_][:, s_:s_ + T1],
                            op0=ALU.mult, op1=ALU.add)))(),
                            reads=[f"xss{d}", f"cw{d}", f"{ca}:{d}"], writes=[f"{ca}:{d}"])
                    EMIT("v", (lambda dd=d: (lambda h: h.tensor_copy(
                        out=sb[f"xss{dd}"][:, 0:3],
                        in_=sb[f"xss{dd}"][:, T1:T1 + 3])))(),
                        reads=[f"xss{d}"], writes=[f"xss{d}"])

                def conv_tail(d):
                    ca = caccn[d % 2]
                    sl = (d // 2) * T1
                    EMIT("s", (lambda dd=d, c_=ca, s_=sl, bx=xcwn: (lambda h: h.activation(
                        out=sb[bx][:, dd * T1:(dd + 1) * T1],
                        in_=sb[c_][:, s_:s_ + T1], func=AF.Silu,
                        bias=sb[f"cb{dd}"][:])))(),
                        reads=[f"{ca}:{d}", f"cb{d}"], writes=[f"{xcwn}:{d}"])

                for e in range(32):
                    p = (2, 3, 5, 6)[e % 4]

                    def mmg(h, ee=e, pp=p, bx=xndC16):
                        return [h.matmul(out=psum[pp][:, 0:T1],
                                         lhsT=sb[f"winT{k}"][:, ee * 128:(ee + 1) * 128],
                                         rhs=sb[bx][:, k * T1:(k + 1) * T1],
                                         start=(k == 0), stop=(k == 7))
                                for k in range(8)]
                    pl.emit_group("t", mmg,
                                  reads=[xndC16]
                                  + [f"winT{k}" for k in range(8)],
                                  writes=[f"pb{p}"])
                    if e < 16:
                        EMIT("s", (lambda ee=e, pp=p: (lambda h: h.activation(
                            out=sb[f"xss{ee}"][:, 3:T1 + 3], in_=psum[pp][:, 0:T1],
                            func=AF.Identity)))(),
                            reads=[f"pb{p}"], writes=[f"xss{e}"])
                        conv_d(e)
                    else:
                        EMIT("s", (lambda ee=e, pp=p: (lambda h: h.activation(
                            out=sb["z16w"][:, (ee - 16) * T1:(ee - 15) * T1],
                            in_=psum[pp][:, 0:T1],
                            func=AF.Identity)))(),
                            reads=[f"pb{p}"], writes=[f"z16w:{e - 16}"])
                EMIT("g", (lambda cc=c0: (lambda h: h.dma_start(
                    out=bass.AP(zs, cc, [[TOK, 128], [128 * TOK, 16], [1, T1]]),
                    in_=sb["z16w"][:].rearrange("p (k j) -> p k j", k=16))))(),
                    reads=[f"z16w:{j}" for j in range(16)],
                    writes=[f"zs_{s1}"], dma=True, fam="zs")

                for d in range(16):
                    conv_tail(d)
                agt = ag_inQ[s1 // 4]
                cl = c0 % (TOK // 4)
                EMIT("g", (lambda cc=cl, at=agt, bx=xcwn: (lambda h: h.dma_start(
                    out=bass.AP(at, cc,
                                [[TOK // 4, 128], [128 * (TOK // 4), 16], [1, T1]]),
                    in_=sb[bx][:].rearrange("p (k j) -> p k j", k=16))))(),
                    reads=[f"{xcwn}:{d}" for d in range(16)],
                    writes=[f"agin_{s1}"], dma=True, fam="agin")
                def wxg(h, bx=xcwn):
                    return [h.matmul(out=psum[4][0:33, 0:T1], lhsT=sb[f"wxT{d}"][:],
                                     rhs=sb[bx][:, d * T1:(d + 1) * T1],
                                     start=(d == 0), stop=(d == 15))
                            for d in range(16)]
                pl.emit_group("t", wxg,
                              reads=[f"{xcwn}:{d}" for d in range(16)]
                              + [f"wxT{d}" for d in range(16)],
                              writes=["pb4"])
                EMIT("v", (lambda pj=projn: (lambda h: h.tensor_copy(out=sb[pj][:],
                     in_=psum[4][0:33, 0:T1])))(),
                     reads=["pb4"], writes=[projn])
                agt2 = ag_inQ[s1 // 4]
                cl2 = c0 % (TOK // 4)
                EMIT("g", (lambda cc=cl2, pj=projn, at=agt2: (lambda h: h.dma_start(
                    out=at.ap()[DI:DI + 33, cc:cc + T1], in_=sb[pj][:])))(),
                    reads=[projn], writes=[f"agin_{s1}_p"], dma=True, fam="agin")

                pending_q[0] = (xndC, q16C, c0, s1)

            if pending_q[0] is not None:
                emit_quant(*pending_q[0])
                pending_q[0] = None

        pl.fence()
        issue_agQ(3)
        HT = TOK // 4
        for cnk in range(8):
            half, qi = cnk // 4, cnk % 4
            dyn_copy(xcp.ap()[:, cnk * HT:(cnk + 1) * HT], ag_outQ[qi], "r_xc",
                     half * AGROWS * HT, [[HT, DH], [1, HT]],
                     reads=[f"ag_outQ{qi}"], writes=[f"xcp_q8_{cnk}"],
                     fam=f"xcq{cnk}")
            dyn_copy(projp.ap()[:, cnk * HT:(cnk + 1) * HT], ag_outQ[qi], "r_pj",
                     half * AGROWS * HT, [[HT, 33], [1, HT]],
                     reads=[f"ag_outQ{qi}"], writes=[f"projp_q8_{cnk}"],
                     fam=f"pjq{cnk}")
        p23 = contextlib.ExitStack()
        with p23:
            def S23(name, shape, dtype=F32):
                ST(name, shape, dtype, stack=p23)

            S23("wld3a", [128, DM]); S23("wld3b", [128, DM])
            for d in range(16):
                S23(f"woT{d}", [128, DM], BF16)
                S23(f"dp{d}", [128, 1])
            for k in range(8):
                S23(f"wfT{k}", [128, DM], BF16)
            for m in range(8):
                S23(f"fb{m}", [128, 1])

            p2 = contextlib.ExitStack()
            with p2:
                def S2(name, shape, dtype=F32):
                    ST(name, shape, dtype, stack=p2)

                for d in range(8):
                    S2(f"dt16_{d}", [128, Q2], BF16)
                    S2(f"u16_{d}", [128, Q2], BF16)
                    S2(f"xc2_{d}", [128, Q2], BF16)
                S2("dtbc", [128, Q2], BF16)
                for i in range(2):
                    S2(f"zp{i}", [128, Q2]); S2(f"rel{i}", [128, Q2])
                    S2(f"nab{i}", [128, Q2]); S2(f"lnb{i}", [128, Q2])
                for p in range(2):
                    S2(f"Bb{p}", [128, Q2], BF16); S2(f"Cb{p}", [128, Q2], BF16)
                    S2(f"dA{p}", [128, Q2]); S2(f"dBx{p}", [128, Q2], BF16)
                    S2(f"hh{p}", [128, Q2], BF16); S2(f"gy{p}", [128, Q2], BF16)
                    S2(f"y16w_{p}", [128, 1024], BF16)
                for d in range(16):
                    EMIT("q", (lambda dd=d: (lambda h: h.dma_start(
                        out=sb[f"dp{dd}"][:],
                        in_=ins["dparam"][dd * 128:(dd + 1) * 128, :])))(),
                        writes=[f"dp{d}"], dma=True, fam="consts4")
                for m in range(8):
                    EMIT("q", (lambda mm=m: (lambda h: h.dma_start(
                        out=sb[f"fb{mm}"][:],
                        in_=ins["fbias"][mm * 128:(mm + 1) * 128, :])))(),
                        writes=[f"fb{m}"], dma=True, fam="consts4")

                for dgrp in range(2):
                    d4 = dgrp * 4
                    for c in range(NC2):
                        EMIT("q", (lambda cc=c: (lambda h: h.dma_start(out=sb["dtbc"][:],
                             in_=bass.AP(projp, cc * Q2, [[0, 128], [1, Q2]]))))(),
                             reads=[f"projp_q8_{c}"], writes=["dtbc"], dma=True, fam="dtbc")
                        for d in range(d4, d4 + 4):
                            EMIT("q", (lambda dd=d, cc=c: (lambda h: h.dma_start(
                                out=sb[f"xc2_{dd}"][:],
                                in_=xcp.ap()[dd * 128:(dd + 1) * 128,
                                             cc * Q2:(cc + 1) * Q2])))(),
                                reads=[f"xcp_q8_{c}"], writes=[f"xc2_{d}"], dma=True, fam="xc2")
                        for j in range(3):
                            w = 3 * (dgrp * NC2 + c) + j
                            if w >= 24:
                                continue
                            stg = "wld3a" if w % 2 == 0 else "wld3b"
                            famw = "wl3a" if w % 2 == 0 else "wl3b"
                            if w < 16:
                                srct, row, dstb = ins["woutT"], w, f"woT{w}"
                            else:
                                srct, row, dstb = ins["wfT"], w - 16, f"wfT{w - 16}"
                            EMIT("q", (lambda s_=srct, r_=row, g_=stg: (lambda h: h.dma_start(
                                out=sb[g_][:], in_=s_[r_ * 128:(r_ + 1) * 128, :])))(),
                                writes=[stg], dma=True, fam=famw)
                            EMIT("g", (lambda g_=stg, b_=dstb: (lambda h: h.tensor_copy(
                                out=sb[b_][:], in_=sb[g_][:])))(),
                                reads=[stg], writes=[dstb])
                        for d in range(d4, d4 + 4):
                            i2 = d % 2
                            zp, rel = f"zp{i2}", f"rel{i2}"
                            nab, lnb = f"nab{i2}", f"lnb{i2}"
                            EMIT("s", (lambda dd=d, z=zp: (lambda h: h.activation(
                                out=sb[z][:], in_=sb["dtbc"][:], func=AF.Identity,
                                scale=sb[f"dtw{dd}"][:], bias=sb[f"dtb{dd}"][:])))(),
                                reads=["dtbc", f"dtw{d}", f"dtb{d}"], writes=[zp])
                            EMIT("v", (lambda z=zp, r=rel: (lambda h: h.tensor_scalar(
                                out=sb[r][:], in0=sb[z][:],
                                scalar1=0.0, scalar2=None, op0=ALU.max)))(),
                                reads=[zp], writes=[rel])
                            EMIT("v", (lambda z=zp, nb=nab: (lambda h: h.scalar_tensor_tensor(
                                out=sb[nb][:],
                                in0=sb[z][:], scalar=-1.0, in1=sb[z][:],
                                op0=ALU.mult, op1=ALU.min)))(),
                                reads=[zp], writes=[nab])
                            EMIT("s", (lambda nb=nab: (lambda h: h.activation(
                                out=sb[nb][:], in_=sb[nb][:],
                                func=AF.Exp)))(), reads=[nab], writes=[nab])
                            EMIT("s", (lambda nb=nab, lb=lnb: (lambda h: h.activation(
                                out=sb[lb][:], in_=sb[nb][:],
                                func=AF.Ln, bias=sb["one"][:])))(),
                                reads=[nab, "one"], writes=[lnb])
                            EMIT("v", (lambda dd=d, r=rel, lb=lnb: (lambda h: h.tensor_tensor(
                                out=sb[f"dt16_{dd}"][:], in0=sb[r][:], in1=sb[lb][:],
                                op=ALU.add)))(),
                                reads=[rel, lnb], writes=[f"dt16_{d}"])
                            EMIT("v", (lambda dd=d: (lambda h: h.tensor_tensor(
                                out=sb[f"u16_{dd}"][:], in0=sb[f"dt16_{dd}"][:],
                                in1=sb[f"xc2_{dd}"][:], op=ALU.mult)))(),
                                reads=[f"dt16_{d}", f"xc2_{d}"], writes=[f"u16_{d}"])
                        for n in range(N):
                            p = n % 2
                            EMIT("q", (lambda nn=n, pp=p, cc=c: (lambda h: h.dma_start(
                                out=sb[f"Bb{pp}"][:],
                                in_=bass.AP(projp, (1 + nn) * L + cc * Q2,
                                            [[0, 128], [1, Q2]]))))(),
                                reads=[f"projp_q8_{c}"], writes=[f"Bb{p}"], dma=True, fam=f"bb{p}")
                            EMIT("q", (lambda nn=n, pp=p, cc=c: (lambda h: h.dma_start(
                                out=sb[f"Cb{pp}"][:],
                                in_=bass.AP(projp, (17 + nn) * L + cc * Q2,
                                            [[0, 128], [1, Q2]]))))(),
                                reads=[f"projp_q8_{c}"], writes=[f"Cb{p}"], dma=True, fam=f"cb2{p}")
                            for d in range(d4, d4 + 4):
                                q = (n * 4 + d) % 2
                                EMIT("s", (lambda dd=d, nn=n, qq=q: (lambda h: h.activation(
                                    out=sb[f"dA{qq}"][:], in_=sb[f"dt16_{dd}"][:],
                                    func=AF.Exp,
                                    scale=sb[f"negA{dd}"][:, nn:nn + 1])))(),
                                    reads=[f"dt16_{d}", f"negA{d}"], writes=[f"dA{q}"])
                                EMIT("v", (lambda dd=d, pp=p, qq=q: (lambda h: h.tensor_tensor(
                                    out=sb[f"dBx{qq}"][:], in0=sb[f"u16_{dd}"][:],
                                    in1=sb[f"Bb{pp}"][:], op=ALU.mult)))(),
                                    reads=[f"u16_{d}", f"Bb{p}"], writes=[f"dBx{q}"])
                                EMIT("v", (lambda dd=d, nn=n, qq=q: (lambda h: h.tensor_tensor_scan(
                                    out=sb[f"hh{qq}"][:], data0=sb[f"dA{qq}"][:],
                                    data1=sb[f"dBx{qq}"][:],
                                    initial=sb[f"carry{dd}"][:, nn:nn + 1],
                                    op0=ALU.mult, op1=ALU.add)))(),
                                    reads=[f"dA{q}", f"dBx{q}", f"carry{d}"],
                                    writes=[f"hh{q}"])
                                EMIT("v", (lambda dd=d, nn=n, qq=q: (lambda h: h.tensor_copy(
                                    out=sb[f"carry{dd}"][:, nn:nn + 1],
                                    in_=sb[f"hh{qq}"][:, Q2 - 1:Q2])))(),
                                    reads=[f"hh{q}"], writes=[f"carry{d}"])
                                EMIT("v", (lambda dd=d, pp=p, qq=q: (lambda h: h.tensor_tensor(
                                    out=sb[f"gy{qq}"][:], in0=sb[f"hh{qq}"][:],
                                    in1=sb[f"Cb{pp}"][:], op=ALU.mult)))(),
                                    reads=[f"hh{q}", f"Cb{p}"], writes=[f"gy{q}"])
                                for hf in range(2):
                                    bk = 2 * (d - d4) + hf
                                    EMIT("t", (lambda dd=d, nn=n, qq=q, hh2=hf, bb=bk:
                                         (lambda h: h.matmul(
                                            out=psum[bb][:],
                                            lhsT=sb["ident16"][:],
                                            rhs=sb[f"gy{qq}"][:, hh2 * 512:(hh2 + 1) * 512],
                                            start=(nn == 0), stop=(nn == N - 1),
                                            skip_group_check=True)))(),
                                        reads=[f"gy{q}", "ident16"], writes=[f"pb{bk}"])
                        for d in range(d4, d4 + 4):
                            yb = f"y16w_{d % 2}"
                            for hf in range(2):
                                bk = 2 * (d - d4) + hf
                                EMIT("s", (lambda bb=bk, y_=yb, hh2=hf: (lambda h: h.activation(
                                    out=sb[y_][:, hh2 * 512:(hh2 + 1) * 512],
                                    in_=psum[bb][:],
                                    func=AF.Identity)))(),
                                    reads=[f"pb{bk}"], writes=[f"{yb}:{hf}"])
                            agt3 = agy_inA if d < 4 else agy_inB
                            EMIT("g", (lambda dd=d, y_=yb, cc=c, at=agt3: (lambda h: h.dma_start(
                                out=at.ap()[(dd % 4) * 128:(dd % 4 + 1) * 128,
                                            cc * Q2:(cc + 1) * Q2],
                                in_=sb[y_][:])))(),
                                reads=[f"{yb}:0", f"{yb}:1"], writes=[f"agy_{c}_{d}"],
                                dma=True, fam=f"agy{d % 2}")
                    if dgrp == 0:
                        agyA_keys = [f"agy_{cc}_{dd}" for cc in range(NC2)
                                     for dd in range(4)]
                        pl.coll(lambda h: h.collective_compute(
                            "AllGather", ALU.bypass, replica_groups=[list(range(8))],
                            ins=[agy_inA.ap().opt()], outs=[agy_outA.ap().opt()]),
                            reads=agyA_keys, writes=["agy_outA"])

            pl.fence()
            agyB_keys = [f"agy_{c}_{d}" for c in range(NC2) for d in range(4, 8)]
            pl.coll(lambda h: h.collective_compute(
                "AllGather", ALU.bypass, replica_groups=[list(range(8))],
                ins=[agy_inB.ap().opt()], outs=[agy_outB.ap().opt()]),
                reads=agyB_keys, writes=["agy_outB"])
            for half in range(2):
                dyn_copy(yp.ap()[half * DH:half * DH + 512, :], agy_outA, "r_y",
                         half * 512 * L, [[L, 512], [1, TOK]],
                         reads=["agy_outA"], writes=[f"yp_q{2 * half}"],
                         fam=f"ypq{2 * half}")
            for half in range(2):
                dyn_copy(yp.ap()[half * DH + 512:(half + 1) * DH, :], agy_outB, "r_y",
                         half * 512 * L, [[L, 512], [1, TOK]],
                         reads=["agy_outB"], writes=[f"yp_q{2 * half + 1}"],
                         fam=f"ypq{2 * half + 1}")
            p3 = contextlib.ExitStack()
            with p3:
                def S3(name, shape, dtype=F32):
                    ST(name, shape, dtype, stack=p3)

                for sp in range(2):
                    S3(f"y3w_{sp}", [128, 16 * T3], BF16)
                    S3(f"xc3w_{sp}", [128, 16 * T3], BF16)
                    S3(f"z3w_{sp}", [128, 16 * T3], BF16)
                    S3(f"q3w_{sp}", [128, 8 * T3], BF16)
                    S3(f"x3w_{sp}", [128, 8 * T3])
                for p in range(8):
                    S3(f"e1_{p}", [128, T3])
                for p in range(2):
                    S3(f"e2_{p}", [128, T3])
                    S3(f"odw_{p}", [128, 8 * T3])

                for s3 in range(NS3):
                    c0 = s3 * T3
                    sp = s3 % 2
                    y3w, xc3w, z3w = f"y3w_{sp}", f"xc3w_{sp}", f"z3w_{sp}"
                    q3w, x3w = f"q3w_{sp}", f"x3w_{sp}"
                    EMIT("q", (lambda cc=c0, b=q3w: (lambda h: h.dma_start(
                        out=sb[b][:].rearrange("p (k j) -> p k j", k=8),
                        in_=bass.AP(qs, cc, [[TOK, 128], [128 * TOK, 8], [1, T3]]))))(),
                        reads=["qsall"], writes=[f"{q3w}:{k}" for k in range(8)],
                        dma=True, fam=f"q3{sp}")
                    EMIT("q", (lambda cc=c0, b=x3w: (lambda h: h.dma_start(
                        out=sb[b][:].rearrange("p (k j) -> p k j", k=8),
                        in_=bass.AP(ins["xsT"], cc,
                                    [[TOK, 128], [128 * TOK, 8], [1, T3]]))))(),
                        writes=[f"{x3w}:{m}" for m in range(8)],
                        dma=True, fam=f"x3{sp}")
                    EMIT("q", (lambda cc=c0, b=y3w: (lambda h: h.dma_start(
                        out=sb[b][:].rearrange("p (k j) -> p k j", k=16),
                        in_=bass.AP(yp, cc, [[TOK, 128], [128 * TOK, 16], [1, T3]]))))(),
                        reads=[f"yp_q{qq}" for qq in range(4)],
                        writes=[f"{y3w}:{d}" for d in range(16)],
                        dma=True, fam=f"y3{sp}")
                    ag3 = ag_inQ[s3 // 4]
                    cl3 = c0 % (TOK // 4)
                    EMIT("q", (lambda cc=cl3, b=xc3w, at=ag3: (lambda h: h.dma_start(
                        out=sb[b][:].rearrange("p (k j) -> p k j", k=16),
                        in_=bass.AP(at, cc,
                                    [[TOK // 4, 128], [128 * (TOK // 4), 16], [1, T3]]))))(),
                        reads=["againall"], writes=[f"{xc3w}:{d}" for d in range(16)],
                        dma=True, fam=f"xc3{sp}")
                    EMIT("q", (lambda cc=c0, b=z3w: (lambda h: h.dma_start(
                        out=sb[b][:].rearrange("p (k j) -> p k j", k=16),
                        in_=bass.AP(zs, cc, [[TOK, 128], [128 * TOK, 16], [1, T3]]))))(),
                        reads=["zsall"], writes=[f"{z3w}:{d}" for d in range(16)],
                        dma=True, fam=f"z3{sp}")
                    for m in range(8):
                        pf = (4, 7)[m % 2]
                        pe1 = m

                        def ffg(h, mm=m, pp=pf, b=q3w):
                            return [h.matmul(out=psum[pp][:, 0:T3],
                                             lhsT=sb[f"wfT{k}"][:, mm * 128:(mm + 1) * 128],
                                             rhs=sb[b][:, k * T3:(k + 1) * T3],
                                             start=(k == 0), stop=(k == 7))
                                    for k in range(8)]
                        pl.emit_group("t", ffg,
                                      reads=[f"{q3w}:{k}" for k in range(8)]
                                      + [f"wfT{k}" for k in range(8)],
                                      writes=[f"pb{pf}"])
                        EMIT("s", (lambda mm=m, pf_=pf, p_=pe1: (lambda h: h.activation(
                            out=sb[f"e1_{p_}"][:], in_=psum[pf_][:, 0:T3],
                            func=AF.Identity, scale=sb["sswb"][:],
                            bias=sb[f"fb{mm}"][:])))(),
                            reads=[f"pb{pf}", "sswb", f"fb{m}"], writes=[f"e1_{pe1}"])
                    for d in range(16):
                        sl = slice(d * T3, (d + 1) * T3)
                        EMIT("s", (lambda b=z3w, s_=sl: (lambda h: h.activation(
                            out=sb[b][:, s_], in_=sb[b][:, s_], func=AF.Silu)))(),
                            reads=[f"{z3w}:{d}"], writes=[f"{z3w}:{d}"])
                        EMIT("v", (lambda dd=d, by=y3w, bx=xc3w, s_=sl: (lambda h: h.scalar_tensor_tensor(
                            out=sb[by][:, s_], in0=sb[bx][:, s_],
                            scalar=sb[f"dp{dd}"][:], in1=sb[by][:, s_],
                            op0=ALU.mult, op1=ALU.add)))(),
                            reads=[f"{xc3w}:{d}", f"dp{d}", f"{y3w}:{d}"],
                            writes=[f"{y3w}:{d}"])
                        EMIT("v", (lambda by=y3w, bz=z3w, s_=sl: (lambda h: h.tensor_tensor(
                            out=sb[by][:, s_], in0=sb[by][:, s_],
                            in1=sb[bz][:, s_], op=ALU.mult)))(),
                            reads=[f"{y3w}:{d}", f"{z3w}:{d}"], writes=[f"{y3w}:{d}"])
                    for m in range(8):
                        pw = (2, 3, 5, 6)[m % 4]
                        pe1 = m
                        pp = m % 2

                        def wog(h, mm=m, pp=pw, b=y3w):
                            return [h.matmul(out=psum[pp][:, 0:T3],
                                             lhsT=sb[f"woT{d}"][:, mm * 128:(mm + 1) * 128],
                                             rhs=sb[b][:, d * T3:(d + 1) * T3],
                                             start=(d == 0), stop=(d == 15))
                                    for d in range(16)]
                        pl.emit_group("t", wog,
                                      reads=[f"{y3w}:{d}" for d in range(16)]
                                      + [f"woT{d}" for d in range(16)],
                                      writes=[f"pb{pw}"])
                        EMIT("v", (lambda mm=m, pw_=pw, p_=pp, pe=pe1: (lambda h: h.tensor_tensor(
                            out=sb[f"e2_{p_}"][:], in0=psum[pw_][:, 0:T3],
                            in1=sb[f"e1_{pe}"][:], op=ALU.add)))(),
                            reads=[f"pb{pw}", f"e1_{pe1}"], writes=[f"e2_{pp}"])
                        EMIT("v", (lambda mm=m, p_=sp, bx=x3w, pe=pp: (lambda h: h.tensor_tensor(
                            out=sb[f"odw_{p_}"][:, mm * T3:(mm + 1) * T3],
                            in0=sb[f"e2_{pe}"][:],
                            in1=sb[bx][:, mm * T3:(mm + 1) * T3], op=ALU.add)))(),
                            reads=[f"e2_{pp}", f"{x3w}:{m}"], writes=[f"odw_{sp}:{m}"])
                    EMIT("g", (lambda p_=sp, cc=c0: (lambda h: h.dma_start(
                        out=bass.AP(out_d, cc, [[TOK, 128], [128 * TOK, 8], [1, T3]]),
                        in_=sb[f"odw_{p_}"][:].rearrange("p (k j) -> p k j", k=8))))(),
                        reads=[f"odw_{sp}:{m}" for m in range(8)],
                        writes=[f"out_{s3}"], dma=True, fam="outd")

        if debug:
            pl.fence(("q",))
            for nm, (dst, srct) in dbg.items():
                pl.emit("q", (lambda d_=dst, s_=srct: (lambda h: h.dma_start(
                    out=d_.ap()[:, :], in_=s_.ap()[:, :])))(),
                    reads=[], writes=[nm], dma=True, fam="outd")
        pl.fence(("g",))
        pl.emit("g", lambda h: h.nop())

        global _LAST_PLAN
        _LAST_PLAN = pl
        with nc.Block() as block:
            pl.replay(nc, block, es)
    return nc


_NC_CACHE = None
_LAST_PLAN = None


def kernel(L0_attn_x, L0_attn_W_in, L0_attn_conv_w, L0_attn_conv_b, L0_attn_W_x,
           L0_attn_dt_w, L0_attn_dt_b, L0_attn_A_log, L0_attn_D_param,
           L0_attn_W_out, L0_ffn_weight_fp8, L0_ffn_bias, L0_ffn_input_scale,
           L0_ffn_weight_scale, _debug=False):
    f32 = np.float32
    x = np.asarray(L0_attn_x, f32)
    winT = np.ascontiguousarray(np.asarray(L0_attn_W_in, f32).T)
    wxT = np.ascontiguousarray(np.asarray(L0_attn_W_x, f32).T)
    woutT = np.ascontiguousarray(np.asarray(L0_attn_W_out, f32).T)
    wfT = np.ascontiguousarray(np.asarray(L0_ffn_weight_fp8, f32).T)
    cw = np.ascontiguousarray(np.asarray(L0_attn_conv_w, f32))
    cb = np.asarray(L0_attn_conv_b, f32).reshape(DI, 1)
    dtw = np.asarray(L0_attn_dt_w, f32).reshape(DI, 1)
    dtb = np.asarray(L0_attn_dt_b, f32).reshape(DI, 1)
    alog = np.ascontiguousarray(np.asarray(L0_attn_A_log, f32))
    dparam = np.asarray(L0_attn_D_param, f32).reshape(DI, 1)
    fbias = np.asarray(L0_ffn_bias, f32).reshape(DM, 1)
    sin = np.asarray(L0_ffn_input_scale, f32).reshape(1, 1)
    sw = np.asarray(L0_ffn_weight_scale, f32).reshape(1, 1)
    ident = np.eye(128, dtype=f32)

    global _NC_CACHE
    if _debug:
        nc = build(debug=True)
    else:
        if _NC_CACHE is None:
            _NC_CACHE = build()
        nc = _NC_CACHE

    in_maps = []
    for k in range(8):
        b, h = k // 2, k % 2
        t0 = h * TOK
        xs_ = np.ascontiguousarray(x[b, t0:t0 + TOK, :])
        xh_ = (np.zeros((3, DM), f32) if h == 0
               else np.ascontiguousarray(x[b, t0 - 3:t0, :]))
        dh0 = h * DH
        ridx_ = np.array([
            [(2 * b * AGROWS + h * DH) * (TOK // 4)],
            [(2 * b * AGROWS + DI) * (TOK // 4)],
            [2 * b * 512 * L + h * TOK],
            [k * AGROWS * TOK]], dtype=np.int32)
        in_maps.append(dict(
            xs=xs_, xsT=np.ascontiguousarray(xs_.T), xh=xh_, identin=ident,
            winT=winT, wxT=wxT, woutT=woutT,
            wfT=wfT, cw=cw, cb=cb,
            dtw_s=np.ascontiguousarray(dtw[dh0:dh0 + DH]),
            dtb_s=np.ascontiguousarray(dtb[dh0:dh0 + DH]),
            alog_s=np.ascontiguousarray(alog[dh0:dh0 + DH]),
            dparam=dparam, fbias=fbias, sin=sin, sw=sw, ridx=ridx_))
    res = run_bass_kernel_spmd(nc, in_maps, core_ids=list(range(8)))
    out = np.empty((B, L, DM), f32)
    for k in range(8):
        b, h = k // 2, k % 2
        out[b, h * TOK:(h + 1) * TOK, :] = np.asarray(res.results[k]["out_d"]).T
    if _debug:
        return out, res
    return out
